# version 14
# baseline (speedup 1.0000x reference)
import contextlib
import numpy as np
import ml_dtypes

import concourse.bass as bass
import concourse.mybir as mybir
from concourse.bass_utils import run_bass_kernel_spmd

F32 = mybir.dt.float32
BF16 = mybir.dt.bfloat16
I32 = mybir.dt.int32
AF = mybir.ActivationFunctionType
ALU = mybir.AluOpType
AX = mybir.AxisListType
bfdt = ml_dtypes.bfloat16

SEQ, D, HALF, NT = 2048, 2048, 1024, 8
NDK = 16
NG, NI, NICH = 8, 1408, 11
NH, NKV, DH = 16, 4, 128
NEG = -1.0e30
DEBUG_TAPS = ()


class S:
    def __init__(self, stack, nc, name):
        self.h = stack.enter_context(nc.semaphore(name))
        self.n = 0

    def inc(self, instr, k=1):
        instr.then_inc(self.h, k)
        self.n += k
        return self.n

    def evmap(self, schedule, step=1):
        m = {}
        n = self.n
        for key in schedule:
            n += step
            m[key] = n
        return m

    def inc_ev(self, instr, m, key, step=1):
        assert self.n + step == m[key], (key, self.n, m[key])
        return self.inc(instr, step)


class Waiter:

    def __init__(self):
        self.seen = {}

    def w(self, e, sem, cnt):
        if self.seen.get(id(sem), -1) >= cnt:
            return
        e.wait_ge(sem.h, cnt)
        self.seen[id(sem)] = cnt


class Banks:

    def __init__(self, wt=None):
        self.last = {}
        self.wt = wt

    def acquire(self, e, bank_id, evac):
        prev = self.last.get(bank_id)
        if prev is not None:
            sem, cnt = prev
            if self.wt is not None:
                self.wt.w(e, sem, cnt)
            else:
                e.wait_ge(sem.h, cnt)
        self.last[bank_id] = evac


def build_program():
    nc = bass.Bass("TRN2", target_bir_lowering=False)

    def din(name, shape, dt=F32):
        return nc.dram_tensor(name, list(shape), dt, kind="ExternalInput").ap()

    x_in = din("x", (SEQ, D))
    wg_in = din("wg", (NG, NICH, 128, NDK, 128))
    wu_in = din("wu", (NG, NICH, 128, NDK, 128))
    wd_in = din("wd", (NG, 4, NICH, 128, 512))
    wq_in = din("wq", (NH, 128, NDK, 128))
    wkv_in = din("wkv", (NKV * 2, 128, NDK, 128))
    wo_in = din("wo", (4, 4, 128, 4, 512))
    gw_in = din("gw", (128, NDK * 64))
    mkw_in = din("mkw", (128, NG * NDK * 8))
    qkt_in = din("qkt", (128, NH))
    inv_in = din("inv", (64, 64), I32)
    cs_in = din("cs", (128, 2 * SEQ))
    mskh_in = din("mskh", (128, 384), BF16)
    idb_in = din("identb", (128, 128), BF16)
    idf_in = din("identf", (128, 128))
    iota_in = din("iota64", (64, 1))
    onr_in = din("onesrow", (1, 128))
    onc_in = din("onescol", (128, 1), BF16)

    out = nc.dram_tensor("out", [SEQ, D], F32, kind="ExternalOutput").ap()

    soft_dram = nc.dram_tensor("soft_scr", [NG, 8, SEQ], BF16, kind="Internal").ap()
    mw_dram = nc.dram_tensor("mw_scr", [2, NG, 8, NI], BF16, kind="Internal").ap()
    kv_dram = nc.dram_tensor("kv_scr", [2, 128, 4096], BF16, kind="Internal").ap()

    import itertools
    _uid = itertools.count()

    tap_outs = {}

    def tap_dram(label, shape, dt=F32):
        t = nc.dram_tensor("tap_" + label, list(shape), dt, kind="ExternalOutput")
        tap_outs[label] = t.ap()
        return tap_outs[label]

    st = contextlib.ExitStack()
    with st:
        def sb(name, p, f, dt):
            return st.enter_context(nc.sbuf_tensor(name, [p, f], dt))

        r_sb = sb("r_sb", 128, NT * D, F32)
        xln = sb("xln", 128, NDK * HALF, BF16)
        scalbuf = sb("scalbuf", 128, 64, F32)
        tempb = sb("tempb", 128, 16, F32)
        tempnb = sb("tempnb", 128, 16, F32)
        tempab = sb("tempab", 128, 16, F32)
        sgnb = sb("sgnb", 128, 16, F32)
        Pm = sb("Pm", 64, 64, F32)
        idb = sb("idb", 128, 128, BF16)
        idf = sb("idf", 128, 128, F32)
        iota = sb("iota", 64, 1, F32)
        onr = sb("onr", 1, 128, F32)
        onc = sb("onc", 128, 1, BF16)
        invf = sb("invf", 64, 64, F32)
        invi = sb("invi", 64, 64, I32)
        junkc = sb("junkc", 128, 1, F32)
        epsc = sb("epsc", 128, 1, F32)

        psf = [st.enter_context(nc.psum_tensor(f"psf{i}", [128, 512], F32))
               for i in range(6)]
        psb = [st.enter_context(nc.psum_tensor(f"psb{i}", [128, 1024], BF16))
               for i in range(2)]

        s_x = S(st, nc, "s_x")
        s_w = S(st, nc, "s_w")
        s_g = S(st, nc, "s_g")
        s_st = S(st, nc, "s_st")
        s_pe = S(st, nc, "s_pe")
        s_pe2 = S(st, nc, "s_pe2")
        s_cast = S(st, nc, "s_cast")
        s_act = S(st, nc, "s_act")
        s_dve = S(st, nc, "s_dve")
        s_dve2 = S(st, nc, "s_dve2")
        s_dve3 = S(st, nc, "s_dve3")
        s_sv = S(st, nc, "s_sv")
        s_pl = S(st, nc, "s_pl")
        s_x2 = [S(st, nc, f"s_x2_{i}") for i in range(2)]
        s_wf = [S(st, nc, f"s_wf_{i}") for i in range(3)]
        s_wa = [S(st, nc, f"s_wa_{i}") for i in range(2)]
        s_gs = [S(st, nc, f"s_gs_{i}") for i in range(2)]
        s_gml = [S(st, nc, f"s_gml_{i}") for i in range(2)]
        s_gms = [S(st, nc, f"s_gms_{i}") for i in range(2)]
        s_ss = [S(st, nc, f"s_ss_{i}") for i in range(2)]
        s_gc = S(st, nc, "s_gc")
        s_gkv = S(st, nc, "s_gkv")
        s_gw = S(st, nc, "s_gw")

        def vs(e):
            n = s_sv.inc(e.engine_nop())
            e.wait_ge(s_sv.h, n)

        with nc.Block() as b:
            @b.gpsimd
            def _(e):
                for dst, src in [(idb, idb_in), (idf, idf_in), (iota, iota_in),
                                 (onr, onr_in), (onc, onc_in), (tempb, qkt_in),
                                 (invi, inv_in)]:
                    s_g.inc(e.dma_start(dst[:], src[:]), 16)
                e.wait_ge(s_g.h, s_g.n)

        with nc.Block() as b:
            @b.vector
            def _(e):
                e.tensor_scalar(tempnb[:], tempb[:], -1.0, None, ALU.mult)
                e.tensor_copy(invf[:], invi[:])
                e.memset(epsc[:], 1e-5)
                e.tensor_tensor(tempnb[:], tempnb[:], tempb[:], ALU.min)

        with nc.Block() as b:
            @b.vector
            def _(e):
                e.tensor_scalar(tempab[:], tempnb[:], -1.0, None, ALU.mult)
                e.tensor_scalar(Pm[:], invf[:], iota[:], None, ALU.is_equal)
                e.tensor_scalar(sgnb[:], tempab[:], 1e-30, None, ALU.max)
                vs(e)
                e.reciprocal(sgnb[:], sgnb[:])
                vs(e)
                e.tensor_tensor(sgnb[:], sgnb[:], tempb[:], ALU.mult)

        def phase_ln_gate(h, src_hbm, gate):
            with contextlib.ExitStack() as ph:
                _u = next(_uid)
                def pb_(name, p, f, dt):
                    return ph.enter_context(
                        nc.sbuf_tensor(f"{name}_{_u}", [p, f], dt))
                xtile = pb_("xtile", 128, 2 * D, F32) if src_hbm else None
                sq = pb_("sq", 128, 2 * D, F32)
                xlt = pb_("xlt", 128, 2 * D, BF16)
                cvar = pb_("cvar", 128, 2, F32)
                cmean = pb_("cmean", 128, 2, F32)
                crstd = pb_("crstd", 128, 2, F32)
                ccol = pb_("ccol", 128, 8, F32)
                if gate:
                    gwt = pb_("gwt", 128, NDK * 64, BF16)
                    gstg = pb_("gstg", 128, NDK * 64, F32)
                    scs = pb_("scs", 64, 2 * 128, F32)
                    sc = pb_("sc", 128, 2 * 64, F32)
                    sce = pb_("sce", 128, 2 * 64, F32)
                    cden = pb_("cden", 128, 2, F32)
                    wesm = pb_("wesm", 128, 2 * 64, F32)
                    vv = pb_("vv", 128, 2 * 64, F32)
                    mskt = pb_("mskt", 128, 2 * 64, F32)
                    wee = pb_("wee", 128, 2 * 64, F32)
                    weT = pb_("weT", 64, 2 * 128, F32)
                    flat = pb_("flat", 128, 2 * 64, F32)
                    fmn = pb_("fmn", 128, 2 * 64, F32)
                    gs8 = pb_("gs8", 128, 2 * 8, F32)
                    gm8 = pb_("gm8", 128, 2 * 8, F32)
                    gr8 = pb_("gr8", 128, 2 * 8, F32)
                    gsm = pb_("gsm", 128, 2 * 64, F32)
                    gex = pb_("gex", 128, 2 * 64, F32)
                    sfb = pb_("sfb", 128, 2 * 64, BF16)
                    sstg = pb_("sstg", 8, 2 * 1024, BF16)

                xb2 = [s.n for s in s_x2]
                ssb = [s.n for s in s_ss]
                gwb = s_gw.n
                db, pb0, stb = s_dve.n, s_pe.n, s_st.n
                EM = {}
                d2m, d3m, am, p2m, dvm, cm = {}, {}, {}, {}, {}, {}

                def dve_tile(t):
                    p = t % 2
                    xt = (xtile[:, p * D:(p + 1) * D] if src_hbm
                          else r_sb[:, t * D:(t + 1) * D])
                    L = []
                    A = L.append
                    if src_hbm:
                        A((lambda e: e.wait_ge(s_x2[t % 2].h,
                                               xb2[t % 2] + 16 * (t // 2 + 1)), []))
                    A((lambda e: e.tensor_mul(sq[:, p * D:(p + 1) * D], xt, xt), []))
                    A((lambda e: e.reduce_sum(ccol[:, 4 * p:4 * p + 1], xt,
                                              axis=AX.X), []))
                    A((lambda e: e.reduce_sum(ccol[:, 4 * p + 1:4 * p + 2],
                                              sq[:, p * D:(p + 1) * D],
                                              axis=AX.X), []))
                    A((lambda e: e.tensor_scalar(cmean[:, p:p + 1],
                                                 ccol[:, 4 * p:4 * p + 1],
                                                 1.0 / D, None, ALU.mult), []))
                    A((lambda e: e.tensor_scalar(ccol[:, 4 * p + 1:4 * p + 2],
                                                 ccol[:, 4 * p + 1:4 * p + 2],
                                                 1.0 / D, None, ALU.mult), []))
                    A((lambda e: e.tensor_mul(ccol[:, 4 * p + 2:4 * p + 3],
                                              cmean[:, p:p + 1],
                                              cmean[:, p:p + 1]), []))
                    if src_hbm:
                        def rinit(e):
                            if h == 1 and t == 0:
                                e.wait_ge(s_st.h, stb)
                            e.tensor_copy(r_sb[:, t * D:(t + 1) * D], xt)
                        A((rinit, []))
                    A((lambda e: s_dve2.inc_ev(e.scalar_tensor_tensor(
                        cvar[:, p:p + 1], ccol[:, 4 * p + 2:4 * p + 3], -1.0,
                        ccol[:, 4 * p + 1:4 * p + 2], ALU.mult, ALU.add),
                        d2m, ("var", t)), [("d2", ("var", t))]))

                    def pe_tr_cnt(tt, grp):
                        return pb0 + (tt // 2) * 4 + grp * 2 + (tt % 2) + 1

                    def xlt_op(e):
                        if t >= 2:
                            e.wait_ge(s_pe.h, pe_tr_cnt(t - 2, 1))
                        e.wait_ge(s_act.h, am[("rstd", t)])
                        s_dve.inc_ev(e.tensor_scalar(
                            xlt[:, p * D:(p + 1) * D], xt, cmean[:, p:p + 1],
                            crstd[:, p:p + 1], ALU.subtract, ALU.mult),
                            dvm, ("xlt", t))
                    A((xlt_op, [("dv", ("xlt", t))]))
                    for grp in range(2):
                        def evac(e, grp=grp):
                            e.wait_ge(s_pe.h, pe_tr_cnt(t, grp))
                            dst = xln[:, grp * 8 * HALF:(grp * 8 + 8) * HALF] \
                                .rearrange("p (dk c) -> p dk c", dk=8)[:, :, t * 128:(t + 1) * 128]
                            s_dve3.inc_ev(
                                e.tensor_copy(dst, psb[grp][:]
                                              .rearrange("p (dk c) -> p dk c", dk=8)),
                                d3m, (f"ev{grp}", t))
                        A((evac, [("d3", (f"ev{grp}", t))]))
                    if not gate:
                        return L

                    def scs_ev(e):
                        e.wait_ge(s_pe2.h, p2m[("scores", t)])
                        s_dve2.inc_ev(e.tensor_copy(
                            scs[:, p * 128:(p + 1) * 128],
                            psf[2 if p == 0 else 5][0:64, 0:128]),
                            d2m, ("scs", t))
                    A((scs_ev, [("d2", ("scs", t))]))

                    def sc_ev(e):
                        e.wait_ge(s_pe2.h, p2m[("sctr", t)])
                        s_dve2.inc_ev(e.tensor_copy(sc[:, p * 64:(p + 1) * 64],
                                                    psf[3][0:128, 0:64]),
                                      d2m, ("scev", t))
                    A((sc_ev, [("d2", ("scev", t))]))
                    A((lambda e: s_dve2.inc_ev(
                        e.reduce_max(ccol[:, 4 * p + 3:4 * p + 4],
                                     sc[:, p * 64:(p + 1) * 64],
                                     axis=AX.X, negate=True),
                        d2m, ("negmax", t)), [("d2", ("negmax", t))]))

                    def recip_op(e):
                        e.wait_ge(s_act.h, am[("exp64", t)])
                        e.reciprocal(cden[:, p:p + 1], cden[:, p:p + 1])
                    A((recip_op, []))
                    A((lambda e: e.tensor_scalar(wesm[:, p * 64:(p + 1) * 64],
                                                 sce[:, p * 64:(p + 1) * 64],
                                                 cden[:, p:p + 1], None,
                                                 ALU.mult), []))
                    A((lambda e: e.tensor_copy(vv[:, p * 64:(p + 1) * 64],
                                               wesm[:, p * 64:(p + 1) * 64]), []))
                    for it in range(8):
                        A((lambda e: e.reduce_max(ccol[:, 4 * p + 3:4 * p + 4],
                                                  vv[:, p * 64:(p + 1) * 64],
                                                  axis=AX.X), []))
                        if it < 7:
                            A((lambda e: e.tensor_scalar(
                                mskt[:, p * 64:(p + 1) * 64],
                                vv[:, p * 64:(p + 1) * 64],
                                ccol[:, 4 * p + 3:4 * p + 4],
                                NEG, ALU.is_ge, ALU.mult), []))
                            A((lambda e: e.tensor_add(
                                vv[:, p * 64:(p + 1) * 64],
                                vv[:, p * 64:(p + 1) * 64],
                                mskt[:, p * 64:(p + 1) * 64]), []))
                    A((lambda e: e.tensor_scalar(mskt[:, p * 64:(p + 1) * 64],
                                                 wesm[:, p * 64:(p + 1) * 64],
                                                 ccol[:, 4 * p + 3:4 * p + 4],
                                                 None, ALU.is_ge), []))
                    A((lambda e: s_dve2.inc_ev(
                        e.tensor_mul(wee[:, p * 64:(p + 1) * 64],
                                     mskt[:, p * 64:(p + 1) * 64],
                                     wesm[:, p * 64:(p + 1) * 64]),
                        d2m, ("we", t)), [("d2", ("we", t))]))

                    def weT_ev(e):
                        e.wait_ge(s_pe2.h, p2m[("wetr", t)])
                        s_dve2.inc_ev(e.tensor_copy(
                            weT[:, p * 128:(p + 1) * 128],
                            psf[4][0:64, 0:128]),
                            d2m, ("weT", t))
                    A((weT_ev, [("d2", ("weT", t))]))

                    def flat_ev(e):
                        e.wait_ge(s_pe2.h, p2m[("flat", t)])
                        s_dve2.inc_ev(e.tensor_copy(flat[:, p * 64:(p + 1) * 64],
                                                    psf[0][0:128, 0:64]),
                                      d2m, ("flatev", t))
                    A((flat_ev, [("d2", ("flatev", t))]))
                    A((lambda e: e.tensor_scalar(mskt[:, p * 64:(p + 1) * 64],
                                                 flat[:, p * 64:(p + 1) * 64],
                                                 0.0, -1.0e9, ALU.is_equal,
                                                 ALU.mult), []))
                    A((lambda e: e.reduce_sum(
                        gs8[:, p * 8:(p + 1) * 8].rearrange("p (g o) -> p g o", g=8),
                        flat[:, p * 64:(p + 1) * 64].rearrange("p (g i) -> p g i", g=8),
                        axis=AX.X), []))
                    A((lambda e: e.tensor_add(fmn[:, p * 64:(p + 1) * 64],
                                              flat[:, p * 64:(p + 1) * 64],
                                              mskt[:, p * 64:(p + 1) * 64]), []))
                    A((lambda e: e.tensor_scalar(scalbuf[:, t * 8:(t + 1) * 8],
                                                 gs8[:, p * 8:(p + 1) * 8],
                                                 0.5, None, ALU.mult), []))
                    A((lambda e: e.reduce_max(
                        gm8[:, p * 8:(p + 1) * 8].rearrange("p (g o) -> p g o", g=8),
                        fmn[:, p * 64:(p + 1) * 64].rearrange("p (g i) -> p g i", g=8),
                        axis=AX.X), []))
                    A((lambda e: s_dve2.inc_ev(e.tensor_tensor(
                        gsm[:, p * 64:(p + 1) * 64].rearrange("p (g i) -> p g i", g=8),
                        fmn[:, p * 64:(p + 1) * 64].rearrange("p (g i) -> p g i", g=8),
                        gm8[:, p * 8:(p + 1) * 8].rearrange("p (g o) -> p g o", g=8)
                        .broadcast_to((128, 8, 8)), ALU.subtract),
                        d2m, ("gsm", t)), [("d2", ("gsm", t))]))

                    def gsum_op(e):
                        e.wait_ge(s_act.h, am[("gex", t)])
                        e.reduce_sum(
                            gs8[:, p * 8:(p + 1) * 8].rearrange("p (g o) -> p g o", g=8),
                            gex[:, p * 64:(p + 1) * 64].rearrange("p (g i) -> p g i", g=8),
                            axis=AX.X)
                    A((gsum_op, []))
                    A((lambda e: e.reciprocal(gr8[:, p * 8:(p + 1) * 8],
                                              gs8[:, p * 8:(p + 1) * 8]), []))
                    A((lambda e: s_dve2.inc_ev(e.tensor_tensor(
                        sfb[:, p * 64:(p + 1) * 64].rearrange("p (g i) -> p g i", g=8),
                        gex[:, p * 64:(p + 1) * 64].rearrange("p (g i) -> p g i", g=8),
                        gr8[:, p * 8:(p + 1) * 8].rearrange("p (g o) -> p g o", g=8)
                        .broadcast_to((128, 8, 8)), ALU.mult),
                        d2m, ("sfb", t)), [("d2", ("sfb", t))]))

                    def softev_op(e):
                        e.wait_ge(s_pe2.h, p2m[("softtr", t)])
                        if t >= 2:
                            e.wait_ge(s_ss[t % 2].h,
                                      ssb[t % 2] + 16 * ((t - 2) // 2 + 1))
                        s_dve3.inc_ev(e.tensor_copy(
                            sstg[0:8, p * 1024:(p + 1) * 1024]
                            .rearrange("i (g c) -> i g c", g=8),
                            psb[1][0:8, :].rearrange("i (g c) -> i g c", g=8)),
                            d3m, ("softev", t))
                        s_dve2.inc_ev(e.engine_nop(), d2m, ("sstg", t))
                    A((softev_op, [("d3", ("softev", t)), ("d2", ("sstg", t))]))
                    return L

                def act_tile(t):
                    p = t % 2
                    L = []

                    def rstd_op(e):
                        e.wait_ge(s_dve2.h, d2m[("var", t)])
                        e.activation(crstd[:, p:p + 1], cvar[:, p:p + 1],
                                     AF.Ln, bias=epsc[:])
                        s_act.inc_ev(e.activation(crstd[:, p:p + 1],
                                                  crstd[:, p:p + 1],
                                                  AF.Exp, scale=-0.5),
                                     am, ("rstd", t))
                    L.append((rstd_op, [("a", ("rstd", t))]))
                    if not gate:
                        return L

                    def exp64_op(e):
                        e.wait_ge(s_dve2.h, d2m[("negmax", t)])
                        s_act.inc_ev(
                            e.activation(sce[:, p * 64:(p + 1) * 64],
                                         sc[:, p * 64:(p + 1) * 64], AF.Exp,
                                         bias=ccol[:, 4 * p + 3:4 * p + 4],
                                         accum_out=cden[:, p:p + 1]),
                            am, ("exp64", t))
                    L.append((exp64_op, [("a", ("exp64", t))]))

                    def gex_op(e):
                        e.wait_ge(s_dve2.h, d2m[("gsm", t)])
                        s_act.inc_ev(e.activation(gex[:, p * 64:(p + 1) * 64],
                                                  gsm[:, p * 64:(p + 1) * 64],
                                                  AF.Exp), am, ("gex", t))
                    L.append((gex_op, [("a", ("gex", t))]))
                    return L

                def pe_tile(t):
                    p = t % 2
                    L = []
                    for grp in range(2):
                        def tr(e, grp=grp):
                            bt = EM["bt"]
                            e.wait_ge(s_dve.h, dvm[("xlt", t)])
                            bt.acquire(e, ("psb", grp),
                                       (s_dve3, d3m[(f"ev{grp}", t)]))
                            for k in range(8):
                                dk = grp * 8 + k
                                mm = e.matmul(
                                    psb[grp][:, k * 128:(k + 1) * 128],
                                    xlt[:, p * D + dk * 128:p * D + (dk + 1) * 128],
                                    idb[:], start=(k == 0), stop=(k == 7),
                                    is_transpose=True)
                            s_pe.inc(mm)
                        L.append((tr, []))
                    if not gate:
                        return L

                    def scores_op(e):
                        bt = EM["bt"]
                        e.wait_ge(s_dve3.h, d3m[("ev1", t)])
                        if t == 0:
                            e.wait_ge(s_cast.h, cm[("gwt",)])
                        bank = 2 if p == 0 else 5
                        bt.acquire(e, ("psf", bank), (s_dve2, d2m[("scs", t)]))
                        for dk in range(NDK):
                            mm = e.matmul(
                                psf[bank][0:64, 0:128],
                                gwt[:, dk * 64:(dk + 1) * 64],
                                xln[:, dk * HALF + t * 128:dk * HALF + (t + 1) * 128],
                                start=(dk == 0), stop=(dk == NDK - 1))
                        s_pe2.inc_ev(mm, p2m, ("scores", t))
                    L.append((scores_op, [("p2", ("scores", t))]))

                    def sctr_op(e):
                        bt = EM["bt"]
                        e.wait_ge(s_dve2.h, d2m[("scs", t)])
                        bt.acquire(e, ("psf", 3), (s_dve2, d2m[("scev", t)]))
                        s_pe2.inc_ev(e.matmul(psf[3][0:128, 0:64],
                                              scs[:, p * 128:(p + 1) * 128],
                                              idf[0:64, 0:64], start=True,
                                              stop=True, is_transpose=True),
                                     p2m, ("sctr", t))
                    L.append((sctr_op, [("p2", ("sctr", t))]))

                    def wetr_op(e):
                        bt = EM["bt"]
                        e.wait_ge(s_dve2.h, d2m[("we", t)])
                        bt.acquire(e, ("psf", 4), (s_dve2, d2m[("weT", t)]))
                        s_pe2.inc_ev(e.matmul(psf[4][0:64, 0:128],
                                              wee[:, p * 64:(p + 1) * 64],
                                              idf[:], start=True, stop=True,
                                              is_transpose=True),
                                     p2m, ("wetr", t))
                    L.append((wetr_op, [("p2", ("wetr", t))]))

                    def flat_op(e):
                        bt = EM["bt"]
                        e.wait_ge(s_dve2.h, d2m[("weT", t)])
                        bt.acquire(e, ("psf", 0), (s_dve2, d2m[("flatev", t)]))
                        e.matmul(psf[0][0:128, 0:64],
                                 weT[:, p * 128:(p + 1) * 128], Pm[:],
                                 start=True, stop=True)
                        s_pe2.inc_ev(e.matmul(
                            psf[2 if p == 0 else 5][0:1, 400:401],
                            idb[0:1, 0:1], idb[0:1, 0:1],
                            start=False, stop=True),
                                     p2m, ("flat", t))
                    L.append((flat_op, [("p2", ("flat", t))]))

                    def softtr_op(e):
                        bt = EM["bt"]
                        e.wait_ge(s_dve2.h, d2m[("sfb", t)])
                        bt.acquire(e, ("psb", 1), (s_dve3, d3m[("softev", t)]))
                        for g in range(8):
                            mm = e.matmul(psb[1][0:8, g * 128:(g + 1) * 128],
                                          sfb[:, p * 64 + g * 8:p * 64 + (g + 1) * 8],
                                          idb[:], start=(g == 0), stop=(g == 7),
                                          is_transpose=True)
                        s_pe2.inc_ev(mm, p2m, ("softtr", t))
                    L.append((softtr_op, [("p2", ("softtr", t))]))
                    return L

                def zipped(mk):
                    seq = []
                    for pp in range(NT // 2):
                        a = mk(2 * pp)
                        bo = mk(2 * pp + 1)
                        for i in range(max(len(a), len(bo))):
                            if i < len(a):
                                seq.append(a[i])
                            if i < len(bo):
                                seq.append(bo[i])
                    return seq

                dve_seq = zipped(dve_tile)
                act_seq = zipped(act_tile)
                pe_seq = zipped(pe_tile)

                def evs_of(seq, ch):
                    return [k for _, evs in seq for c, k in evs if c == ch]

                d2m.update(s_dve2.evmap(evs_of(dve_seq, "d2")))
                d3m.update(s_dve3.evmap(evs_of(dve_seq, "d3")))
                dvm.update(s_dve.evmap(evs_of(dve_seq, "dv")))
                am.update(s_act.evmap(evs_of(act_seq, "a")))
                p2m.update(s_pe2.evmap(evs_of(pe_seq, "p2")))
                if gate:
                    cm.update(s_cast.evmap([("gwt",)]))

                with nc.Block() as b:
                    if src_hbm:
                        @b.sync
                        def _(e):
                            for t in range(NT):
                                if t >= 2:
                                    e.wait_ge(s_dve.h, dvm[("xlt", t - 2)])
                                s_x2[t % 2].inc(e.dma_start(
                                    xtile[:, (t % 2) * D:(t % 2 + 1) * D],
                                    x_in[h * HALF + t * 128:h * HALF + (t + 1) * 128, :]),
                                    16)

                    if gate:
                        @b.gpsimd
                        def _(e):
                            s_gw.inc(e.dma_start(gstg[:], gw_in), 16)
                            for t in range(NT):
                                e.wait_ge(s_dve2.h, d2m[("sstg", t)])
                                s_ss[t % 2].inc(e.dma_start(
                                    soft_dram[:, :, h * HALF + t * 128:
                                              h * HALF + (t + 1) * 128]
                                    .rearrange("g i c -> i g c"),
                                    sstg[0:8, (t % 2) * 1024:(t % 2 + 1) * 1024]
                                    .rearrange("i (g c) -> i g c", g=8)), 16)

                    @b.scalar
                    def _(e):
                        if gate:
                            e.wait_ge(s_gw.h, gwb + 16)
                            s_cast.inc_ev(e.activation(gwt[:], gstg[:], AF.Copy),
                                          cm, ("gwt",))
                        for fn, _evs in act_seq:
                            fn(e)

                    @b.vector
                    def _(e):
                        for fn, _evs in dve_seq:
                            fn(e)

                    @b.tensor
                    def _(e):
                        EM["bt"] = Banks()
                        for fn, _evs in pe_seq:
                            fn(e)

        def phase_ffn(h, build_mw):
            with contextlib.ExitStack() as ph:
                _u = next(_uid)
                ap_ = ph.enter_context(
                    nc.sbuf_tensor(f"apA_{_u}", [128, NICH * HALF], BF16))
                silg = ph.enter_context(
                    nc.sbuf_tensor(f"silg_{_u}", [128, 2 * HALF], F32))
                wdbf = ph.enter_context(
                    nc.sbuf_tensor(f"wdbf_{_u}", [128, 2 * NICH * 512], BF16))
                mskw = (ph.enter_context(
                    nc.sbuf_tensor(f"mskw_{_u}", [128, NG * NDK * 8], BF16))
                    if build_mw else None)
                mstg = (ph.enter_context(
                    nc.sbuf_tensor(f"mstg_{_u}", [128, NG * NDK * 8], F32))
                    if build_mw else None)
                soft_rg = ph.enter_context(
                    nc.sbuf_tensor(f"softrg_{_u}", [8, 2 * HALF], BF16))
                fstage = ph.enter_context(
                    nc.sbuf_tensor(f"fstage_{_u}", [128, 3 * 2048], F32))
                fwbf = ph.enter_context(
                    nc.sbuf_tensor(f"fwbf_{_u}", [128, 3 * 2048], BF16))
                mw_rg = ph.enter_context(
                    nc.sbuf_tensor(f"mwrg_{_u}", [8, 2 * 2 * NI], BF16))

                wfb = [s.n for s in s_wf]
                gsb = [s.n for s in s_gs]
                gmlb = [s.n for s in s_gml]
                gmsb = [s.n for s in s_gms]
                ssb_tot = [s.n for s in s_ss]
                gwb = s_gw.n
                NU = 34
                down_units = [(dblk, u * 4, min(4, NICH - u * 4))
                              for dblk in range(4) for u in range(3)]

                cast_ev = ([("mk",)] if build_mw else [])
                pe_ev, pe2_ev, act_ev, dve_ev, d2_ev, d3_ev = [], [], [], [], [], []
                for g in range(NG):
                    for k in range(22):
                        cast_ev.append(("gu", g, k))
                    for ich in range(NICH):
                        if build_mw:
                            for m in range(2):
                                pe2_ev.append(("mw", g, ich, m))
                                d3_ev.append(("mwev", g, ich, m))
                        for m in range(2):
                            for tb in range(2):
                                pe_ev.append(("gu", g, ich, m, tb))
                                if m == 0:
                                    act_ev.append(("sil", g, ich, tb))
                                else:
                                    dve_ev.append(("ap", g, ich, tb))
                    for j in range(12):
                        cast_ev.append(("dn", g, j))
                    for dblk in range(4):
                        for t in range(NT):
                            pe2_ev.append(("dn", g, dblk, t))
                            d2_ev.append(("dnev", g, dblk, t))
                cmap = s_cast.evmap(cast_ev)
                pem = s_pe.evmap(pe_ev)
                pe2m = s_pe2.evmap(pe2_ev)
                actm = s_act.evmap(act_ev)
                dvem = s_dve.evmap(dve_ev)
                d2map = s_dve2.evmap(d2_ev)
                d3map = s_dve3.evmap(d3_ev)

                def cast_key(g, k):
                    return ("gu", g, k) if k < 22 else ("dn", g, k - 22)

                def glob_unit(ui):
                    return divmod(ui, NU)

                with nc.Block() as b:
                    @b.sync
                    def _(e):
                        for g in range(NG):
                            for k in range(NU):
                                ui = g * NU + k
                                if ui >= 3:
                                    pg, pk = glob_unit(ui - 3)
                                    e.wait_ge(s_cast.h, cmap[cast_key(pg, pk)])
                                slot = fstage[:, (ui % 3) * 2048:(ui % 3 + 1) * 2048]
                                sem = s_wf[ui % 3]
                                if k < 22:
                                    ich, m = divmod(k, 2)
                                    sem.inc(e.dma_start(
                                        slot.rearrange("p (dk a) -> p dk a", dk=NDK),
                                        (wg_in if m == 0 else wu_in)[g, ich]), 16)
                                else:
                                    dblk, i0, ni = down_units[k - 22]
                                    sem.inc(e.dma_start(
                                        slot.rearrange("p (i c) -> p i c", i=4)[:, 0:ni, :],
                                        wd_in[g, dblk, i0:i0 + ni]
                                        .rearrange("i p c -> p i c")), 16)

                    @b.gpsimd
                    def _(e):
                        if build_mw:
                            s_gw.inc(e.dma_start(mstg[:], mkw_in), 16)
                        for i in range(2):
                            e.wait_ge(s_ss[i].h, s_ss[i].n)
                        if not build_mw:
                            for i in range(2):
                                e.wait_ge(s_gms[i].h, s_gms[i].n)
                        for g in range(NG):
                            if g >= 2:
                                e.wait_ge(s_pe.h, pem[("gu", g - 2, NICH - 1, 1, 1)])
                            sslot = soft_rg[0:8, (g % 2) * HALF:(g % 2 + 1) * HALF]
                            mslot = mw_rg[0:8, (g % 2) * 2 * NI:(g % 2 + 1) * 2 * NI]
                            s_gs[g % 2].inc(e.dma_start(
                                sslot,
                                soft_dram[g, :, h * HALF:(h + 1) * HALF]), 16)
                            if build_mw:
                                e.wait_ge(s_dve3.h, d3map[("mwev", g, NICH - 1, 1)])
                                s_gms[g % 2].inc(e.dma_start(
                                    mw_dram[:, g].rearrange("m i c -> i m c"),
                                    mslot.rearrange("i (m c) -> i m c", m=2)), 16)
                            else:
                                s_gml[g % 2].inc(e.dma_start(
                                    mslot.rearrange("i (m c) -> i m c", m=2),
                                    mw_dram[:, g].rearrange("m i c -> i m c")), 16)

                    def ap_evac(e, g, ich):
                        for tb in range(2):
                            e.wait_ge(s_pe.h, pem[("gu", g, ich, 1, tb)])
                            e.wait_ge(s_act.h, actm[("sil", g, ich, tb)])
                            s_dve.inc_ev(e.tensor_mul(
                                ap_[:, ich * HALF + tb * 512:ich * HALF + (tb + 1) * 512],
                                psf[2 + tb][:],
                                silg[:, (ich % 2) * HALF + tb * 512:
                                     (ich % 2) * HALF + (tb + 1) * 512]),
                                dvem, ("ap", g, ich, tb))

                    def down_evac(e, g, dblk, t):
                        e.wait_ge(s_pe2.h, pe2m[("dn", g, dblk, t)])
                        s_dve2.inc_ev(e.scalar_tensor_tensor(
                            r_sb[:, t * D + dblk * 512:t * D + (dblk + 1) * 512],
                            psf[4 + (g * 32 + dblk * 8 + t) % 2][:],
                            scalbuf[:, t * 8 + g:t * 8 + g + 1],
                            r_sb[:, t * D + dblk * 512:t * D + (dblk + 1) * 512],
                            ALU.mult, ALU.add), d2map, ("dnev", g, dblk, t))

                    @b.vector
                    def _(e):
                        for g in range(NG):
                            for ich in range(NICH):
                                if build_mw:
                                    for m in range(2):
                                        if g >= 2 and ich == 0 and m == 0:
                                            e.wait_ge(s_gms[g % 2].h,
                                                      gmsb[g % 2]
                                                      + 16 * ((g - 2) // 2 + 1))
                                        e.wait_ge(s_pe2.h, pe2m[("mw", g, ich, m)])
                                        s_dve3.inc_ev(e.tensor_copy(
                                            mw_rg[0:8, (g % 2) * 2 * NI + m * NI
                                                  + ich * 128:(g % 2) * 2 * NI
                                                  + m * NI + (ich + 1) * 128],
                                            psf[4 + m][0:8, 0:128]),
                                            d3map, ("mwev", g, ich, m))
                                if ich >= 1:
                                    ap_evac(e, g, ich - 1)
                            ap_evac(e, g, NICH - 1)
                            for dblk in range(1, 4):
                                for t in range(NT):
                                    down_evac(e, g, dblk - 1, t)
                            for t in range(NT):
                                down_evac(e, g, 3, t)

                    @b.scalar
                    def _(e):
                        if build_mw:
                            e.wait_ge(s_gw.h, s_gw.n)
                            s_cast.inc_ev(e.activation(mskw[:], mstg[:], AF.Copy),
                                          cmap, ("mk",))

                        def cast_gu(e, g, ich):
                            for m in range(2):
                                k = ich * 2 + m
                                Kgu = g * 22 + k
                                ui = g * NU + k
                                e.wait_ge(s_wf[ui % 3].h,
                                          wfb[ui % 3] + 16 * (ui // 3 + 1))
                                if Kgu >= 3:
                                    pk = Kgu - 3
                                    pg, pkk = divmod(pk, 22)
                                    pich, pm = divmod(pkk, 2)
                                    e.wait_ge(s_pe.h, pem[("gu", pg, pich, pm, 1)])
                                s_cast.inc_ev(e.activation(
                                    fwbf[:, (Kgu % 3) * 2048:(Kgu % 3 + 1) * 2048],
                                    fstage[:, (ui % 3) * 2048:(ui % 3 + 1) * 2048],
                                    AF.Copy), cmap, ("gu", g, k))

                        def sil_pair(e, g, ich):
                            I_ = g * NICH + ich
                            for tb in range(2):
                                e.wait_ge(s_pe.h, pem[("gu", g, ich, 0, tb)])
                                if I_ >= 2:
                                    pg, pich = divmod(I_ - 2, NICH)
                                    e.wait_ge(s_dve.h, dvem[("ap", pg, pich, tb)])
                                s_act.inc_ev(e.activation(
                                    silg[:, (ich % 2) * HALF + tb * 512:
                                         (ich % 2) * HALF + (tb + 1) * 512],
                                    psf[tb][:], AF.Silu),
                                    actm, ("sil", g, ich, tb))

                        for g in range(NG):
                            for ich in range(NICH):
                                cast_gu(e, g, ich)
                                if ich >= 1:
                                    sil_pair(e, g, ich - 1)
                            sil_pair(e, g, NICH - 1)
                            for j, (dblk, i0, ni) in enumerate(down_units):
                                k = 22 + j
                                ui = g * NU + k
                                DJ = g * 4 + dblk
                                e.wait_ge(s_wf[ui % 3].h,
                                          wfb[ui % 3] + 16 * (ui // 3 + 1))
                                if DJ >= 2 and i0 == 0:
                                    pg, pd = divmod(DJ - 2, 4)
                                    e.wait_ge(s_pe2.h, pe2m[("dn", pg, pd, NT - 1)])
                                s_cast.inc_ev(e.activation(
                                    wdbf[:, (dblk % 2) * NICH * 512 + i0 * 512:
                                         (dblk % 2) * NICH * 512 + (i0 + ni) * 512],
                                    fstage[:, (ui % 3) * 2048:(ui % 3) * 2048 + ni * 512],
                                    AF.Copy), cmap, ("dn", g, j))

                    @b.tensor
                    def _(e):
                        bt = Banks()
                        for g in range(NG):
                            for ich in range(NICH):
                                if build_mw:
                                    for m in range(2):
                                        k = ich * 2 + m
                                        e.wait_ge(s_cast.h, cmap[("gu", g, k)])
                                        if ich == 0 and m == 0:
                                            e.wait_ge(s_cast.h, cmap[("mk",)])
                                        bt.acquire(e, ("psf", 4 + m),
                                                   (s_dve3, d3map[("mwev", g, ich, m)]))
                                        for dk in range(NDK):
                                            mm = e.matmul(
                                                psf[4 + m][0:8, 0:128],
                                                mskw[:, g * 128 + dk * 8:
                                                     g * 128 + (dk + 1) * 8],
                                                fwbf[:, ((g * 22 + k) % 3) * 2048 + dk * 128:
                                                    ((g * 22 + k) % 3) * 2048 + (dk + 1) * 128],
                                                start=(dk == 0), stop=(dk == NDK - 1))
                                        s_pe2.inc_ev(mm, pe2m, ("mw", g, ich, m))
                                for m in range(2):
                                    k = ich * 2 + m
                                    e.wait_ge(s_cast.h, cmap[("gu", g, k)])
                                    if k == 0:
                                        e.wait_ge(s_gs[g % 2].h,
                                                  gsb[g % 2] + 16 * (g // 2 + 1))
                                        if not build_mw:
                                            e.wait_ge(s_gml[g % 2].h,
                                                      gmlb[g % 2]
                                                      + 16 * (g // 2 + 1))
                                    if build_mw:
                                        e.wait_ge(s_dve3.h, d3map[("mwev", g, ich, m)])
                                    for tb in range(2):
                                        bank = m * 2 + tb
                                        evac = ((s_act, actm[("sil", g, ich, tb)])
                                                if m == 0 else
                                                (s_dve, dvem[("ap", g, ich, tb)]))
                                        bt.acquire(e, ("psf", bank), evac)
                                        for dk in range(NDK):
                                            e.matmul(
                                                psf[bank][:],
                                                fwbf[:, ((g * 22 + k) % 3) * 2048 + dk * 128:
                                                    ((g * 22 + k) % 3) * 2048 + (dk + 1) * 128],
                                                xln[:, dk * HALF + tb * 512:
                                                    dk * HALF + (tb + 1) * 512],
                                                start=(dk == 0), stop=False)
                                        mm = e.matmul(
                                            psf[bank][:],
                                            mw_rg[0:8, (g % 2) * 2 * NI + m * NI
                                                  + ich * 128:(g % 2) * 2 * NI
                                                  + m * NI + (ich + 1) * 128],
                                            soft_rg[0:8, (g % 2) * HALF + tb * 512:
                                                    (g % 2) * HALF + (tb + 1) * 512],
                                            start=False, stop=True)
                                        s_pe.inc_ev(mm, pem, ("gu", g, ich, m, tb))
                            e.wait_ge(s_dve.h, dvem[("ap", g, NICH - 1, 1)])
                            for dblk in range(4):
                                e.wait_ge(s_cast.h, cmap[("dn", g, dblk * 3 + 2)])
                                for t in range(NT):
                                    gi = g * 32 + dblk * 8 + t
                                    bt.acquire(e, ("psf", 4 + gi % 2),
                                               (s_dve2, d2map[("dnev", g, dblk, t)]))
                                    for ich in range(NICH):
                                        mm = e.matmul(
                                            psf[4 + gi % 2][:],
                                            ap_[:, ich * HALF + t * 128:
                                                ich * HALF + (t + 1) * 128],
                                            wdbf[:, (dblk % 2) * NICH * 512 + ich * 512:
                                                 (dblk % 2) * NICH * 512 + (ich + 1) * 512],
                                            start=(ich == 0), stop=(ich == NICH - 1))
                                    s_pe2.inc_ev(mm, pe2m, ("dn", g, dblk, t))

        def phase_attn(h):
            phase_ln_gate(h, src_hbm=False, gate=False)
            with contextlib.ExitStack() as phA:
                _uA = next(_uid)
                def pbA(name, p, f, dt):
                    return phA.enter_context(
                        nc.sbuf_tensor(f"{name}_{_uA}", [p, f], dt))
                kfm = pbA("kfm", 128, NKV * 1280, BF16)
                vtm = pbA("vtm", 128, 10 * 512, BF16)
                qao = pbA("qao", 128, 8 * HALF, BF16)
                wstage = pbA("wstage", 128, 2 * 2048, F32)
                wbf = pbA("wbf", 128, 2 * 2048, BF16)
                for pass_ in range(2):
                    attn_qkv(h, pass_, kfm, vtm, qao, wstage, wbf)
                    if h == 0 and "qao" in DEBUG_TAPS:
                        qt = tap_dram(f"qao{pass_}", (128, 8 * HALF), BF16)
                        with nc.Block() as b:
                            @b.gpsimd
                            def _(e):
                                s_g.inc(e.dma_start(qt[:], qao[:]), 16)
                                e.wait_ge(s_g.h, s_g.n)
                    attn_oproj(h, pass_, qao, wstage)
                if h == 0 and "kv" in DEBUG_TAPS:
                    kt = tap_dram("kfm", (128, NKV * 1280), BF16)
                    vt = tap_dram("vtm", (128, 10 * 512), BF16)
                    with nc.Block() as b:
                        @b.gpsimd
                        def _(e):
                            s_g.inc(e.dma_start(kt[:], kfm[:]), 16)
                            s_g.inc(e.dma_start(vt[:], vtm[:]), 16)
                            e.wait_ge(s_g.h, s_g.n)
                if h == 0:
                    with nc.Block() as b:
                        @b.gpsimd
                        def _(e):
                            s_gkv.inc(e.dma_start(
                                kv_dram[0][:, 0:1024]
                                .rearrange("p (kv s) -> p kv s", kv=4),
                                kfm[:].rearrange("p (kv s) -> p kv s", kv=4)
                                [:, :, 768:1024]), 16)
                            s_gkv.inc(e.dma_start(kv_dram[1][:, 0:1024],
                                                  vtm[:, 6 * 512:8 * 512]), 16)
                            e.wait_ge(s_gkv.h, s_gkv.n)

        def attn_qkv(h, pass_, kfm, vtm, qao, wstage, wbf):
            with contextlib.ExitStack() as ph:
                _u = next(_uid)
                def pb_(name, p, f, dt):
                    return ph.enter_context(
                        nc.sbuf_tensor(f"{name}_{_u}", [p, f], dt))
                csb = pb_("csb", 128, 2 * HALF, F32)
                mskb = pb_("mskb", 128, 384, BF16)
                qraw = pb_("qraw", 128, HALF, F32)
                rot = pb_("rot", 128, HALF, F32)
                xfer = pb_("xfer", 128, HALF, BF16)
                sqtm = pb_("sqtm", 128, 2 * HALF, BF16)
                sscol = pb_("sscol", 128, 2 * 8, F32)
                rstdc = pb_("rstdc", 128, 2 * 8, F32)
                cqc = pb_("cqc", 128, 2 * 8, F32)
                exb = pb_("exb", 128, 6 * 384, F32)
                abf = pb_("abf", 128, 6 * 384, BF16)
                atT = pb_("atT", 128, 6 * 384, BF16)
                cden = pb_("cdena", 128, 6, F32)

                if pass_ == 0:
                    units = ([("k", kv) for kv in range(NKV)]
                             + [("v", kv) for kv in range(NKV)]
                             + [("q", hh) for hh in range(8)])
                else:
                    units = [("q", hh) for hh in range(8, 16)]
                NUq = len(units)
                qk_us = [i for i, (tp, _) in enumerate(units) if tp != "v"]
                q_us = [i for i, (tp, _) in enumerate(units) if tp == "q"]

                wab = [s.n for s in s_wa]
                gcb, gkvb = s_gc.n, s_gkv.n
                peb = s_pe.n

                def att_pipeline(emit_head, emit_tail):
                    for p in range(NT // 2):
                        emit_head(p)
                        if p >= 2:
                            emit_tail(p - 2)
                    emit_tail(NT // 2 - 2)
                    emit_tail(NT // 2 - 1)

                def asl(tl):
                    return ((tl // 2) % 3) * 2 + (tl % 2)

                cast_ev, pe2_ev, act_ev, d2_ev = [], [], [], []
                for ui, (tp, idx) in enumerate(units):
                    cast_ev.append(("w", ui))
                for ui, (tp, idx) in enumerate(units):
                    if tp == "q":
                        break
                    if tp == "k":
                        d2_ev += [("ev", ui), ("qro", ui), ("bf", ui), ("ssm", ui)]
                        act_ev += [("sqT", ui), ("rstd", ui)]
                        pe2_ev += [("trk1", ui), ("trk2", ui)]
                        d2_ev += [("knrm", ui), ("kfmev", ui)]
                    else:
                        d2_ev += [("ev", ui), ("vstg", ui), ("vtm", ui)]
                        pe2_ev.append(("vtr", ui))
                for qi, ui in enumerate(q_us):
                    pe2_ev.append(("trq", ui))
                    d2_ev += [("ev", ui), ("qro", ui), ("bf", ui)]
                    act_ev += [("sqT", ui)]
                    if qi >= 1:
                        pu = q_us[qi - 1]
                        att_pipeline(
                            lambda p: pe2_ev.extend([("sc", pu, 2 * p),
                                                     ("sc", pu, 2 * p + 1)]),
                            lambda p: pe2_ev.extend([("atrp", pu, p),
                                                     ("pv", pu, 2 * p),
                                                     ("pv", pu, 2 * p + 1)]))
                        att_pipeline(
                            lambda p: None,
                            lambda p: d2_ev.extend([("rec", pu, p),
                                                    ("abf", pu, 2 * p),
                                                    ("abf", pu, 2 * p + 1),
                                                    ("atTp", pu, p)]))
                        att_pipeline(
                            lambda p: act_ev.extend([("exp", pu, 2 * p),
                                                     ("exp", pu, 2 * p + 1)]),
                            lambda p: act_ev.extend([("out", pu, 2 * p),
                                                     ("out", pu, 2 * p + 1)]))
                    act_ev.append(("rstd", ui))
                    d2_ev += [("ssm", ui), ("cq", ui)]
                pu = q_us[-1]
                att_pipeline(
                    lambda p: pe2_ev.extend([("sc", pu, 2 * p),
                                             ("sc", pu, 2 * p + 1)]),
                    lambda p: pe2_ev.extend([("atrp", pu, p),
                                             ("pv", pu, 2 * p),
                                             ("pv", pu, 2 * p + 1)]))
                att_pipeline(
                    lambda p: None,
                    lambda p: d2_ev.extend([("rec", pu, p),
                                            ("abf", pu, 2 * p),
                                            ("abf", pu, 2 * p + 1),
                                            ("atTp", pu, p)]))
                att_pipeline(
                    lambda p: act_ev.extend([("exp", pu, 2 * p),
                                             ("exp", pu, 2 * p + 1)]),
                    lambda p: act_ev.extend([("out", pu, 2 * p),
                                             ("out", pu, 2 * p + 1)]))

                cmap = s_cast.evmap(cast_ev)
                pe2m = s_pe2.evmap(pe2_ev)
                actm = s_act.evmap(act_ev)
                d2map = s_dve2.evmap(d2_ev)

                def kwin(gt):
                    c0 = max(0, gt - 2)
                    return c0, gt - c0 + 1

                with nc.Block() as b:
                    @b.gpsimd
                    def _(e):
                        s_gc.inc(e.dma_start(csb[:, 0:HALF],
                                             cs_in[:, h * HALF:(h + 1) * HALF]), 16)
                        s_gc.inc(e.dma_start(
                            csb[:, HALF:],
                            cs_in[:, SEQ + h * HALF:SEQ + (h + 1) * HALF]), 16)
                        s_gc.inc(e.dma_start(mskb[:], mskh_in[:]), 16)
                        if pass_ == 0 and h == 1:
                            s_gkv.inc(e.dma_start(
                                kfm[:].rearrange("p (kv s) -> p kv s", kv=4)
                                [:, :, 0:256],
                                kv_dram[0][:, 0:1024]
                                .rearrange("p (kv s) -> p kv s", kv=4)), 16)
                            s_gkv.inc(e.dma_start(vtm[:, 0:1024],
                                                  kv_dram[1][:, 0:1024]), 16)

                    @b.sync
                    def _(e):
                        for ui, (tp, idx) in enumerate(units):
                            if ui >= 2:
                                e.wait_ge(s_cast.h, cmap[("w", ui - 2)])
                            if tp == "k":
                                src = wkv_in[idx]
                            elif tp == "v":
                                src = wkv_in[NKV + idx]
                            else:
                                src = wq_in[idx]
                            s_wa[ui % 2].inc(e.dma_start(
                                wstage[:, (ui % 2) * 2048:(ui % 2 + 1) * 2048]
                                .rearrange("p (dk a) -> p dk a", dk=NDK), src), 16)

                    @b.scalar
                    def _(e):
                        wt = Waiter()
                        def a_cast(ui):
                            wt.w(e, s_wa[ui % 2],
                                      wab[ui % 2] + 16 * (ui // 2 + 1))
                            if ui >= 2:
                                wt.w(e, s_pe, peb + 2 * (ui - 2) + 2)
                            s_cast.inc_ev(e.activation(
                                wbf[:, (ui % 2) * 2048:(ui % 2 + 1) * 2048],
                                wstage[:, (ui % 2) * 2048:(ui % 2 + 1) * 2048],
                                AF.Copy), cmap, ("w", ui))

                        def a_sq(ui, tp, qi):
                            wt.w(e, s_pe2, pe2m[("trq" if tp == "q" else "trk1",
                                                     ui)])
                            if tp == "q" and qi >= 2:
                                wt.w(e, s_dve2, d2map[("ssm", q_us[qi - 2])])
                            sl = (qi % 2) if tp == "q" else 0
                            s_act.inc_ev(e.activation(
                                sqtm[:, sl * HALF:(sl + 1) * HALF],
                                psb[0][:], AF.Square), actm, ("sqT", ui))

                        def a_rstd(ui, tp, qi):
                            sl = (qi % 2) if tp == "q" else 0
                            wt.w(e, s_dve2, d2map[("ssm", ui)])
                            e.activation(rstdc[:, sl * 8:(sl + 1) * 8],
                                         sscol[:, sl * 8:(sl + 1) * 8], AF.Ln)
                            s_act.inc_ev(e.activation(
                                rstdc[:, sl * 8:(sl + 1) * 8],
                                rstdc[:, sl * 8:(sl + 1) * 8],
                                AF.Exp, scale=-0.5), actm, ("rstd", ui))

                        def a_att(ui, idx):
                            hl = idx - 8 * pass_
                            qi = q_us.index(ui)
                            sl = qi % 2

                            def a_head(p):
                                wt.w(e, s_pe2, pe2m[("sc", ui, 2 * p + 1)])
                                for tl in (2 * p, 2 * p + 1):
                                    wq_ = (min(h * 8 + tl, 2) + 1) * 128
                                    s_act.inc_ev(e.activation(
                                        exb[:, asl(tl) * 384:
                                            asl(tl) * 384 + wq_],
                                        psf[(p % 2) * 2 + tl % 2][:, 0:wq_],
                                        AF.Exp,
                                        scale=cqc[:, sl * 8 + tl:sl * 8 + tl + 1],
                                        bias=tempnb[:, idx:idx + 1],
                                        accum_out=cden[:, asl(tl):
                                                       asl(tl) + 1]),
                                        actm, ("exp", ui, tl))

                            def a_tail(p):
                                wt.w(e, s_pe2, pe2m[("pv", ui, 2 * p + 1)])
                                for tl in (2 * p, 2 * p + 1):
                                    s_act.inc_ev(e.activation(
                                        qao[:, hl * HALF + tl * 128:
                                            hl * HALF + (tl + 1) * 128],
                                        psf[4 + tl % 2][:, 0:128], AF.Copy),
                                        actm, ("out", ui, tl))

                            att_pipeline(a_head, a_tail)

                        for ui, (tp, idx) in enumerate(units):
                            if tp == "q":
                                break
                            a_cast(ui)
                            if tp == "k":
                                a_sq(ui, tp, 0)
                                a_rstd(ui, tp, 0)
                        for qi, ui in enumerate(q_us):
                            tp, idx = units[ui]
                            a_cast(ui)
                            a_sq(ui, tp, qi)
                            if qi >= 1:
                                a_att(q_us[qi - 1], units[q_us[qi - 1]][1])
                            a_rstd(ui, tp, qi)
                        a_att(q_us[-1], units[q_us[-1]][1])

                    @b.tensor
                    def _(e):
                        wt = Waiter()
                        bt = Banks(wt)

                        def p_proj(ui, tp):
                            wt.w(e, s_cast, cmap[("w", ui)])
                            if ui == 0:
                                wt.w(e, s_gc, s_gc.n)
                                wt.w(e, s_gkv, s_gkv.n)
                            for tb in range(2):
                                bank = (ui % 2) * 2 + tb
                                bt.acquire(e, ("psf", bank),
                                           (s_dve2, d2map[("ev", ui)]))
                                for dk in range(NDK):
                                    mm = e.matmul(
                                        psf[bank][:],
                                        wbf[:, (ui % 2) * 2048 + dk * 128:
                                            (ui % 2) * 2048 + (dk + 1) * 128],
                                        xln[:, dk * HALF + tb * 512:
                                            dk * HALF + (tb + 1) * 512],
                                        start=(dk == 0), stop=(dk == NDK - 1))
                                s_pe.inc(mm)

                        def p_tr(ui, tp, idx, qi):
                            wt.w(e, s_dve2, d2map[("bf", ui)])
                            src_fm = (qao[:, (idx - 8 * pass_) * HALF:
                                          (idx - 8 * pass_ + 1) * HALF]
                                      if tp == "q" else xfer)
                            evac1 = ((s_act, actm[("sqT", ui)]) if tp == "q"
                                     else (s_dve2, d2map[("knrm", ui)]))
                            bt.acquire(e, ("psb", 0), evac1)
                            for t in range(NT):
                                mm = e.matmul(psb[0][:, t * 128:(t + 1) * 128],
                                              src_fm[:, t * 128:(t + 1) * 128],
                                              idb[:], start=(t == 0),
                                              stop=(t == NT - 1), is_transpose=True)
                            s_pe2.inc_ev(mm, pe2m,
                                         ("trq" if tp == "q" else "trk1", ui))

                        def p_att(ui, idx):
                            kv = idx // 4
                            hl = idx - 8 * pass_
                            wt.w(e, s_dve2, d2map[("cq", ui)])
                            kb = 768 * h

                            def pe_head(p):
                                for tl in (2 * p, 2 * p + 1):
                                    gt = h * 8 + tl
                                    c0, nch = kwin(gt)
                                    w = nch * 128
                                    moff = (3 - nch) * 128
                                    bank = (p % 2) * 2 + tl % 2
                                    bt.acquire(e, ("psf", bank),
                                               (s_act, actm[("exp", ui, tl)]))
                                    e.matmul(psf[bank][:, 0:w], idb[:],
                                             mskb[:, moff:moff + w],
                                             start=True, stop=False)
                                    s_pe2.inc_ev(e.matmul(
                                        psf[bank][:, 0:w],
                                        qao[:, hl * HALF + tl * 128:
                                            hl * HALF + (tl + 1) * 128],
                                        kfm[:, kv * 1280 + c0 * 128 - kb:
                                            kv * 1280 + c0 * 128 - kb + w],
                                        start=False, stop=True),
                                        pe2m, ("sc", ui, tl))

                            def pe_tail(p):
                                wt.w(e, s_dve2, d2map[("abf", ui, 2 * p + 1)])
                                bt.acquire(e, ("psb", p % 2),
                                           (s_dve2, d2map[("atTp", ui, p)]))
                                mms = [(tl, c)
                                       for tl in (2 * p, 2 * p + 1)
                                       for c in range(kwin(h * 8 + tl)[1])]
                                for i, (tl, c) in enumerate(mms):
                                    mm = e.matmul(
                                        psb[p % 2][0:128,
                                                   (tl % 2) * 384 + c * 128:
                                                   (tl % 2) * 384 + (c + 1) * 128],
                                        abf[:, asl(tl) * 384 + c * 128:
                                            asl(tl) * 384 + (c + 1) * 128],
                                        idb[:], start=(i == 0),
                                        stop=(i == len(mms) - 1),
                                        is_transpose=True)
                                s_pe2.inc_ev(mm, pe2m, ("atrp", ui, p))
                                wt.w(e, s_dve2, d2map[("atTp", ui, p)])
                                for tl in (2 * p, 2 * p + 1):
                                    c0, nch = kwin(h * 8 + tl)
                                    bt.acquire(e, ("psf", 4 + tl % 2),
                                               (s_act, actm[("out", ui, tl)]))
                                    for c in range(nch):
                                        mm = e.matmul(
                                            psf[4 + tl % 2][:, 0:128],
                                            vtm[:, (c0 + c - 6 * h) * 512
                                                + kv * 128:
                                                (c0 + c - 6 * h) * 512
                                                + (kv + 1) * 128],
                                            atT[:, (p % 3) * 768
                                                + (tl % 2) * 384 + c * 128:
                                                (p % 3) * 768
                                                + (tl % 2) * 384 + (c + 1) * 128],
                                            start=(c == 0), stop=(c == nch - 1))
                                    s_pe2.inc_ev(mm, pe2m, ("pv", ui, tl))

                            att_pipeline(pe_head, pe_tail)

                        for ui, (tp, idx) in enumerate(units):
                            if tp == "q":
                                break
                            p_proj(ui, tp)
                            if tp == "v":
                                wt.w(e, s_dve2, d2map[("vstg", ui)])
                                bt.acquire(e, ("psb", 0),
                                           (s_dve2, d2map[("vtm", ui)]))
                                for t in range(NT):
                                    mm = e.matmul(psb[0][:, t * 128:(t + 1) * 128],
                                                  xfer[:, t * 128:(t + 1) * 128],
                                                  idb[:], start=(t == 0),
                                                  stop=(t == NT - 1),
                                                  is_transpose=True)
                                s_pe2.inc_ev(mm, pe2m, ("vtr", ui))
                            else:
                                p_tr(ui, tp, idx, 0)
                                wt.w(e, s_dve2, d2map[("knrm", ui)])
                                bt.acquire(e, ("psb", 1),
                                           (s_dve2, d2map[("kfmev", ui)]))
                                for t in range(NT):
                                    mm = e.matmul(psb[1][:, t * 128:(t + 1) * 128],
                                                  xfer[:, t * 128:(t + 1) * 128],
                                                  idb[:], start=(t == 0),
                                                  stop=(t == NT - 1),
                                                  is_transpose=True)
                                s_pe2.inc_ev(mm, pe2m, ("trk2", ui))
                        for qi, ui in enumerate(q_us):
                            tp, idx = units[ui]
                            p_proj(ui, tp)
                            p_tr(ui, tp, idx, qi)
                            if qi >= 1:
                                p_att(q_us[qi - 1], units[q_us[qi - 1]][1])
                        p_att(q_us[-1], units[q_us[-1]][1])

                    @b.vector
                    def _(e):
                        wt = Waiter()
                        def v_d1(ui, tp, idx, qi):
                            qkprev = qk_us.index(ui) if ui in qk_us else None
                            if tp in ("k", "q") and qkprev is not None and qkprev >= 1:
                                wt.w(e, s_act, actm[("sqT", qk_us[qkprev - 1])])
                            for tb in range(2):
                                wt.w(e, s_pe, peb + 2 * ui + tb + 1)
                                e.tensor_copy(qraw[:, tb * 512:(tb + 1) * 512],
                                              psf[(ui % 2) * 2 + tb][:])
                            if tp == "v":
                                vs(e)
                                s_dve2.inc_ev(e.engine_nop(), d2map, ("ev", ui))
                                if idx == 0:
                                    wt.w(e, s_act, actm[("sqT", qk_us[NKV - 1])])
                                else:
                                    wt.w(e, s_pe2, pe2m[("vtr", ui - 1)])
                                e.tensor_copy(xfer[:], qraw[:])
                                vs(e)
                                s_dve2.inc_ev(e.engine_nop(), d2map, ("vstg", ui))
                                wt.w(e, s_pe2, pe2m[("vtr", ui)])
                                dst = vtm[:].rearrange("p (t c) -> p t c", c=512)[
                                    :, h * 2:h * 2 + 8, idx * 128:(idx + 1) * 128]
                                s_dve2.inc_ev(
                                    e.tensor_copy(dst, psb[0][:]
                                                  .rearrange("p (t c) -> p t c",
                                                             c=128)),
                                    d2map, ("vtm", ui))
                                return
                            if ui == 0:
                                wt.w(e, s_gc, s_gc.n)
                            vs(e)
                            s_dve2.inc_ev(e.engine_nop(), d2map, ("ev", ui))
                            e.tensor_copy(rot[0:64, :], qraw[64:128, :])
                            e.tensor_copy(rot[64:128, :], qraw[0:64, :])
                            vs(e)
                            e.tensor_mul(qraw[:], qraw[:], csb[:, 0:HALF])
                            e.tensor_mul(rot[:], rot[:], csb[:, HALF:])
                            vs(e)
                            s_dve2.inc_ev(e.tensor_add(qraw[:], qraw[:], rot[:]),
                                          d2map, ("qro", ui))
                            vs(e)
                            if tp == "q":
                                hl = idx - 8 * pass_
                                s_dve2.inc_ev(
                                    e.tensor_scalar(
                                        qao[:, hl * HALF:(hl + 1) * HALF],
                                        qraw[:], sgnb[:, idx:idx + 1], None,
                                        ALU.mult),
                                    d2map, ("bf", ui))
                            else:
                                s_dve2.inc_ev(e.tensor_copy(xfer[:], qraw[:]),
                                              d2map, ("bf", ui))

                        def v_norm(ui, tp, qi):
                            sl = (qi % 2) if tp == "q" else 0
                            wt.w(e, s_act, actm[("sqT", ui)])
                            e.reduce_sum(
                                sscol[:, sl * 8:(sl + 1) * 8]
                                .rearrange("p (t o) -> p t o", t=8),
                                sqtm[:, sl * HALF:(sl + 1) * HALF]
                                .rearrange("p (t c) -> p t c", t=8),
                                axis=AX.X)
                            vs(e)
                            s_dve2.inc_ev(e.tensor_scalar(
                                sscol[:, sl * 8:(sl + 1) * 8],
                                sscol[:, sl * 8:(sl + 1) * 8],
                                1e-24, None, ALU.max), d2map, ("ssm", ui))

                        def v_katt(ui, idx):
                            wt.w(e, s_act, actm[("rstd", ui)])
                            for t in range(NT):
                                e.tensor_scalar(xfer[:, t * 128:(t + 1) * 128],
                                                psb[0][:, t * 128:(t + 1) * 128],
                                                rstdc[:, t:t + 1], None, ALU.mult)
                            s_dve2.inc_ev(e.engine_nop(), d2map, ("knrm", ui))
                            wt.w(e, s_pe2, pe2m[("trk2", ui)])
                            s_dve2.inc_ev(e.tensor_copy(
                                kfm[:, idx * 1280 + 256 * h:
                                    idx * 1280 + 256 * h + HALF],
                                psb[1][:]), d2map, ("kfmev", ui))

                        def v_cq(ui, idx, qi):
                            sl = qi % 2
                            wt.w(e, s_act, actm[("rstd", ui)])
                            s_dve2.inc_ev(e.tensor_scalar(
                                cqc[:, sl * 8:(sl + 1) * 8],
                                rstdc[:, sl * 8:(sl + 1) * 8],
                                tempab[:, idx:idx + 1], None, ALU.mult),
                                d2map, ("cq", ui))

                        def v_att(ui, idx):
                            def v_head(p):
                                pass

                            def v_tail(p):
                                t0, t1 = 2 * p, 2 * p + 1
                                c = asl(t0)
                                wt.w(e, s_act, actm[("exp", ui, t1)])
                                e.reciprocal(cden[:, c:c + 1], cden[:, c:c + 1])
                                s_dve2.inc_ev(e.reciprocal(
                                    cden[:, c + 1:c + 2], cden[:, c + 1:c + 2]),
                                    d2map, ("rec", ui, p))
                                for tl in (t0, t1):
                                    wq_ = (min(h * 8 + tl, 2) + 1) * 128
                                    s_dve2.inc_ev(e.tensor_scalar(
                                        abf[:, asl(tl) * 384:
                                            asl(tl) * 384 + wq_],
                                        exb[:, asl(tl) * 384:
                                            asl(tl) * 384 + wq_],
                                        cden[:, asl(tl):asl(tl) + 1], None,
                                        ALU.mult), d2map, ("abf", ui, tl))
                                wt.w(e, s_pe2, pe2m[("atrp", ui, p)])
                                s_dve2.inc_ev(e.tensor_copy(
                                    atT[:, (p % 3) * 768:(p % 3) * 768 + 768],
                                    psb[p % 2][:, 0:768]),
                                    d2map, ("atTp", ui, p))

                            att_pipeline(v_head, v_tail)

                        for ui, (tp, idx) in enumerate(units):
                            if tp == "q":
                                break
                            v_d1(ui, tp, idx, 0)
                            if tp == "k":
                                v_norm(ui, tp, 0)
                                v_katt(ui, idx)
                        for qi, ui in enumerate(q_us):
                            tp, idx = units[ui]
                            v_d1(ui, tp, idx, qi)
                            if qi >= 1:
                                v_att(q_us[qi - 1], units[q_us[qi - 1]][1])
                            v_norm(ui, tp, qi)
                            v_cq(ui, idx, qi)
                        v_att(q_us[-1], units[q_us[-1]][1])

        def attn_oproj(h, pass_, qao, wstage):
            with contextlib.ExitStack() as ph:
                wobf = ph.enter_context(
                    nc.sbuf_tensor(f"wobf_{next(_uid)}", [128, 2 * 2048], BF16))
                wab = [s.n for s in s_wa]
                qs = [2 * pass_, 2 * pass_ + 1]
                cast_ev = [("q", dblk, j) for dblk in range(4) for j in range(2)]
                pe2_ev = [("o", dblk, t) for dblk in range(4) for t in range(NT)]
                d2_ev = [("oev", dblk, t) for dblk in range(4) for t in range(NT)]
                cmap = s_cast.evmap(cast_ev)
                pe2m = s_pe2.evmap(pe2_ev)
                d2map = s_dve2.evmap(d2_ev)

                with nc.Block() as b:
                    @b.sync
                    def _(e):
                        for dblk in range(4):
                            for j in range(2):
                                ui = dblk * 2 + j
                                if ui >= 2:
                                    pd, pj = divmod(ui - 2, 2)
                                    e.wait_ge(s_cast.h, cmap[("q", pd, pj)])
                                s_wa[ui % 2].inc(e.dma_start(
                                    wstage[:, (ui % 2) * 2048:(ui % 2 + 1) * 2048]
                                    .rearrange("p (a c) -> p a c", a=4),
                                    wo_in[dblk, qs[j]]), 16)

                    @b.scalar
                    def _(e):
                        for dblk in range(4):
                            for j in range(2):
                                ui = dblk * 2 + j
                                e.wait_ge(s_wa[ui % 2].h,
                                          wab[ui % 2] + 16 * (ui // 2 + 1))
                                if dblk >= 1:
                                    e.wait_ge(s_pe2.h,
                                              pe2m[("o", dblk - 1, NT - 1)])
                                s_cast.inc_ev(e.activation(
                                    wobf[:, j * 2048:(j + 1) * 2048],
                                    wstage[:, (ui % 2) * 2048:(ui % 2 + 1) * 2048],
                                    AF.Copy), cmap, ("q", dblk, j))

                    @b.tensor
                    def _(e):
                        bt = Banks()
                        for dblk in range(4):
                            for t in range(NT):
                                gi = dblk * 8 + t
                                bt.acquire(e, ("psf", gi % 6),
                                           (s_dve2, d2map[("oev", dblk, t)]))
                                for j in range(2):
                                    if t == 0:
                                        e.wait_ge(s_cast.h, cmap[("q", dblk, j)])
                                    for a in range(4):
                                        hl = j * 4 + a
                                        mm = e.matmul(
                                            psf[gi % 6][:],
                                            qao[:, hl * HALF + t * 128:
                                                hl * HALF + (t + 1) * 128],
                                            wobf[:, j * 2048 + a * 512:
                                                 j * 2048 + (a + 1) * 512],
                                            start=(j == 0 and a == 0),
                                            stop=(j == 1 and a == 3))
                                s_pe2.inc_ev(mm, pe2m, ("o", dblk, t))

                    @b.vector
                    def _(e):
                        for dblk in range(4):
                            for t in range(NT):
                                gi = dblk * 8 + t
                                e.wait_ge(s_pe2.h, pe2m[("o", dblk, t)])
                                s_dve2.inc_ev(e.tensor_add(
                                    r_sb[:, t * D + dblk * 512:t * D + (dblk + 1) * 512],
                                    psf[gi % 6][:],
                                    r_sb[:, t * D + dblk * 512:t * D + (dblk + 1) * 512]),
                                    d2map, ("oev", dblk, t))

        def phase_store(h):
            with nc.Block() as b:
                @b.gpsimd
                def _(e):
                    for t in range(NT):
                        s_st.inc(e.dma_start(
                            out[h * HALF + t * 128:h * HALF + (t + 1) * 128, :],
                            r_sb[:, t * D:(t + 1) * D]), 16)

        def tap_phase(label, h):
            if h != 0 or label not in DEBUG_TAPS:
                return
            rt = tap_dram(label + "_r", (128, NT * D))
            xt = tap_dram(label + "_xln", (128, NDK * HALF), BF16)
            sct = tap_dram(label + "_scal", (128, 64))
            with nc.Block() as b:
                @b.gpsimd
                def _(e):
                    n0 = s_g.n
                    s_g.inc(e.dma_start(rt[:], r_sb[:]), 16)
                    s_g.inc(e.dma_start(xt[:], xln[:]), 16)
                    s_g.inc(e.dma_start(sct[:], scalbuf[:]), 16)
                    e.wait_ge(s_g.h, s_g.n)

        for h in range(2):
            phase_ln_gate(h, src_hbm=True, gate=True)
            tap_phase("ln1", h)
            phase_ffn(h, build_mw=(h == 0))
            tap_phase("ffn1", h)
            phase_attn(h)
            tap_phase("attn", h)
            phase_ln_gate(h, src_hbm=False, gate=True)
            tap_phase("ln3", h)
            phase_ffn(h, build_mw=False)
            tap_phase("ffn2", h)
            phase_store(h)

        with nc.Block() as b:
            @b.gpsimd
            def _(e):
                e.wait_ge(s_st.h, s_st.n)

    return nc


def _stage_weights(i):
    f = np.float32
    gpw = np.asarray(i["L0_ffn_gate_proj_w"], f)
    upw = np.asarray(i["L0_ffn_up_proj_w"], f)
    dpw = np.asarray(i["L0_ffn_down_proj_w"], f)
    Wq = np.asarray(i["L0_attn_Wq"], f)
    Wkv = np.asarray(i["L0_attn_Wkv"], f)
    Wo = np.asarray(i["L0_attn_Wo"], f)
    gw = np.asarray(i["L0_ffn_gate_w"], f)
    mkw = np.asarray(i["L0_ffn_mask_up_w"], f)

    def gu_tiles(w):
        return np.ascontiguousarray(
            w.reshape(NG, NICH, 128, NDK, 128).transpose(0, 1, 4, 3, 2))

    return dict(
        wg=gu_tiles(gpw),
        wu=gu_tiles(upw),
        wd=np.ascontiguousarray(
            dpw.reshape(NG, 4, 512, NICH, 128).transpose(0, 1, 3, 4, 2)),
        wq=np.ascontiguousarray(
            Wq.reshape(NH, 128, NDK, 128).transpose(0, 3, 2, 1)),
        wkv=np.ascontiguousarray(
            Wkv.reshape(8, 128, NDK, 128).transpose(0, 3, 2, 1)),
        wo=np.ascontiguousarray(
            Wo.reshape(4, 512, 4, 4, 128).transpose(0, 2, 4, 3, 1)),
        gw=np.ascontiguousarray(
            gw.T.reshape(NDK, 128, 64).transpose(1, 0, 2)).reshape(128, NDK * 64),
        mkw=np.ascontiguousarray(
            mkw.reshape(NG, NDK, 128, 8).transpose(2, 0, 1, 3)).reshape(128, NG * NDK * 8),
    )


def _constants():
    f = np.float32
    pos = np.arange(SEQ, dtype=np.float64)
    invfreq = 1.0 / (10000.0 ** (np.arange(0, DH, 2, dtype=np.float64) / DH))
    ang = np.concatenate([invfreq, invfreq])[:, None] * pos[None, :]
    cos = np.cos(ang).astype(f)
    sin = np.sin(ang).astype(f)
    sin[:64] = -sin[:64]
    cs = np.concatenate([cos, sin], axis=1)

    ii = np.arange(128)[:, None]
    jj = np.arange(384)[None, :]
    mI = np.where((jj > ii) & (jj <= ii + 256), 0.0, NEG).astype(f)

    return dict(
        cs=cs,
        mskh=mI.astype(bfdt),
        identb=np.eye(128).astype(bfdt),
        identf=np.eye(128, dtype=f),
        iota64=np.arange(64, dtype=f).reshape(64, 1),
        onesrow=np.ones((1, 128), f),
        onescol=np.ones((128, 1), bfdt),
    )


_NC_CACHE = {}


def kernel(**inputs):
    x = np.asarray(inputs["L0_attn_x"], np.float32)
    shared = _stage_weights(inputs)
    shared.update(_constants())
    shared["qkt"] = np.ascontiguousarray(np.broadcast_to(
        np.asarray(inputs["L0_attn_qk_temp"], np.float32).reshape(1, NH),
        (128, NH)))
    shared["inv"] = np.ascontiguousarray(np.broadcast_to(
        np.asarray(inputs["L0_ffn_inv_map"]).astype(np.int32).reshape(1, 64),
        (64, 64)))

    if "nc" not in _NC_CACHE:
        _NC_CACHE["nc"] = build_program()
    nc = _NC_CACHE["nc"]

    in_maps = []
    for c in range(8):
        m = dict(shared)
        m["x"] = np.ascontiguousarray(x[c])
        in_maps.append(m)
    res = run_bass_kernel_spmd(nc, in_maps, core_ids=list(range(8)))
    return np.stack([np.asarray(res.results[c]["out"]) for c in range(8)], axis=0)


# version 15 (final)
# speedup vs baseline: 1.0028x; 1.0028x over previous
import contextlib
import numpy as np
import ml_dtypes

import concourse.bass as bass
import concourse.mybir as mybir
from concourse.bass_utils import run_bass_kernel_spmd

F32 = mybir.dt.float32
BF16 = mybir.dt.bfloat16
I32 = mybir.dt.int32
AF = mybir.ActivationFunctionType
ALU = mybir.AluOpType
AX = mybir.AxisListType
bfdt = ml_dtypes.bfloat16

SEQ, D, HALF, NT = 2048, 2048, 1024, 8
NDK = 16
NG, NI, NICH = 8, 1408, 11
NH, NKV, DH = 16, 4, 128
NEG = -1.0e30
DEBUG_TAPS = ()


class S:
    def __init__(self, stack, nc, name):
        self.h = stack.enter_context(nc.semaphore(name))
        self.n = 0

    def inc(self, instr, k=1):
        instr.then_inc(self.h, k)
        self.n += k
        return self.n

    def evmap(self, schedule, step=1):
        m = {}
        n = self.n
        for key in schedule:
            n += step
            m[key] = n
        return m

    def inc_ev(self, instr, m, key, step=1):
        assert self.n + step == m[key], (key, self.n, m[key])
        return self.inc(instr, step)


class Waiter:

    def __init__(self):
        self.seen = {}

    def w(self, e, sem, cnt):
        if self.seen.get(id(sem), -1) >= cnt:
            return
        e.wait_ge(sem.h, cnt)
        self.seen[id(sem)] = cnt


class Banks:

    def __init__(self, wt=None):
        self.last = {}
        self.wt = wt

    def acquire(self, e, bank_id, evac):
        prev = self.last.get(bank_id)
        if prev is not None:
            sem, cnt = prev
            if self.wt is not None:
                self.wt.w(e, sem, cnt)
            else:
                e.wait_ge(sem.h, cnt)
        self.last[bank_id] = evac


def build_program():
    nc = bass.Bass("TRN2", target_bir_lowering=False)

    def din(name, shape, dt=F32):
        return nc.dram_tensor(name, list(shape), dt, kind="ExternalInput").ap()

    x_in = din("x", (SEQ, D))
    wg_in = din("wg", (NG, NICH, 128, NDK, 128))
    wu_in = din("wu", (NG, NICH, 128, NDK, 128))
    wd_in = din("wd", (NG, 4, NICH, 128, 512))
    wq_in = din("wq", (NH, 128, NDK, 128))
    wkv_in = din("wkv", (NKV * 2, 128, NDK, 128))
    wo_in = din("wo", (4, 4, 128, 4, 512))
    gw_in = din("gw", (128, NDK * 64))
    mkw_in = din("mkw", (128, NG * NDK * 8))
    qkt_in = din("qkt", (128, NH))
    inv_in = din("inv", (64, 64), I32)
    cs_in = din("cs", (128, 2 * SEQ))
    mskh_in = din("mskh", (128, 384), BF16)
    idb_in = din("identb", (128, 128), BF16)
    idf_in = din("identf", (128, 128))
    iota_in = din("iota64", (64, 1))
    onr_in = din("onesrow", (1, 128))
    onc_in = din("onescol", (128, 1), BF16)

    out = nc.dram_tensor("out", [SEQ, D], F32, kind="ExternalOutput").ap()

    soft_dram = nc.dram_tensor("soft_scr", [NG, 8, SEQ], BF16, kind="Internal").ap()
    mw_dram = nc.dram_tensor("mw_scr", [2, NG, 8, NI], BF16, kind="Internal").ap()
    kv_dram = nc.dram_tensor("kv_scr", [2, 128, 4096], BF16, kind="Internal").ap()

    import itertools
    _uid = itertools.count()

    tap_outs = {}

    def tap_dram(label, shape, dt=F32):
        t = nc.dram_tensor("tap_" + label, list(shape), dt, kind="ExternalOutput")
        tap_outs[label] = t.ap()
        return tap_outs[label]

    st = contextlib.ExitStack()
    with st:
        def sb(name, p, f, dt):
            return st.enter_context(nc.sbuf_tensor(name, [p, f], dt))

        r_sb = sb("r_sb", 128, NT * D, F32)
        xln = sb("xln", 128, NDK * HALF, BF16)
        scalbuf = sb("scalbuf", 128, 64, F32)
        tempb = sb("tempb", 128, 16, F32)
        tempnb = sb("tempnb", 128, 16, F32)
        tempab = sb("tempab", 128, 16, F32)
        sgnb = sb("sgnb", 128, 16, F32)
        Pm = sb("Pm", 64, 64, F32)
        idb = sb("idb", 128, 128, BF16)
        idf = sb("idf", 128, 128, F32)
        iota = sb("iota", 64, 1, F32)
        onr = sb("onr", 1, 128, F32)
        onc = sb("onc", 128, 1, BF16)
        invf = sb("invf", 64, 64, F32)
        invi = sb("invi", 64, 64, I32)
        junkc = sb("junkc", 128, 1, F32)
        epsc = sb("epsc", 128, 1, F32)

        psf = [st.enter_context(nc.psum_tensor(f"psf{i}", [128, 512], F32))
               for i in range(6)]
        psb = [st.enter_context(nc.psum_tensor(f"psb{i}", [128, 1024], BF16))
               for i in range(2)]

        s_x = S(st, nc, "s_x")
        s_w = S(st, nc, "s_w")
        s_g = S(st, nc, "s_g")
        s_st = S(st, nc, "s_st")
        s_pe = S(st, nc, "s_pe")
        s_pe2 = S(st, nc, "s_pe2")
        s_cast = S(st, nc, "s_cast")
        s_act = S(st, nc, "s_act")
        s_dve = S(st, nc, "s_dve")
        s_dve2 = S(st, nc, "s_dve2")
        s_dve3 = S(st, nc, "s_dve3")
        s_sv = S(st, nc, "s_sv")
        s_pl = S(st, nc, "s_pl")
        s_x2 = [S(st, nc, f"s_x2_{i}") for i in range(2)]
        s_wf = [S(st, nc, f"s_wf_{i}") for i in range(3)]
        s_wa = [S(st, nc, f"s_wa_{i}") for i in range(2)]
        s_gs = [S(st, nc, f"s_gs_{i}") for i in range(2)]
        s_gml = [S(st, nc, f"s_gml_{i}") for i in range(2)]
        s_gms = [S(st, nc, f"s_gms_{i}") for i in range(2)]
        s_ss = [S(st, nc, f"s_ss_{i}") for i in range(2)]
        s_gc = S(st, nc, "s_gc")
        s_gkv = S(st, nc, "s_gkv")
        s_gw = S(st, nc, "s_gw")

        def vs(e):
            n = s_sv.inc(e.engine_nop())
            e.wait_ge(s_sv.h, n)

        with nc.Block() as b:
            @b.gpsimd
            def _(e):
                for dst, src in [(idb, idb_in), (idf, idf_in), (iota, iota_in),
                                 (onr, onr_in), (onc, onc_in), (tempb, qkt_in),
                                 (invi, inv_in)]:
                    s_g.inc(e.dma_start(dst[:], src[:]), 16)
                e.wait_ge(s_g.h, s_g.n)

        with nc.Block() as b:
            @b.vector
            def _(e):
                e.tensor_scalar(tempnb[:], tempb[:], -1.0, None, ALU.mult)
                e.tensor_copy(invf[:], invi[:])
                e.memset(epsc[:], 1e-5)
                e.tensor_tensor(tempnb[:], tempnb[:], tempb[:], ALU.min)

        with nc.Block() as b:
            @b.vector
            def _(e):
                e.tensor_scalar(tempab[:], tempnb[:], -1.0, None, ALU.mult)
                e.tensor_scalar(Pm[:], invf[:], iota[:], None, ALU.is_equal)
                e.tensor_scalar(sgnb[:], tempab[:], 1e-30, None, ALU.max)
                vs(e)
                e.reciprocal(sgnb[:], sgnb[:])
                vs(e)
                e.tensor_tensor(sgnb[:], sgnb[:], tempb[:], ALU.mult)

        def phase_ln_gate(h, src_hbm, gate):
            with contextlib.ExitStack() as ph:
                _u = next(_uid)
                def pb_(name, p, f, dt):
                    return ph.enter_context(
                        nc.sbuf_tensor(f"{name}_{_u}", [p, f], dt))
                xtile = pb_("xtile", 128, 2 * D, F32) if src_hbm else None
                sq = pb_("sq", 128, 2 * D, F32)
                xlt = pb_("xlt", 128, 2 * D, BF16)
                cvar = pb_("cvar", 128, 2, F32)
                cmean = pb_("cmean", 128, 2, F32)
                crstd = pb_("crstd", 128, 2, F32)
                ccol = pb_("ccol", 128, 8, F32)
                if gate:
                    gwt = pb_("gwt", 128, NDK * 64, BF16)
                    gstg = pb_("gstg", 128, NDK * 64, F32)
                    scs = pb_("scs", 64, 2 * 128, F32)
                    sc = pb_("sc", 128, 2 * 64, F32)
                    sce = pb_("sce", 128, 2 * 64, F32)
                    cden = pb_("cden", 128, 2, F32)
                    wesm = pb_("wesm", 128, 2 * 64, F32)
                    vv = pb_("vv", 128, 2 * 64, F32)
                    mskt = pb_("mskt", 128, 2 * 64, F32)
                    wee = pb_("wee", 128, 2 * 64, F32)
                    weT = pb_("weT", 64, 2 * 128, F32)
                    flat = pb_("flat", 128, 2 * 64, F32)
                    fmn = pb_("fmn", 128, 2 * 64, F32)
                    gs8 = pb_("gs8", 128, 2 * 8, F32)
                    gm8 = pb_("gm8", 128, 2 * 8, F32)
                    gr8 = pb_("gr8", 128, 2 * 8, F32)
                    gsm = pb_("gsm", 128, 2 * 64, F32)
                    gex = pb_("gex", 128, 2 * 64, F32)
                    sfb = pb_("sfb", 128, 2 * 64, BF16)
                    sstg = pb_("sstg", 8, 2 * 1024, BF16)

                xb2 = [s.n for s in s_x2]
                ssb = [s.n for s in s_ss]
                gwb = s_gw.n
                db, pb0, stb = s_dve.n, s_pe.n, s_st.n
                EM = {}
                d2m, d3m, am, p2m, dvm, cm = {}, {}, {}, {}, {}, {}

                def dve_tile(t):
                    p = t % 2
                    xt = (xtile[:, p * D:(p + 1) * D] if src_hbm
                          else r_sb[:, t * D:(t + 1) * D])
                    L = []
                    A = L.append
                    if src_hbm:
                        A((lambda e: e.wait_ge(s_x2[t % 2].h,
                                               xb2[t % 2] + 16 * (t // 2 + 1)), []))
                    A((lambda e: e.tensor_mul(sq[:, p * D:(p + 1) * D], xt, xt), []))
                    A((lambda e: e.reduce_sum(ccol[:, 4 * p:4 * p + 1], xt,
                                              axis=AX.X), []))
                    A((lambda e: e.reduce_sum(ccol[:, 4 * p + 1:4 * p + 2],
                                              sq[:, p * D:(p + 1) * D],
                                              axis=AX.X), []))
                    A((lambda e: e.tensor_scalar(cmean[:, p:p + 1],
                                                 ccol[:, 4 * p:4 * p + 1],
                                                 1.0 / D, None, ALU.mult), []))
                    A((lambda e: e.tensor_scalar(ccol[:, 4 * p + 1:4 * p + 2],
                                                 ccol[:, 4 * p + 1:4 * p + 2],
                                                 1.0 / D, None, ALU.mult), []))
                    A((lambda e: e.tensor_mul(ccol[:, 4 * p + 2:4 * p + 3],
                                              cmean[:, p:p + 1],
                                              cmean[:, p:p + 1]), []))
                    if src_hbm:
                        def rinit(e):
                            if h == 1 and t == 0:
                                e.wait_ge(s_st.h, stb)
                            e.tensor_copy(r_sb[:, t * D:(t + 1) * D], xt)
                        A((rinit, []))
                    A((lambda e: s_dve2.inc_ev(e.scalar_tensor_tensor(
                        cvar[:, p:p + 1], ccol[:, 4 * p + 2:4 * p + 3], -1.0,
                        ccol[:, 4 * p + 1:4 * p + 2], ALU.mult, ALU.add),
                        d2m, ("var", t)), [("d2", ("var", t))]))

                    def pe_tr_cnt(tt, grp):
                        return pb0 + (tt // 2) * 4 + grp * 2 + (tt % 2) + 1

                    def xlt_op(e):
                        if t >= 2:
                            e.wait_ge(s_pe.h, pe_tr_cnt(t - 2, 1))
                        e.wait_ge(s_act.h, am[("rstd", t)])
                        s_dve.inc_ev(e.tensor_scalar(
                            xlt[:, p * D:(p + 1) * D], xt, cmean[:, p:p + 1],
                            crstd[:, p:p + 1], ALU.subtract, ALU.mult),
                            dvm, ("xlt", t))
                    A((xlt_op, [("dv", ("xlt", t))]))
                    for grp in range(2):
                        def evac(e, grp=grp):
                            e.wait_ge(s_pe.h, pe_tr_cnt(t, grp))
                            dst = xln[:, grp * 8 * HALF:(grp * 8 + 8) * HALF] \
                                .rearrange("p (dk c) -> p dk c", dk=8)[:, :, t * 128:(t + 1) * 128]
                            s_dve3.inc_ev(
                                e.tensor_copy(dst, psb[grp][:]
                                              .rearrange("p (dk c) -> p dk c", dk=8)),
                                d3m, (f"ev{grp}", t))
                        A((evac, [("d3", (f"ev{grp}", t))]))
                    if not gate:
                        return L

                    def scs_ev(e):
                        e.wait_ge(s_pe2.h, p2m[("scores", t)])
                        s_dve2.inc_ev(e.tensor_copy(
                            scs[:, p * 128:(p + 1) * 128],
                            psf[2 if p == 0 else 5][0:64, 0:128]),
                            d2m, ("scs", t))
                    A((scs_ev, [("d2", ("scs", t))]))

                    def sc_ev(e):
                        e.wait_ge(s_pe2.h, p2m[("sctr", t)])
                        s_dve2.inc_ev(e.tensor_copy(sc[:, p * 64:(p + 1) * 64],
                                                    psf[3][0:128, 0:64]),
                                      d2m, ("scev", t))
                    A((sc_ev, [("d2", ("scev", t))]))
                    A((lambda e: s_dve2.inc_ev(
                        e.reduce_max(ccol[:, 4 * p + 3:4 * p + 4],
                                     sc[:, p * 64:(p + 1) * 64],
                                     axis=AX.X, negate=True),
                        d2m, ("negmax", t)), [("d2", ("negmax", t))]))

                    def recip_op(e):
                        e.wait_ge(s_act.h, am[("exp64", t)])
                        e.reciprocal(cden[:, p:p + 1], cden[:, p:p + 1])
                    A((recip_op, []))
                    A((lambda e: e.tensor_scalar(wesm[:, p * 64:(p + 1) * 64],
                                                 sce[:, p * 64:(p + 1) * 64],
                                                 cden[:, p:p + 1], None,
                                                 ALU.mult), []))
                    A((lambda e: e.tensor_copy(vv[:, p * 64:(p + 1) * 64],
                                               wesm[:, p * 64:(p + 1) * 64]), []))
                    for it in range(8):
                        A((lambda e: e.reduce_max(ccol[:, 4 * p + 3:4 * p + 4],
                                                  vv[:, p * 64:(p + 1) * 64],
                                                  axis=AX.X), []))
                        if it < 7:
                            A((lambda e: e.tensor_scalar(
                                mskt[:, p * 64:(p + 1) * 64],
                                vv[:, p * 64:(p + 1) * 64],
                                ccol[:, 4 * p + 3:4 * p + 4],
                                NEG, ALU.is_ge, ALU.mult), []))
                            A((lambda e: e.tensor_add(
                                vv[:, p * 64:(p + 1) * 64],
                                vv[:, p * 64:(p + 1) * 64],
                                mskt[:, p * 64:(p + 1) * 64]), []))
                    A((lambda e: e.tensor_scalar(mskt[:, p * 64:(p + 1) * 64],
                                                 wesm[:, p * 64:(p + 1) * 64],
                                                 ccol[:, 4 * p + 3:4 * p + 4],
                                                 None, ALU.is_ge), []))
                    A((lambda e: s_dve2.inc_ev(
                        e.tensor_mul(wee[:, p * 64:(p + 1) * 64],
                                     mskt[:, p * 64:(p + 1) * 64],
                                     wesm[:, p * 64:(p + 1) * 64]),
                        d2m, ("we", t)), [("d2", ("we", t))]))

                    def weT_ev(e):
                        e.wait_ge(s_pe2.h, p2m[("wetr", t)])
                        s_dve2.inc_ev(e.tensor_copy(
                            weT[:, p * 128:(p + 1) * 128],
                            psf[4][0:64, 0:128]),
                            d2m, ("weT", t))
                    A((weT_ev, [("d2", ("weT", t))]))

                    def flat_ev(e):
                        e.wait_ge(s_pe2.h, p2m[("flat", t)])
                        s_dve2.inc_ev(e.tensor_copy(flat[:, p * 64:(p + 1) * 64],
                                                    psf[0][0:128, 0:64]),
                                      d2m, ("flatev", t))
                    A((flat_ev, [("d2", ("flatev", t))]))
                    A((lambda e: e.tensor_scalar(mskt[:, p * 64:(p + 1) * 64],
                                                 flat[:, p * 64:(p + 1) * 64],
                                                 0.0, -1.0e9, ALU.is_equal,
                                                 ALU.mult), []))
                    A((lambda e: e.reduce_sum(
                        gs8[:, p * 8:(p + 1) * 8].rearrange("p (g o) -> p g o", g=8),
                        flat[:, p * 64:(p + 1) * 64].rearrange("p (g i) -> p g i", g=8),
                        axis=AX.X), []))
                    A((lambda e: e.tensor_add(fmn[:, p * 64:(p + 1) * 64],
                                              flat[:, p * 64:(p + 1) * 64],
                                              mskt[:, p * 64:(p + 1) * 64]), []))
                    A((lambda e: e.tensor_scalar(scalbuf[:, t * 8:(t + 1) * 8],
                                                 gs8[:, p * 8:(p + 1) * 8],
                                                 0.5, None, ALU.mult), []))
                    A((lambda e: e.reduce_max(
                        gm8[:, p * 8:(p + 1) * 8].rearrange("p (g o) -> p g o", g=8),
                        fmn[:, p * 64:(p + 1) * 64].rearrange("p (g i) -> p g i", g=8),
                        axis=AX.X), []))
                    A((lambda e: s_dve2.inc_ev(e.tensor_tensor(
                        gsm[:, p * 64:(p + 1) * 64].rearrange("p (g i) -> p g i", g=8),
                        fmn[:, p * 64:(p + 1) * 64].rearrange("p (g i) -> p g i", g=8),
                        gm8[:, p * 8:(p + 1) * 8].rearrange("p (g o) -> p g o", g=8)
                        .broadcast_to((128, 8, 8)), ALU.subtract),
                        d2m, ("gsm", t)), [("d2", ("gsm", t))]))

                    def gsum_op(e):
                        e.wait_ge(s_act.h, am[("gex", t)])
                        e.reduce_sum(
                            gs8[:, p * 8:(p + 1) * 8].rearrange("p (g o) -> p g o", g=8),
                            gex[:, p * 64:(p + 1) * 64].rearrange("p (g i) -> p g i", g=8),
                            axis=AX.X)
                    A((gsum_op, []))
                    A((lambda e: e.reciprocal(gr8[:, p * 8:(p + 1) * 8],
                                              gs8[:, p * 8:(p + 1) * 8]), []))
                    A((lambda e: s_dve2.inc_ev(e.tensor_tensor(
                        sfb[:, p * 64:(p + 1) * 64].rearrange("p (g i) -> p g i", g=8),
                        gex[:, p * 64:(p + 1) * 64].rearrange("p (g i) -> p g i", g=8),
                        gr8[:, p * 8:(p + 1) * 8].rearrange("p (g o) -> p g o", g=8)
                        .broadcast_to((128, 8, 8)), ALU.mult),
                        d2m, ("sfb", t)), [("d2", ("sfb", t))]))

                    def softev_op(e):
                        e.wait_ge(s_pe2.h, p2m[("softtr", t)])
                        if t >= 2:
                            e.wait_ge(s_ss[t % 2].h,
                                      ssb[t % 2] + 16 * ((t - 2) // 2 + 1))
                        s_dve3.inc_ev(e.tensor_copy(
                            sstg[0:8, p * 1024:(p + 1) * 1024]
                            .rearrange("i (g c) -> i g c", g=8),
                            psb[1][0:8, :].rearrange("i (g c) -> i g c", g=8)),
                            d3m, ("softev", t))
                        s_dve2.inc_ev(e.engine_nop(), d2m, ("sstg", t))
                    A((softev_op, [("d3", ("softev", t)), ("d2", ("sstg", t))]))
                    return L

                def act_tile(t):
                    p = t % 2
                    L = []

                    def rstd_op(e):
                        e.wait_ge(s_dve2.h, d2m[("var", t)])
                        e.activation(crstd[:, p:p + 1], cvar[:, p:p + 1],
                                     AF.Ln, bias=epsc[:])
                        s_act.inc_ev(e.activation(crstd[:, p:p + 1],
                                                  crstd[:, p:p + 1],
                                                  AF.Exp, scale=-0.5),
                                     am, ("rstd", t))
                    L.append((rstd_op, [("a", ("rstd", t))]))
                    if not gate:
                        return L

                    def exp64_op(e):
                        e.wait_ge(s_dve2.h, d2m[("negmax", t)])
                        s_act.inc_ev(
                            e.activation(sce[:, p * 64:(p + 1) * 64],
                                         sc[:, p * 64:(p + 1) * 64], AF.Exp,
                                         bias=ccol[:, 4 * p + 3:4 * p + 4],
                                         accum_out=cden[:, p:p + 1]),
                            am, ("exp64", t))
                    L.append((exp64_op, [("a", ("exp64", t))]))

                    def gex_op(e):
                        e.wait_ge(s_dve2.h, d2m[("gsm", t)])
                        s_act.inc_ev(e.activation(gex[:, p * 64:(p + 1) * 64],
                                                  gsm[:, p * 64:(p + 1) * 64],
                                                  AF.Exp), am, ("gex", t))
                    L.append((gex_op, [("a", ("gex", t))]))
                    return L

                def pe_tile(t):
                    p = t % 2
                    L = []
                    for grp in range(2):
                        def tr(e, grp=grp):
                            bt = EM["bt"]
                            e.wait_ge(s_dve.h, dvm[("xlt", t)])
                            bt.acquire(e, ("psb", grp),
                                       (s_dve3, d3m[(f"ev{grp}", t)]))
                            for k in range(8):
                                dk = grp * 8 + k
                                mm = e.matmul(
                                    psb[grp][:, k * 128:(k + 1) * 128],
                                    xlt[:, p * D + dk * 128:p * D + (dk + 1) * 128],
                                    idb[:], start=(k == 0), stop=(k == 7),
                                    is_transpose=True)
                            s_pe.inc(mm)
                        L.append((tr, []))
                    if not gate:
                        return L

                    def scores_op(e):
                        bt = EM["bt"]
                        e.wait_ge(s_dve3.h, d3m[("ev1", t)])
                        if t == 0:
                            e.wait_ge(s_cast.h, cm[("gwt",)])
                        bank = 2 if p == 0 else 5
                        bt.acquire(e, ("psf", bank), (s_dve2, d2m[("scs", t)]))
                        for dk in range(NDK):
                            mm = e.matmul(
                                psf[bank][0:64, 0:128],
                                gwt[:, dk * 64:(dk + 1) * 64],
                                xln[:, dk * HALF + t * 128:dk * HALF + (t + 1) * 128],
                                start=(dk == 0), stop=(dk == NDK - 1))
                        s_pe2.inc_ev(mm, p2m, ("scores", t))
                    L.append((scores_op, [("p2", ("scores", t))]))

                    def sctr_op(e):
                        bt = EM["bt"]
                        e.wait_ge(s_dve2.h, d2m[("scs", t)])
                        bt.acquire(e, ("psf", 3), (s_dve2, d2m[("scev", t)]))
                        s_pe2.inc_ev(e.matmul(psf[3][0:128, 0:64],
                                              scs[:, p * 128:(p + 1) * 128],
                                              idf[0:64, 0:64], start=True,
                                              stop=True, is_transpose=True),
                                     p2m, ("sctr", t))
                    L.append((sctr_op, [("p2", ("sctr", t))]))

                    def wetr_op(e):
                        bt = EM["bt"]
                        e.wait_ge(s_dve2.h, d2m[("we", t)])
                        bt.acquire(e, ("psf", 4), (s_dve2, d2m[("weT", t)]))
                        s_pe2.inc_ev(e.matmul(psf[4][0:64, 0:128],
                                              wee[:, p * 64:(p + 1) * 64],
                                              idf[:], start=True, stop=True,
                                              is_transpose=True),
                                     p2m, ("wetr", t))
                    L.append((wetr_op, [("p2", ("wetr", t))]))

                    def flat_op(e):
                        bt = EM["bt"]
                        e.wait_ge(s_dve2.h, d2m[("weT", t)])
                        bt.acquire(e, ("psf", 0), (s_dve2, d2m[("flatev", t)]))
                        e.matmul(psf[0][0:128, 0:64],
                                 weT[:, p * 128:(p + 1) * 128], Pm[:],
                                 start=True, stop=True)
                        s_pe2.inc_ev(e.matmul(
                            psf[2 if p == 0 else 5][0:1, 400:401],
                            idb[0:1, 0:1], idb[0:1, 0:1],
                            start=False, stop=True),
                                     p2m, ("flat", t))
                    L.append((flat_op, [("p2", ("flat", t))]))

                    def softtr_op(e):
                        bt = EM["bt"]
                        e.wait_ge(s_dve2.h, d2m[("sfb", t)])
                        bt.acquire(e, ("psb", 1), (s_dve3, d3m[("softev", t)]))
                        for g in range(8):
                            mm = e.matmul(psb[1][0:8, g * 128:(g + 1) * 128],
                                          sfb[:, p * 64 + g * 8:p * 64 + (g + 1) * 8],
                                          idb[:], start=(g == 0), stop=(g == 7),
                                          is_transpose=True)
                        s_pe2.inc_ev(mm, p2m, ("softtr", t))
                    L.append((softtr_op, [("p2", ("softtr", t))]))
                    return L

                def zipped(mk):
                    seq = []
                    for pp in range(NT // 2):
                        a = mk(2 * pp)
                        bo = mk(2 * pp + 1)
                        for i in range(max(len(a), len(bo))):
                            if i < len(a):
                                seq.append(a[i])
                            if i < len(bo):
                                seq.append(bo[i])
                    return seq

                dve_seq = zipped(dve_tile)
                act_seq = zipped(act_tile)
                pe_seq = zipped(pe_tile)

                def evs_of(seq, ch):
                    return [k for _, evs in seq for c, k in evs if c == ch]

                d2m.update(s_dve2.evmap(evs_of(dve_seq, "d2")))
                d3m.update(s_dve3.evmap(evs_of(dve_seq, "d3")))
                dvm.update(s_dve.evmap(evs_of(dve_seq, "dv")))
                am.update(s_act.evmap(evs_of(act_seq, "a")))
                p2m.update(s_pe2.evmap(evs_of(pe_seq, "p2")))
                if gate:
                    cm.update(s_cast.evmap([("gwt",)]))

                with nc.Block() as b:
                    if src_hbm:
                        @b.sync
                        def _(e):
                            for t in range(NT):
                                if t >= 2:
                                    e.wait_ge(s_dve.h, dvm[("xlt", t - 2)])
                                s_x2[t % 2].inc(e.dma_start(
                                    xtile[:, (t % 2) * D:(t % 2 + 1) * D],
                                    x_in[h * HALF + t * 128:h * HALF + (t + 1) * 128, :]),
                                    16)

                    if gate:
                        @b.gpsimd
                        def _(e):
                            s_gw.inc(e.dma_start(gstg[:], gw_in), 16)
                            for t in range(NT):
                                e.wait_ge(s_dve2.h, d2m[("sstg", t)])
                                s_ss[t % 2].inc(e.dma_start(
                                    soft_dram[:, :, h * HALF + t * 128:
                                              h * HALF + (t + 1) * 128]
                                    .rearrange("g i c -> i g c"),
                                    sstg[0:8, (t % 2) * 1024:(t % 2 + 1) * 1024]
                                    .rearrange("i (g c) -> i g c", g=8)), 16)

                    @b.scalar
                    def _(e):
                        if gate:
                            e.wait_ge(s_gw.h, gwb + 16)
                            s_cast.inc_ev(e.activation(gwt[:], gstg[:], AF.Copy),
                                          cm, ("gwt",))
                        for fn, _evs in act_seq:
                            fn(e)

                    @b.vector
                    def _(e):
                        for fn, _evs in dve_seq:
                            fn(e)

                    @b.tensor
                    def _(e):
                        EM["bt"] = Banks()
                        for fn, _evs in pe_seq:
                            fn(e)

        def phase_ffn(h, build_mw):
            with contextlib.ExitStack() as ph:
                _u = next(_uid)
                ap_ = ph.enter_context(
                    nc.sbuf_tensor(f"apA_{_u}", [128, NICH * HALF], BF16))
                silg = ph.enter_context(
                    nc.sbuf_tensor(f"silg_{_u}", [128, 2 * HALF], F32))
                wdbf = ph.enter_context(
                    nc.sbuf_tensor(f"wdbf_{_u}", [128, 2 * NICH * 512], BF16))
                mskw = (ph.enter_context(
                    nc.sbuf_tensor(f"mskw_{_u}", [128, NG * NDK * 8], BF16))
                    if build_mw else None)
                mstg = (ph.enter_context(
                    nc.sbuf_tensor(f"mstg_{_u}", [128, NG * NDK * 8], F32))
                    if build_mw else None)
                soft_rg = ph.enter_context(
                    nc.sbuf_tensor(f"softrg_{_u}", [8, 2 * HALF], BF16))
                fstage = ph.enter_context(
                    nc.sbuf_tensor(f"fstage_{_u}", [128, 3 * 2048], F32))
                fwbf = ph.enter_context(
                    nc.sbuf_tensor(f"fwbf_{_u}", [128, 3 * 2048], BF16))
                mw_rg = ph.enter_context(
                    nc.sbuf_tensor(f"mwrg_{_u}", [8, 2 * 2 * NI], BF16))

                wfb = [s.n for s in s_wf]
                gsb = [s.n for s in s_gs]
                gmlb = [s.n for s in s_gml]
                gmsb = [s.n for s in s_gms]
                ssb_tot = [s.n for s in s_ss]
                gwb = s_gw.n
                NU = 34
                down_units = [(dblk, u * 4, min(4, NICH - u * 4))
                              for dblk in range(4) for u in range(3)]

                cast_ev = ([("mk",)] if build_mw else [])
                pe_ev, pe2_ev, act_ev, dve_ev, d2_ev, d3_ev = [], [], [], [], [], []
                for g in range(NG):
                    for k in range(22):
                        cast_ev.append(("gu", g, k))
                    for ich in range(NICH):
                        if build_mw:
                            for m in range(2):
                                pe2_ev.append(("mw", g, ich, m))
                                d3_ev.append(("mwev", g, ich, m))
                        for m in range(2):
                            for tb in range(2):
                                pe_ev.append(("gu", g, ich, m, tb))
                                if m == 0:
                                    act_ev.append(("sil", g, ich, tb))
                                else:
                                    dve_ev.append(("ap", g, ich, tb))
                    for j in range(12):
                        cast_ev.append(("dn", g, j))
                    for dblk in range(4):
                        for t in range(NT):
                            pe2_ev.append(("dn", g, dblk, t))
                            d2_ev.append(("dnev", g, dblk, t))
                cmap = s_cast.evmap(cast_ev)
                pem = s_pe.evmap(pe_ev)
                pe2m = s_pe2.evmap(pe2_ev)
                actm = s_act.evmap(act_ev)
                dvem = s_dve.evmap(dve_ev)
                d2map = s_dve2.evmap(d2_ev)
                d3map = s_dve3.evmap(d3_ev)

                def cast_key(g, k):
                    return ("gu", g, k) if k < 22 else ("dn", g, k - 22)

                def glob_unit(ui):
                    return divmod(ui, NU)

                with nc.Block() as b:
                    @b.sync
                    def _(e):
                        for g in range(NG):
                            for k in range(NU):
                                ui = g * NU + k
                                if ui >= 3:
                                    pg, pk = glob_unit(ui - 3)
                                    e.wait_ge(s_cast.h, cmap[cast_key(pg, pk)])
                                slot = fstage[:, (ui % 3) * 2048:(ui % 3 + 1) * 2048]
                                sem = s_wf[ui % 3]
                                if k < 22:
                                    ich, m = divmod(k, 2)
                                    sem.inc(e.dma_start(
                                        slot.rearrange("p (dk a) -> p dk a", dk=NDK),
                                        (wg_in if m == 0 else wu_in)[g, ich]), 16)
                                else:
                                    dblk, i0, ni = down_units[k - 22]
                                    sem.inc(e.dma_start(
                                        slot.rearrange("p (i c) -> p i c", i=4)[:, 0:ni, :],
                                        wd_in[g, dblk, i0:i0 + ni]
                                        .rearrange("i p c -> p i c")), 16)

                    @b.gpsimd
                    def _(e):
                        if build_mw:
                            s_gw.inc(e.dma_start(mstg[:], mkw_in), 16)
                        for i in range(2):
                            e.wait_ge(s_ss[i].h, s_ss[i].n)
                        if not build_mw:
                            for i in range(2):
                                e.wait_ge(s_gms[i].h, s_gms[i].n)
                        for g in range(NG):
                            if g >= 2:
                                e.wait_ge(s_pe.h, pem[("gu", g - 2, NICH - 1, 1, 1)])
                            sslot = soft_rg[0:8, (g % 2) * HALF:(g % 2 + 1) * HALF]
                            mslot = mw_rg[0:8, (g % 2) * 2 * NI:(g % 2 + 1) * 2 * NI]
                            s_gs[g % 2].inc(e.dma_start(
                                sslot,
                                soft_dram[g, :, h * HALF:(h + 1) * HALF]), 16)
                            if build_mw:
                                e.wait_ge(s_dve3.h, d3map[("mwev", g, NICH - 1, 1)])
                                s_gms[g % 2].inc(e.dma_start(
                                    mw_dram[:, g].rearrange("m i c -> i m c"),
                                    mslot.rearrange("i (m c) -> i m c", m=2)), 16)
                            else:
                                s_gml[g % 2].inc(e.dma_start(
                                    mslot.rearrange("i (m c) -> i m c", m=2),
                                    mw_dram[:, g].rearrange("m i c -> i m c")), 16)

                    def ap_evac(e, g, ich):
                        for tb in range(2):
                            e.wait_ge(s_pe.h, pem[("gu", g, ich, 1, tb)])
                            e.wait_ge(s_act.h, actm[("sil", g, ich, tb)])
                            s_dve.inc_ev(e.tensor_mul(
                                ap_[:, ich * HALF + tb * 512:ich * HALF + (tb + 1) * 512],
                                psf[2 + tb][:],
                                silg[:, (ich % 2) * HALF + tb * 512:
                                     (ich % 2) * HALF + (tb + 1) * 512]),
                                dvem, ("ap", g, ich, tb))

                    def down_evac(e, g, dblk, t):
                        e.wait_ge(s_pe2.h, pe2m[("dn", g, dblk, t)])
                        s_dve2.inc_ev(e.scalar_tensor_tensor(
                            r_sb[:, t * D + dblk * 512:t * D + (dblk + 1) * 512],
                            psf[4 + (g * 32 + dblk * 8 + t) % 2][:],
                            scalbuf[:, t * 8 + g:t * 8 + g + 1],
                            r_sb[:, t * D + dblk * 512:t * D + (dblk + 1) * 512],
                            ALU.mult, ALU.add), d2map, ("dnev", g, dblk, t))

                    @b.vector
                    def _(e):
                        for g in range(NG):
                            for ich in range(NICH):
                                if build_mw:
                                    for m in range(2):
                                        if g >= 2 and ich == 0 and m == 0:
                                            e.wait_ge(s_gms[g % 2].h,
                                                      gmsb[g % 2]
                                                      + 16 * ((g - 2) // 2 + 1))
                                        e.wait_ge(s_pe2.h, pe2m[("mw", g, ich, m)])
                                        s_dve3.inc_ev(e.tensor_copy(
                                            mw_rg[0:8, (g % 2) * 2 * NI + m * NI
                                                  + ich * 128:(g % 2) * 2 * NI
                                                  + m * NI + (ich + 1) * 128],
                                            psf[4 + m][0:8, 0:128]),
                                            d3map, ("mwev", g, ich, m))
                                if ich >= 1:
                                    ap_evac(e, g, ich - 1)
                            ap_evac(e, g, NICH - 1)
                            for dblk in range(1, 4):
                                for t in range(NT):
                                    down_evac(e, g, dblk - 1, t)
                            for t in range(NT):
                                down_evac(e, g, 3, t)

                    @b.scalar
                    def _(e):
                        if build_mw:
                            e.wait_ge(s_gw.h, s_gw.n)
                            s_cast.inc_ev(e.activation(mskw[:], mstg[:], AF.Copy),
                                          cmap, ("mk",))

                        def cast_gu(e, g, ich):
                            for m in range(2):
                                k = ich * 2 + m
                                Kgu = g * 22 + k
                                ui = g * NU + k
                                e.wait_ge(s_wf[ui % 3].h,
                                          wfb[ui % 3] + 16 * (ui // 3 + 1))
                                if Kgu >= 3:
                                    pk = Kgu - 3
                                    pg, pkk = divmod(pk, 22)
                                    pich, pm = divmod(pkk, 2)
                                    e.wait_ge(s_pe.h, pem[("gu", pg, pich, pm, 1)])
                                s_cast.inc_ev(e.activation(
                                    fwbf[:, (Kgu % 3) * 2048:(Kgu % 3 + 1) * 2048],
                                    fstage[:, (ui % 3) * 2048:(ui % 3 + 1) * 2048],
                                    AF.Copy), cmap, ("gu", g, k))

                        def sil_pair(e, g, ich):
                            I_ = g * NICH + ich
                            for tb in range(2):
                                e.wait_ge(s_pe.h, pem[("gu", g, ich, 0, tb)])
                                if I_ >= 2:
                                    pg, pich = divmod(I_ - 2, NICH)
                                    e.wait_ge(s_dve.h, dvem[("ap", pg, pich, tb)])
                                s_act.inc_ev(e.activation(
                                    silg[:, (ich % 2) * HALF + tb * 512:
                                         (ich % 2) * HALF + (tb + 1) * 512],
                                    psf[tb][:], AF.Silu),
                                    actm, ("sil", g, ich, tb))

                        for g in range(NG):
                            for ich in range(NICH):
                                cast_gu(e, g, ich)
                                if ich >= 1:
                                    sil_pair(e, g, ich - 1)
                            sil_pair(e, g, NICH - 1)
                            for j, (dblk, i0, ni) in enumerate(down_units):
                                k = 22 + j
                                ui = g * NU + k
                                DJ = g * 4 + dblk
                                e.wait_ge(s_wf[ui % 3].h,
                                          wfb[ui % 3] + 16 * (ui // 3 + 1))
                                if DJ >= 2 and i0 == 0:
                                    pg, pd = divmod(DJ - 2, 4)
                                    e.wait_ge(s_pe2.h, pe2m[("dn", pg, pd, NT - 1)])
                                s_cast.inc_ev(e.activation(
                                    wdbf[:, (dblk % 2) * NICH * 512 + i0 * 512:
                                         (dblk % 2) * NICH * 512 + (i0 + ni) * 512],
                                    fstage[:, (ui % 3) * 2048:(ui % 3) * 2048 + ni * 512],
                                    AF.Copy), cmap, ("dn", g, j))

                    @b.tensor
                    def _(e):
                        bt = Banks()
                        for g in range(NG):
                            for ich in range(NICH):
                                if build_mw:
                                    for m in range(2):
                                        k = ich * 2 + m
                                        e.wait_ge(s_cast.h, cmap[("gu", g, k)])
                                        if ich == 0 and m == 0:
                                            e.wait_ge(s_cast.h, cmap[("mk",)])
                                        bt.acquire(e, ("psf", 4 + m),
                                                   (s_dve3, d3map[("mwev", g, ich, m)]))
                                        for dk in range(NDK):
                                            mm = e.matmul(
                                                psf[4 + m][0:8, 0:128],
                                                mskw[:, g * 128 + dk * 8:
                                                     g * 128 + (dk + 1) * 8],
                                                fwbf[:, ((g * 22 + k) % 3) * 2048 + dk * 128:
                                                    ((g * 22 + k) % 3) * 2048 + (dk + 1) * 128],
                                                start=(dk == 0), stop=(dk == NDK - 1))
                                        s_pe2.inc_ev(mm, pe2m, ("mw", g, ich, m))
                                for m in range(2):
                                    k = ich * 2 + m
                                    e.wait_ge(s_cast.h, cmap[("gu", g, k)])
                                    if k == 0:
                                        e.wait_ge(s_gs[g % 2].h,
                                                  gsb[g % 2] + 16 * (g // 2 + 1))
                                        if not build_mw:
                                            e.wait_ge(s_gml[g % 2].h,
                                                      gmlb[g % 2]
                                                      + 16 * (g // 2 + 1))
                                    if build_mw:
                                        e.wait_ge(s_dve3.h, d3map[("mwev", g, ich, m)])
                                    for tb in range(2):
                                        bank = m * 2 + tb
                                        evac = ((s_act, actm[("sil", g, ich, tb)])
                                                if m == 0 else
                                                (s_dve, dvem[("ap", g, ich, tb)]))
                                        bt.acquire(e, ("psf", bank), evac)
                                        for dk in range(NDK):
                                            e.matmul(
                                                psf[bank][:],
                                                fwbf[:, ((g * 22 + k) % 3) * 2048 + dk * 128:
                                                    ((g * 22 + k) % 3) * 2048 + (dk + 1) * 128],
                                                xln[:, dk * HALF + tb * 512:
                                                    dk * HALF + (tb + 1) * 512],
                                                start=(dk == 0), stop=False)
                                        mm = e.matmul(
                                            psf[bank][:],
                                            mw_rg[0:8, (g % 2) * 2 * NI + m * NI
                                                  + ich * 128:(g % 2) * 2 * NI
                                                  + m * NI + (ich + 1) * 128],
                                            soft_rg[0:8, (g % 2) * HALF + tb * 512:
                                                    (g % 2) * HALF + (tb + 1) * 512],
                                            start=False, stop=True)
                                        s_pe.inc_ev(mm, pem, ("gu", g, ich, m, tb))
                            e.wait_ge(s_dve.h, dvem[("ap", g, NICH - 1, 1)])
                            for dblk in range(4):
                                e.wait_ge(s_cast.h, cmap[("dn", g, dblk * 3 + 2)])
                                for t in range(NT):
                                    gi = g * 32 + dblk * 8 + t
                                    bt.acquire(e, ("psf", 4 + gi % 2),
                                               (s_dve2, d2map[("dnev", g, dblk, t)]))
                                    for ich in range(NICH):
                                        mm = e.matmul(
                                            psf[4 + gi % 2][:],
                                            ap_[:, ich * HALF + t * 128:
                                                ich * HALF + (t + 1) * 128],
                                            wdbf[:, (dblk % 2) * NICH * 512 + ich * 512:
                                                 (dblk % 2) * NICH * 512 + (ich + 1) * 512],
                                            start=(ich == 0), stop=(ich == NICH - 1))
                                    s_pe2.inc_ev(mm, pe2m, ("dn", g, dblk, t))

        def phase_attn(h):
            phase_ln_gate(h, src_hbm=False, gate=False)
            with contextlib.ExitStack() as phA:
                _uA = next(_uid)
                def pbA(name, p, f, dt):
                    return phA.enter_context(
                        nc.sbuf_tensor(f"{name}_{_uA}", [p, f], dt))
                kfm = pbA("kfm", 128, NKV * 1280, BF16)
                vtm = pbA("vtm", 128, 10 * 512, BF16)
                qao = pbA("qao", 128, 8 * HALF, BF16)
                wstage = pbA("wstage", 128, 2 * 2048, F32)
                wbf = pbA("wbf", 128, 2 * 2048, BF16)
                for pass_ in range(2):
                    attn_qkv(h, pass_, kfm, vtm, qao, wstage, wbf)
                    if h == 0 and "qao" in DEBUG_TAPS:
                        qt = tap_dram(f"qao{pass_}", (128, 8 * HALF), BF16)
                        with nc.Block() as b:
                            @b.gpsimd
                            def _(e):
                                s_g.inc(e.dma_start(qt[:], qao[:]), 16)
                                e.wait_ge(s_g.h, s_g.n)
                    attn_oproj(h, pass_, qao, wstage)
                if h == 0 and "kv" in DEBUG_TAPS:
                    kt = tap_dram("kfm", (128, NKV * 1280), BF16)
                    vt = tap_dram("vtm", (128, 10 * 512), BF16)
                    with nc.Block() as b:
                        @b.gpsimd
                        def _(e):
                            s_g.inc(e.dma_start(kt[:], kfm[:]), 16)
                            s_g.inc(e.dma_start(vt[:], vtm[:]), 16)
                            e.wait_ge(s_g.h, s_g.n)
                if h == 0:
                    with nc.Block() as b:
                        @b.gpsimd
                        def _(e):
                            s_gkv.inc(e.dma_start(
                                kv_dram[0][:, 0:1024]
                                .rearrange("p (kv s) -> p kv s", kv=4),
                                kfm[:].rearrange("p (kv s) -> p kv s", kv=4)
                                [:, :, 768:1024]), 16)
                            s_gkv.inc(e.dma_start(kv_dram[1][:, 0:1024],
                                                  vtm[:, 6 * 512:8 * 512]), 16)
                            e.wait_ge(s_gkv.h, s_gkv.n)

        def attn_qkv(h, pass_, kfm, vtm, qao, wstage, wbf):
            with contextlib.ExitStack() as ph:
                _u = next(_uid)
                def pb_(name, p, f, dt):
                    return ph.enter_context(
                        nc.sbuf_tensor(f"{name}_{_u}", [p, f], dt))
                csb = pb_("csb", 128, 2 * HALF, F32)
                mskb = pb_("mskb", 128, 384, BF16)
                qraw = pb_("qraw", 128, HALF, F32)
                rot = pb_("rot", 128, HALF, F32)
                xfer = pb_("xfer", 128, HALF, BF16)
                sqtm = pb_("sqtm", 128, 2 * HALF, BF16)
                sscol = pb_("sscol", 128, 2 * 8, F32)
                rstdc = pb_("rstdc", 128, 2 * 8, F32)
                cqc = pb_("cqc", 128, 2 * 8, F32)
                exb = pb_("exb", 128, 6 * 384, F32)
                abf = pb_("abf", 128, 6 * 384, BF16)
                atT = pb_("atT", 128, 6 * 384, BF16)
                cden = pb_("cdena", 128, 6, F32)

                if pass_ == 0:
                    units = ([("k", kv) for kv in range(NKV)]
                             + [("v", kv) for kv in range(NKV)]
                             + [("q", hh) for hh in range(8)])
                else:
                    units = [("q", hh) for hh in range(8, 16)]
                NUq = len(units)
                qk_us = [i for i, (tp, _) in enumerate(units) if tp != "v"]
                q_us = [i for i, (tp, _) in enumerate(units) if tp == "q"]

                wab = [s.n for s in s_wa]
                gcb, gkvb = s_gc.n, s_gkv.n
                peb = s_pe.n

                def att_pipeline(emit_head, emit_tail):
                    for p in range(NT // 2):
                        emit_head(p)
                        if p >= 2:
                            emit_tail(p - 2)
                    emit_tail(NT // 2 - 2)
                    emit_tail(NT // 2 - 1)

                def asl(tl):
                    return ((tl // 2) % 3) * 2 + (tl % 2)

                cast_ev, pe2_ev, act_ev, d2_ev = [], [], [], []
                for ui, (tp, idx) in enumerate(units):
                    cast_ev.append(("w", ui))
                for ui, (tp, idx) in enumerate(units):
                    if tp == "q":
                        break
                    if tp == "k":
                        d2_ev += [("ev", ui), ("qro", ui), ("bf", ui), ("ssm", ui)]
                        act_ev += [("sqT", ui), ("rstd", ui)]
                        pe2_ev += [("trk1", ui), ("trk2", ui)]
                        d2_ev += [("knrm", ui), ("kfmev", ui)]
                    else:
                        d2_ev += [("ev", ui), ("vstg", ui), ("vtm", ui)]
                        pe2_ev.append(("vtr", ui))
                def sched_att_pe(pu):
                    att_pipeline(
                        lambda p: pe2_ev.extend([("sc", pu, 2 * p),
                                                 ("sc", pu, 2 * p + 1)]),
                        lambda p: pe2_ev.extend([("atrp", pu, p),
                                                 ("pv", pu, 2 * p),
                                                 ("pv", pu, 2 * p + 1)]))

                def sched_att_act(pu):
                    att_pipeline(
                        lambda p: act_ev.extend([("exp", pu, 2 * p),
                                                 ("exp", pu, 2 * p + 1)]),
                        lambda p: act_ev.extend([("out", pu, 2 * p),
                                                 ("out", pu, 2 * p + 1)]))

                def d2_tail(pu, p):
                    return [("rec", pu, p), ("abf", pu, 2 * p),
                            ("abf", pu, 2 * p + 1), ("atTp", pu, p)]

                for qi, ui in enumerate(q_us):
                    if qi >= 1:
                        pu = q_us[qi - 1]
                        sched_att_pe(pu)
                        sched_att_act(pu)
                        d2_ev += ([("ev", ui)] + d2_tail(pu, 0)
                                  + [("qro", ui)] + d2_tail(pu, 1)
                                  + [("bf", ui)] + d2_tail(pu, 2)
                                  + d2_tail(pu, 3))
                    else:
                        d2_ev += [("ev", ui), ("qro", ui), ("bf", ui)]
                    pe2_ev.append(("trq", ui))
                    act_ev += [("sqT", ui), ("rstd", ui)]
                    d2_ev += [("ssm", ui), ("cq", ui)]
                pu = q_us[-1]
                sched_att_pe(pu)
                sched_att_act(pu)
                for p in range(4):
                    d2_ev += d2_tail(pu, p)

                cmap = s_cast.evmap(cast_ev)
                pe2m = s_pe2.evmap(pe2_ev)
                actm = s_act.evmap(act_ev)
                d2map = s_dve2.evmap(d2_ev)

                def kwin(gt):
                    c0 = max(0, gt - 2)
                    return c0, gt - c0 + 1

                with nc.Block() as b:
                    @b.gpsimd
                    def _(e):
                        s_gc.inc(e.dma_start(csb[:, 0:HALF],
                                             cs_in[:, h * HALF:(h + 1) * HALF]), 16)
                        s_gc.inc(e.dma_start(
                            csb[:, HALF:],
                            cs_in[:, SEQ + h * HALF:SEQ + (h + 1) * HALF]), 16)
                        s_gc.inc(e.dma_start(mskb[:], mskh_in[:]), 16)
                        if pass_ == 0 and h == 1:
                            s_gkv.inc(e.dma_start(
                                kfm[:].rearrange("p (kv s) -> p kv s", kv=4)
                                [:, :, 0:256],
                                kv_dram[0][:, 0:1024]
                                .rearrange("p (kv s) -> p kv s", kv=4)), 16)
                            s_gkv.inc(e.dma_start(vtm[:, 0:1024],
                                                  kv_dram[1][:, 0:1024]), 16)

                    @b.sync
                    def _(e):
                        for ui, (tp, idx) in enumerate(units):
                            if ui >= 2:
                                e.wait_ge(s_cast.h, cmap[("w", ui - 2)])
                            if tp == "k":
                                src = wkv_in[idx]
                            elif tp == "v":
                                src = wkv_in[NKV + idx]
                            else:
                                src = wq_in[idx]
                            s_wa[ui % 2].inc(e.dma_start(
                                wstage[:, (ui % 2) * 2048:(ui % 2 + 1) * 2048]
                                .rearrange("p (dk a) -> p dk a", dk=NDK), src), 16)

                    @b.scalar
                    def _(e):
                        wt = Waiter()
                        def a_cast(ui):
                            wt.w(e, s_wa[ui % 2],
                                      wab[ui % 2] + 16 * (ui // 2 + 1))
                            if ui >= 2:
                                wt.w(e, s_pe, peb + 2 * (ui - 2) + 2)
                            s_cast.inc_ev(e.activation(
                                wbf[:, (ui % 2) * 2048:(ui % 2 + 1) * 2048],
                                wstage[:, (ui % 2) * 2048:(ui % 2 + 1) * 2048],
                                AF.Copy), cmap, ("w", ui))

                        def a_sq(ui, tp, qi):
                            wt.w(e, s_pe2, pe2m[("trq" if tp == "q" else "trk1",
                                                     ui)])
                            if tp == "q" and qi >= 2:
                                wt.w(e, s_dve2, d2map[("ssm", q_us[qi - 2])])
                            sl = (qi % 2) if tp == "q" else 0
                            s_act.inc_ev(e.activation(
                                sqtm[:, sl * HALF:(sl + 1) * HALF],
                                psb[0][:], AF.Square), actm, ("sqT", ui))

                        def a_rstd(ui, tp, qi):
                            sl = (qi % 2) if tp == "q" else 0
                            wt.w(e, s_dve2, d2map[("ssm", ui)])
                            e.activation(rstdc[:, sl * 8:(sl + 1) * 8],
                                         sscol[:, sl * 8:(sl + 1) * 8], AF.Ln)
                            s_act.inc_ev(e.activation(
                                rstdc[:, sl * 8:(sl + 1) * 8],
                                rstdc[:, sl * 8:(sl + 1) * 8],
                                AF.Exp, scale=-0.5), actm, ("rstd", ui))

                        def a_att(ui, idx):
                            hl = idx - 8 * pass_
                            qi = q_us.index(ui)
                            sl = qi % 2

                            def a_head(p):
                                wt.w(e, s_pe2, pe2m[("sc", ui, 2 * p + 1)])
                                for tl in (2 * p, 2 * p + 1):
                                    wq_ = (min(h * 8 + tl, 2) + 1) * 128
                                    s_act.inc_ev(e.activation(
                                        exb[:, asl(tl) * 384:
                                            asl(tl) * 384 + wq_],
                                        psf[(p % 2) * 2 + tl % 2][:, 0:wq_],
                                        AF.Exp,
                                        scale=cqc[:, sl * 8 + tl:sl * 8 + tl + 1],
                                        bias=tempnb[:, idx:idx + 1],
                                        accum_out=cden[:, asl(tl):
                                                       asl(tl) + 1]),
                                        actm, ("exp", ui, tl))

                            def a_tail(p):
                                wt.w(e, s_pe2, pe2m[("pv", ui, 2 * p + 1)])
                                for tl in (2 * p, 2 * p + 1):
                                    s_act.inc_ev(e.activation(
                                        qao[:, hl * HALF + tl * 128:
                                            hl * HALF + (tl + 1) * 128],
                                        psf[4 + tl % 2][:, 0:128], AF.Copy),
                                        actm, ("out", ui, tl))

                            att_pipeline(a_head, a_tail)

                        for ui, (tp, idx) in enumerate(units):
                            if tp == "q":
                                break
                            a_cast(ui)
                            if tp == "k":
                                a_sq(ui, tp, 0)
                                a_rstd(ui, tp, 0)
                        a_cast(q_us[0])
                        for qi, ui in enumerate(q_us):
                            tp, idx = units[ui]
                            if qi + 1 < len(q_us):
                                a_cast(q_us[qi + 1])
                            if qi >= 1:
                                a_att(q_us[qi - 1], units[q_us[qi - 1]][1])
                            a_sq(ui, tp, qi)
                            a_rstd(ui, tp, qi)
                        a_att(q_us[-1], units[q_us[-1]][1])

                    @b.tensor
                    def _(e):
                        wt = Waiter()
                        bt = Banks(wt)

                        def p_proj(ui, tp):
                            wt.w(e, s_cast, cmap[("w", ui)])
                            if ui == 0:
                                wt.w(e, s_gc, s_gc.n)
                                wt.w(e, s_gkv, s_gkv.n)
                            for tb in range(2):
                                bank = (ui % 2) * 2 + tb
                                bt.acquire(e, ("psf", bank),
                                           (s_dve2, d2map[("ev", ui)]))
                                for dk in range(NDK):
                                    mm = e.matmul(
                                        psf[bank][:],
                                        wbf[:, (ui % 2) * 2048 + dk * 128:
                                            (ui % 2) * 2048 + (dk + 1) * 128],
                                        xln[:, dk * HALF + tb * 512:
                                            dk * HALF + (tb + 1) * 512],
                                        start=(dk == 0), stop=(dk == NDK - 1))
                                s_pe.inc(mm)

                        def p_tr(ui, tp, idx, qi):
                            wt.w(e, s_dve2, d2map[("bf", ui)])
                            src_fm = (qao[:, (idx - 8 * pass_) * HALF:
                                          (idx - 8 * pass_ + 1) * HALF]
                                      if tp == "q" else xfer)
                            evac1 = ((s_act, actm[("sqT", ui)]) if tp == "q"
                                     else (s_dve2, d2map[("knrm", ui)]))
                            bt.acquire(e, ("psb", 0), evac1)
                            for t in range(NT):
                                mm = e.matmul(psb[0][:, t * 128:(t + 1) * 128],
                                              src_fm[:, t * 128:(t + 1) * 128],
                                              idb[:], start=(t == 0),
                                              stop=(t == NT - 1), is_transpose=True)
                            s_pe2.inc_ev(mm, pe2m,
                                         ("trq" if tp == "q" else "trk1", ui))

                        def p_att(ui, idx):
                            kv = idx // 4
                            hl = idx - 8 * pass_
                            wt.w(e, s_dve2, d2map[("cq", ui)])
                            kb = 768 * h

                            def pe_head(p):
                                for tl in (2 * p, 2 * p + 1):
                                    gt = h * 8 + tl
                                    c0, nch = kwin(gt)
                                    w = nch * 128
                                    moff = (3 - nch) * 128
                                    bank = (p % 2) * 2 + tl % 2
                                    bt.acquire(e, ("psf", bank),
                                               (s_act, actm[("exp", ui, tl)]))
                                    e.matmul(psf[bank][:, 0:w], idb[:],
                                             mskb[:, moff:moff + w],
                                             start=True, stop=False)
                                    s_pe2.inc_ev(e.matmul(
                                        psf[bank][:, 0:w],
                                        qao[:, hl * HALF + tl * 128:
                                            hl * HALF + (tl + 1) * 128],
                                        kfm[:, kv * 1280 + c0 * 128 - kb:
                                            kv * 1280 + c0 * 128 - kb + w],
                                        start=False, stop=True),
                                        pe2m, ("sc", ui, tl))

                            def pe_tail(p):
                                wt.w(e, s_dve2, d2map[("abf", ui, 2 * p + 1)])
                                bt.acquire(e, ("psb", p % 2),
                                           (s_dve2, d2map[("atTp", ui, p)]))
                                mms = [(tl, c)
                                       for tl in (2 * p, 2 * p + 1)
                                       for c in range(kwin(h * 8 + tl)[1])]
                                for i, (tl, c) in enumerate(mms):
                                    mm = e.matmul(
                                        psb[p % 2][0:128,
                                                   (tl % 2) * 384 + c * 128:
                                                   (tl % 2) * 384 + (c + 1) * 128],
                                        abf[:, asl(tl) * 384 + c * 128:
                                            asl(tl) * 384 + (c + 1) * 128],
                                        idb[:], start=(i == 0),
                                        stop=(i == len(mms) - 1),
                                        is_transpose=True)
                                s_pe2.inc_ev(mm, pe2m, ("atrp", ui, p))
                                wt.w(e, s_dve2, d2map[("atTp", ui, p)])
                                for tl in (2 * p, 2 * p + 1):
                                    c0, nch = kwin(h * 8 + tl)
                                    bt.acquire(e, ("psf", 4 + tl % 2),
                                               (s_act, actm[("out", ui, tl)]))
                                    for c in range(nch):
                                        mm = e.matmul(
                                            psf[4 + tl % 2][:, 0:128],
                                            vtm[:, (c0 + c - 6 * h) * 512
                                                + kv * 128:
                                                (c0 + c - 6 * h) * 512
                                                + (kv + 1) * 128],
                                            atT[:, (p % 3) * 768
                                                + (tl % 2) * 384 + c * 128:
                                                (p % 3) * 768
                                                + (tl % 2) * 384 + (c + 1) * 128],
                                            start=(c == 0), stop=(c == nch - 1))
                                    s_pe2.inc_ev(mm, pe2m, ("pv", ui, tl))

                            att_pipeline(pe_head, pe_tail)

                        for ui, (tp, idx) in enumerate(units):
                            if tp == "q":
                                break
                            p_proj(ui, tp)
                            if tp == "v":
                                wt.w(e, s_dve2, d2map[("vstg", ui)])
                                bt.acquire(e, ("psb", 0),
                                           (s_dve2, d2map[("vtm", ui)]))
                                for t in range(NT):
                                    mm = e.matmul(psb[0][:, t * 128:(t + 1) * 128],
                                                  xfer[:, t * 128:(t + 1) * 128],
                                                  idb[:], start=(t == 0),
                                                  stop=(t == NT - 1),
                                                  is_transpose=True)
                                s_pe2.inc_ev(mm, pe2m, ("vtr", ui))
                            else:
                                p_tr(ui, tp, idx, 0)
                                wt.w(e, s_dve2, d2map[("knrm", ui)])
                                bt.acquire(e, ("psb", 1),
                                           (s_dve2, d2map[("kfmev", ui)]))
                                for t in range(NT):
                                    mm = e.matmul(psb[1][:, t * 128:(t + 1) * 128],
                                                  xfer[:, t * 128:(t + 1) * 128],
                                                  idb[:], start=(t == 0),
                                                  stop=(t == NT - 1),
                                                  is_transpose=True)
                                s_pe2.inc_ev(mm, pe2m, ("trk2", ui))
                        for qi, ui in enumerate(q_us):
                            tp, idx = units[ui]
                            p_proj(ui, tp)
                            if qi >= 1:
                                p_att(q_us[qi - 1], units[q_us[qi - 1]][1])
                            p_tr(ui, tp, idx, qi)
                        p_att(q_us[-1], units[q_us[-1]][1])

                    @b.vector
                    def _(e):
                        wt = Waiter()
                        def v_d1(ui, tp, idx, qi):
                            qkprev = qk_us.index(ui) if ui in qk_us else None
                            if tp in ("k", "q") and qkprev is not None and qkprev >= 1:
                                wt.w(e, s_act, actm[("sqT", qk_us[qkprev - 1])])
                            for tb in range(2):
                                wt.w(e, s_pe, peb + 2 * ui + tb + 1)
                                e.tensor_copy(qraw[:, tb * 512:(tb + 1) * 512],
                                              psf[(ui % 2) * 2 + tb][:])
                            if tp == "v":
                                vs(e)
                                s_dve2.inc_ev(e.engine_nop(), d2map, ("ev", ui))
                                if idx == 0:
                                    wt.w(e, s_act, actm[("sqT", qk_us[NKV - 1])])
                                else:
                                    wt.w(e, s_pe2, pe2m[("vtr", ui - 1)])
                                e.tensor_copy(xfer[:], qraw[:])
                                vs(e)
                                s_dve2.inc_ev(e.engine_nop(), d2map, ("vstg", ui))
                                wt.w(e, s_pe2, pe2m[("vtr", ui)])
                                dst = vtm[:].rearrange("p (t c) -> p t c", c=512)[
                                    :, h * 2:h * 2 + 8, idx * 128:(idx + 1) * 128]
                                s_dve2.inc_ev(
                                    e.tensor_copy(dst, psb[0][:]
                                                  .rearrange("p (t c) -> p t c",
                                                             c=128)),
                                    d2map, ("vtm", ui))
                                return
                            if ui == 0:
                                wt.w(e, s_gc, s_gc.n)
                            vs(e)
                            s_dve2.inc_ev(e.engine_nop(), d2map, ("ev", ui))
                            e.tensor_copy(rot[0:64, :], qraw[64:128, :])
                            e.tensor_copy(rot[64:128, :], qraw[0:64, :])
                            vs(e)
                            e.tensor_mul(qraw[:], qraw[:], csb[:, 0:HALF])
                            e.tensor_mul(rot[:], rot[:], csb[:, HALF:])
                            vs(e)
                            s_dve2.inc_ev(e.tensor_add(qraw[:], qraw[:], rot[:]),
                                          d2map, ("qro", ui))
                            vs(e)
                            if tp == "q":
                                hl = idx - 8 * pass_
                                s_dve2.inc_ev(
                                    e.tensor_scalar(
                                        qao[:, hl * HALF:(hl + 1) * HALF],
                                        qraw[:], sgnb[:, idx:idx + 1], None,
                                        ALU.mult),
                                    d2map, ("bf", ui))
                            else:
                                s_dve2.inc_ev(e.tensor_copy(xfer[:], qraw[:]),
                                              d2map, ("bf", ui))

                        def v_d1q(ui, idx, qi):
                            qkprev = qk_us.index(ui)

                            def c0():
                                if qkprev >= 1:
                                    wt.w(e, s_act,
                                         actm[("sqT", qk_us[qkprev - 1])])
                                for tb in range(2):
                                    wt.w(e, s_pe, peb + 2 * ui + tb + 1)
                                    e.tensor_copy(
                                        qraw[:, tb * 512:(tb + 1) * 512],
                                        psf[(ui % 2) * 2 + tb][:])
                                if ui == 0:
                                    wt.w(e, s_gc, s_gc.n)
                                vs(e)
                                s_dve2.inc_ev(e.engine_nop(), d2map,
                                              ("ev", ui))

                            def c1():
                                e.tensor_copy(rot[0:64, :], qraw[64:128, :])
                                e.tensor_copy(rot[64:128, :], qraw[0:64, :])

                            def c2():
                                vs(e)
                                e.tensor_mul(qraw[:], qraw[:], csb[:, 0:HALF])
                                e.tensor_mul(rot[:], rot[:], csb[:, HALF:])
                                vs(e)
                                s_dve2.inc_ev(
                                    e.tensor_add(qraw[:], qraw[:], rot[:]),
                                    d2map, ("qro", ui))

                            def c3():
                                vs(e)
                                hl = idx - 8 * pass_
                                s_dve2.inc_ev(
                                    e.tensor_scalar(
                                        qao[:, hl * HALF:(hl + 1) * HALF],
                                        qraw[:], sgnb[:, idx:idx + 1], None,
                                        ALU.mult),
                                    d2map, ("bf", ui))

                            return [c0, c1, c2, c3]

                        def v_norm(ui, tp, qi):
                            sl = (qi % 2) if tp == "q" else 0
                            wt.w(e, s_act, actm[("sqT", ui)])
                            e.reduce_sum(
                                sscol[:, sl * 8:(sl + 1) * 8]
                                .rearrange("p (t o) -> p t o", t=8),
                                sqtm[:, sl * HALF:(sl + 1) * HALF]
                                .rearrange("p (t c) -> p t c", t=8),
                                axis=AX.X)
                            vs(e)
                            s_dve2.inc_ev(e.tensor_scalar(
                                sscol[:, sl * 8:(sl + 1) * 8],
                                sscol[:, sl * 8:(sl + 1) * 8],
                                1e-24, None, ALU.max), d2map, ("ssm", ui))

                        def v_katt(ui, idx):
                            wt.w(e, s_act, actm[("rstd", ui)])
                            for t in range(NT):
                                e.tensor_scalar(xfer[:, t * 128:(t + 1) * 128],
                                                psb[0][:, t * 128:(t + 1) * 128],
                                                rstdc[:, t:t + 1], None, ALU.mult)
                            s_dve2.inc_ev(e.engine_nop(), d2map, ("knrm", ui))
                            wt.w(e, s_pe2, pe2m[("trk2", ui)])
                            s_dve2.inc_ev(e.tensor_copy(
                                kfm[:, idx * 1280 + 256 * h:
                                    idx * 1280 + 256 * h + HALF],
                                psb[1][:]), d2map, ("kfmev", ui))

                        def v_cq(ui, idx, qi):
                            sl = qi % 2
                            wt.w(e, s_act, actm[("rstd", ui)])
                            s_dve2.inc_ev(e.tensor_scalar(
                                cqc[:, sl * 8:(sl + 1) * 8],
                                rstdc[:, sl * 8:(sl + 1) * 8],
                                tempab[:, idx:idx + 1], None, ALU.mult),
                                d2map, ("cq", ui))

                        def v_tails(ui, idx):
                            def mk(p):
                                def v_tail():
                                    t0, t1 = 2 * p, 2 * p + 1
                                    c = asl(t0)
                                    wt.w(e, s_act, actm[("exp", ui, t1)])
                                    e.reciprocal(cden[:, c:c + 1],
                                                 cden[:, c:c + 1])
                                    s_dve2.inc_ev(e.reciprocal(
                                        cden[:, c + 1:c + 2],
                                        cden[:, c + 1:c + 2]),
                                        d2map, ("rec", ui, p))
                                    for tl in (t0, t1):
                                        wq_ = (min(h * 8 + tl, 2) + 1) * 128
                                        s_dve2.inc_ev(e.tensor_scalar(
                                            abf[:, asl(tl) * 384:
                                                asl(tl) * 384 + wq_],
                                            exb[:, asl(tl) * 384:
                                                asl(tl) * 384 + wq_],
                                            cden[:, asl(tl):asl(tl) + 1],
                                            None, ALU.mult),
                                            d2map, ("abf", ui, tl))
                                    wt.w(e, s_pe2, pe2m[("atrp", ui, p)])
                                    s_dve2.inc_ev(e.tensor_copy(
                                        atT[:, (p % 3) * 768:
                                            (p % 3) * 768 + 768],
                                        psb[p % 2][:, 0:768]),
                                        d2map, ("atTp", ui, p))
                                return v_tail
                            return [mk(p) for p in range(4)]

                        for ui, (tp, idx) in enumerate(units):
                            if tp == "q":
                                break
                            v_d1(ui, tp, idx, 0)
                            if tp == "k":
                                v_norm(ui, tp, 0)
                                v_katt(ui, idx)
                        for qi, ui in enumerate(q_us):
                            tp, idx = units[ui]
                            C = v_d1q(ui, idx, qi)
                            if qi >= 1:
                                pu = q_us[qi - 1]
                                T = v_tails(pu, units[pu][1])
                                seq = [C[0], C[1], T[0], C[2], T[1], C[3],
                                       T[2], T[3]]
                            else:
                                seq = C
                            for f in seq:
                                f()
                            v_norm(ui, tp, qi)
                            v_cq(ui, idx, qi)
                        for f in v_tails(q_us[-1], units[q_us[-1]][1]):
                            f()

        def attn_oproj(h, pass_, qao, wstage):
            with contextlib.ExitStack() as ph:
                wobf = ph.enter_context(
                    nc.sbuf_tensor(f"wobf_{next(_uid)}", [128, 2 * 2048], BF16))
                wab = [s.n for s in s_wa]
                qs = [2 * pass_, 2 * pass_ + 1]
                cast_ev = [("q", dblk, j) for dblk in range(4) for j in range(2)]
                pe2_ev = [("o", dblk, t) for dblk in range(4) for t in range(NT)]
                d2_ev = [("oev", dblk, t) for dblk in range(4) for t in range(NT)]
                cmap = s_cast.evmap(cast_ev)
                pe2m = s_pe2.evmap(pe2_ev)
                d2map = s_dve2.evmap(d2_ev)

                with nc.Block() as b:
                    @b.sync
                    def _(e):
                        for dblk in range(4):
                            for j in range(2):
                                ui = dblk * 2 + j
                                if ui >= 2:
                                    pd, pj = divmod(ui - 2, 2)
                                    e.wait_ge(s_cast.h, cmap[("q", pd, pj)])
                                s_wa[ui % 2].inc(e.dma_start(
                                    wstage[:, (ui % 2) * 2048:(ui % 2 + 1) * 2048]
                                    .rearrange("p (a c) -> p a c", a=4),
                                    wo_in[dblk, qs[j]]), 16)

                    @b.scalar
                    def _(e):
                        for dblk in range(4):
                            for j in range(2):
                                ui = dblk * 2 + j
                                e.wait_ge(s_wa[ui % 2].h,
                                          wab[ui % 2] + 16 * (ui // 2 + 1))
                                if dblk >= 1:
                                    e.wait_ge(s_pe2.h,
                                              pe2m[("o", dblk - 1, NT - 1)])
                                s_cast.inc_ev(e.activation(
                                    wobf[:, j * 2048:(j + 1) * 2048],
                                    wstage[:, (ui % 2) * 2048:(ui % 2 + 1) * 2048],
                                    AF.Copy), cmap, ("q", dblk, j))

                    @b.tensor
                    def _(e):
                        bt = Banks()
                        for dblk in range(4):
                            for t in range(NT):
                                gi = dblk * 8 + t
                                bt.acquire(e, ("psf", gi % 6),
                                           (s_dve2, d2map[("oev", dblk, t)]))
                                for j in range(2):
                                    if t == 0:
                                        e.wait_ge(s_cast.h, cmap[("q", dblk, j)])
                                    for a in range(4):
                                        hl = j * 4 + a
                                        mm = e.matmul(
                                            psf[gi % 6][:],
                                            qao[:, hl * HALF + t * 128:
                                                hl * HALF + (t + 1) * 128],
                                            wobf[:, j * 2048 + a * 512:
                                                 j * 2048 + (a + 1) * 512],
                                            start=(j == 0 and a == 0),
                                            stop=(j == 1 and a == 3))
                                s_pe2.inc_ev(mm, pe2m, ("o", dblk, t))

                    @b.vector
                    def _(e):
                        for dblk in range(4):
                            for t in range(NT):
                                gi = dblk * 8 + t
                                e.wait_ge(s_pe2.h, pe2m[("o", dblk, t)])
                                s_dve2.inc_ev(e.tensor_add(
                                    r_sb[:, t * D + dblk * 512:t * D + (dblk + 1) * 512],
                                    psf[gi % 6][:],
                                    r_sb[:, t * D + dblk * 512:t * D + (dblk + 1) * 512]),
                                    d2map, ("oev", dblk, t))

        def phase_store(h):
            with nc.Block() as b:
                @b.gpsimd
                def _(e):
                    for t in range(NT):
                        s_st.inc(e.dma_start(
                            out[h * HALF + t * 128:h * HALF + (t + 1) * 128, :],
                            r_sb[:, t * D:(t + 1) * D]), 16)

        def tap_phase(label, h):
            if h != 0 or label not in DEBUG_TAPS:
                return
            rt = tap_dram(label + "_r", (128, NT * D))
            xt = tap_dram(label + "_xln", (128, NDK * HALF), BF16)
            sct = tap_dram(label + "_scal", (128, 64))
            with nc.Block() as b:
                @b.gpsimd
                def _(e):
                    n0 = s_g.n
                    s_g.inc(e.dma_start(rt[:], r_sb[:]), 16)
                    s_g.inc(e.dma_start(xt[:], xln[:]), 16)
                    s_g.inc(e.dma_start(sct[:], scalbuf[:]), 16)
                    e.wait_ge(s_g.h, s_g.n)

        for h in range(2):
            phase_ln_gate(h, src_hbm=True, gate=True)
            tap_phase("ln1", h)
            phase_ffn(h, build_mw=(h == 0))
            tap_phase("ffn1", h)
            phase_attn(h)
            tap_phase("attn", h)
            phase_ln_gate(h, src_hbm=False, gate=True)
            tap_phase("ln3", h)
            phase_ffn(h, build_mw=False)
            tap_phase("ffn2", h)
            phase_store(h)

        with nc.Block() as b:
            @b.gpsimd
            def _(e):
                e.wait_ge(s_st.h, s_st.n)

    return nc


def _stage_weights(i):
    f = np.float32
    gpw = np.asarray(i["L0_ffn_gate_proj_w"], f)
    upw = np.asarray(i["L0_ffn_up_proj_w"], f)
    dpw = np.asarray(i["L0_ffn_down_proj_w"], f)
    Wq = np.asarray(i["L0_attn_Wq"], f)
    Wkv = np.asarray(i["L0_attn_Wkv"], f)
    Wo = np.asarray(i["L0_attn_Wo"], f)
    gw = np.asarray(i["L0_ffn_gate_w"], f)
    mkw = np.asarray(i["L0_ffn_mask_up_w"], f)

    def gu_tiles(w):
        return np.ascontiguousarray(
            w.reshape(NG, NICH, 128, NDK, 128).transpose(0, 1, 4, 3, 2))

    return dict(
        wg=gu_tiles(gpw),
        wu=gu_tiles(upw),
        wd=np.ascontiguousarray(
            dpw.reshape(NG, 4, 512, NICH, 128).transpose(0, 1, 3, 4, 2)),
        wq=np.ascontiguousarray(
            Wq.reshape(NH, 128, NDK, 128).transpose(0, 3, 2, 1)),
        wkv=np.ascontiguousarray(
            Wkv.reshape(8, 128, NDK, 128).transpose(0, 3, 2, 1)),
        wo=np.ascontiguousarray(
            Wo.reshape(4, 512, 4, 4, 128).transpose(0, 2, 4, 3, 1)),
        gw=np.ascontiguousarray(
            gw.T.reshape(NDK, 128, 64).transpose(1, 0, 2)).reshape(128, NDK * 64),
        mkw=np.ascontiguousarray(
            mkw.reshape(NG, NDK, 128, 8).transpose(2, 0, 1, 3)).reshape(128, NG * NDK * 8),
    )


def _constants():
    f = np.float32
    pos = np.arange(SEQ, dtype=np.float64)
    invfreq = 1.0 / (10000.0 ** (np.arange(0, DH, 2, dtype=np.float64) / DH))
    ang = np.concatenate([invfreq, invfreq])[:, None] * pos[None, :]
    cos = np.cos(ang).astype(f)
    sin = np.sin(ang).astype(f)
    sin[:64] = -sin[:64]
    cs = np.concatenate([cos, sin], axis=1)

    ii = np.arange(128)[:, None]
    jj = np.arange(384)[None, :]
    mI = np.where((jj > ii) & (jj <= ii + 256), 0.0, NEG).astype(f)

    return dict(
        cs=cs,
        mskh=mI.astype(bfdt),
        identb=np.eye(128).astype(bfdt),
        identf=np.eye(128, dtype=f),
        iota64=np.arange(64, dtype=f).reshape(64, 1),
        onesrow=np.ones((1, 128), f),
        onescol=np.ones((128, 1), bfdt),
    )


_NC_CACHE = {}


def kernel(**inputs):
    x = np.asarray(inputs["L0_attn_x"], np.float32)
    shared = _stage_weights(inputs)
    shared.update(_constants())
    shared["qkt"] = np.ascontiguousarray(np.broadcast_to(
        np.asarray(inputs["L0_attn_qk_temp"], np.float32).reshape(1, NH),
        (128, NH)))
    shared["inv"] = np.ascontiguousarray(np.broadcast_to(
        np.asarray(inputs["L0_ffn_inv_map"]).astype(np.int32).reshape(1, 64),
        (64, 64)))

    if "nc" not in _NC_CACHE:
        _NC_CACHE["nc"] = build_program()
    nc = _NC_CACHE["nc"]

    in_maps = []
    for c in range(8):
        m = dict(shared)
        m["x"] = np.ascontiguousarray(x[c])
        in_maps.append(m)
    res = run_bass_kernel_spmd(nc, in_maps, core_ids=list(range(8)))
    return np.stack([np.asarray(res.results[c]["out"]) for c in range(8)], axis=0)


# version 17
# speedup vs baseline: 1.0127x; 1.0099x over previous
import contextlib
import numpy as np
import ml_dtypes

import concourse.bass as bass
import concourse.mybir as mybir
from concourse.bass_utils import run_bass_kernel_spmd

F32 = mybir.dt.float32
BF16 = mybir.dt.bfloat16
I32 = mybir.dt.int32
AF = mybir.ActivationFunctionType
ALU = mybir.AluOpType
AX = mybir.AxisListType
bfdt = ml_dtypes.bfloat16

SEQ, D, HALF, NT = 2048, 2048, 1024, 8
NDK = 16
NG, NI, NICH = 8, 1408, 11
NH, NKV, DH = 16, 4, 128
NEG = -1.0e30
DEBUG_TAPS = ()


class S:
    def __init__(self, stack, nc, name):
        self.h = stack.enter_context(nc.semaphore(name))
        self.n = 0

    def inc(self, instr, k=1):
        instr.then_inc(self.h, k)
        self.n += k
        return self.n

    def evmap(self, schedule, step=1):
        m = {}
        n = self.n
        for key in schedule:
            n += step
            m[key] = n
        return m

    def inc_ev(self, instr, m, key, step=1):
        assert self.n + step == m[key], (key, self.n, m[key])
        return self.inc(instr, step)


class Waiter:

    def __init__(self):
        self.seen = {}

    def w(self, e, sem, cnt):
        if self.seen.get(id(sem), -1) >= cnt:
            return
        e.wait_ge(sem.h, cnt)
        self.seen[id(sem)] = cnt


class Banks:

    def __init__(self, wt=None):
        self.last = {}
        self.wt = wt

    def acquire(self, e, bank_id, evac):
        prev = self.last.get(bank_id)
        if prev is not None:
            sem, cnt = prev
            if self.wt is not None:
                self.wt.w(e, sem, cnt)
            else:
                e.wait_ge(sem.h, cnt)
        self.last[bank_id] = evac


def build_program():
    nc = bass.Bass("TRN2", target_bir_lowering=False)

    def din(name, shape, dt=F32):
        return nc.dram_tensor(name, list(shape), dt, kind="ExternalInput").ap()

    x_in = din("x", (SEQ, D))
    wg_in = din("wg", (NG, NICH, 128, NDK, 128))
    wu_in = din("wu", (NG, NICH, 128, NDK, 128))
    wd_in = din("wd", (NG, 4, NICH, 128, 512))
    wq_in = din("wq", (NH, 128, NDK, 128))
    wkv_in = din("wkv", (NKV * 2, 128, NDK, 128))
    wo_in = din("wo", (4, 4, 128, 4, 512))
    gw_in = din("gw", (128, NDK * 64))
    mkw_in = din("mkw", (128, NG * NDK * 8))
    qkt_in = din("qkt", (128, NH))
    inv_in = din("inv", (64, 64), I32)
    cs_in = din("cs", (128, 2 * SEQ))
    mskh_in = din("mskh", (128, 384), BF16)
    idb_in = din("identb", (128, 128), BF16)
    idf_in = din("identf", (128, 128))
    iota_in = din("iota64", (64, 1))
    onr_in = din("onesrow", (1, 128))
    onc_in = din("onescol", (128, 1), BF16)

    out = nc.dram_tensor("out", [SEQ, D], F32, kind="ExternalOutput").ap()

    soft_dram = nc.dram_tensor("soft_scr", [NG, 8, SEQ], BF16, kind="Internal").ap()
    mw_dram = nc.dram_tensor("mw_scr", [2, NG, 8, NI], BF16, kind="Internal").ap()
    kv_dram = nc.dram_tensor("kv_scr", [2, 128, 4096], BF16, kind="Internal").ap()

    import itertools
    _uid = itertools.count()

    tap_outs = {}

    def tap_dram(label, shape, dt=F32):
        t = nc.dram_tensor("tap_" + label, list(shape), dt, kind="ExternalOutput")
        tap_outs[label] = t.ap()
        return tap_outs[label]

    st = contextlib.ExitStack()
    with st:
        def sb(name, p, f, dt):
            return st.enter_context(nc.sbuf_tensor(name, [p, f], dt))

        r_sb = sb("r_sb", 128, NT * D, F32)
        xln = sb("xln", 128, NDK * HALF, BF16)
        scalbuf = sb("scalbuf", 128, 64, F32)
        tempb = sb("tempb", 128, 16, F32)
        tempnb = sb("tempnb", 128, 16, F32)
        tempab = sb("tempab", 128, 16, F32)
        sgnb = sb("sgnb", 128, 16, F32)
        Pm = sb("Pm", 64, 64, F32)
        idb = sb("idb", 128, 128, BF16)
        idf = sb("idf", 128, 128, F32)
        iota = sb("iota", 64, 1, F32)
        onr = sb("onr", 1, 128, F32)
        onc = sb("onc", 128, 1, BF16)
        invf = sb("invf", 64, 64, F32)
        invi = sb("invi", 64, 64, I32)
        junkc = sb("junkc", 128, 1, F32)
        epsc = sb("epsc", 128, 1, F32)

        psf = [st.enter_context(nc.psum_tensor(f"psf{i}", [128, 512], F32))
               for i in range(6)]
        psb = [st.enter_context(nc.psum_tensor(f"psb{i}", [128, 1024], BF16))
               for i in range(2)]

        s_x = S(st, nc, "s_x")
        s_w = S(st, nc, "s_w")
        s_g = S(st, nc, "s_g")
        s_st = S(st, nc, "s_st")
        s_pe = S(st, nc, "s_pe")
        s_pe2 = S(st, nc, "s_pe2")
        s_cast = S(st, nc, "s_cast")
        s_act = S(st, nc, "s_act")
        s_dve = S(st, nc, "s_dve")
        s_dve2 = S(st, nc, "s_dve2")
        s_dve3 = S(st, nc, "s_dve3")
        s_sv = S(st, nc, "s_sv")
        s_sa = S(st, nc, "s_sa")
        s_pl = S(st, nc, "s_pl")
        s_x2 = [S(st, nc, f"s_x2_{i}") for i in range(2)]
        s_wf = [S(st, nc, f"s_wf_{i}") for i in range(3)]
        s_wa = [S(st, nc, f"s_wa_{i}") for i in range(2)]
        s_gs = [S(st, nc, f"s_gs_{i}") for i in range(2)]
        s_gml = [S(st, nc, f"s_gml_{i}") for i in range(2)]
        s_gms = [S(st, nc, f"s_gms_{i}") for i in range(2)]
        s_ss = [S(st, nc, f"s_ss_{i}") for i in range(2)]
        s_gc = S(st, nc, "s_gc")
        s_gkv = S(st, nc, "s_gkv")
        s_gw = S(st, nc, "s_gw")

        def vs(e):
            n = s_sv.inc(e.engine_nop())
            e.wait_ge(s_sv.h, n)

        def sca(e):
            n = s_sa.inc(e.activation(junkc[:], junkc[:], AF.Copy))
            e.wait_ge(s_sa.h, n)

        with nc.Block() as b:
            @b.gpsimd
            def _(e):
                for dst, src in [(idb, idb_in), (idf, idf_in), (iota, iota_in),
                                 (onr, onr_in), (onc, onc_in), (tempb, qkt_in),
                                 (invi, inv_in)]:
                    s_g.inc(e.dma_start(dst[:], src[:]), 16)
                e.wait_ge(s_g.h, s_g.n)

        with nc.Block() as b:
            @b.vector
            def _(e):
                e.tensor_scalar(tempnb[:], tempb[:], -1.0, None, ALU.mult)
                e.tensor_copy(invf[:], invi[:])
                e.memset(epsc[:], 1e-5)
                e.tensor_tensor(tempnb[:], tempnb[:], tempb[:], ALU.min)

        with nc.Block() as b:
            @b.vector
            def _(e):
                e.tensor_scalar(tempab[:], tempnb[:], -1.0, None, ALU.mult)
                e.tensor_scalar(Pm[:], invf[:], iota[:], None, ALU.is_equal)
                e.tensor_scalar(sgnb[:], tempab[:], 1e-30, None, ALU.max)
                vs(e)
                e.reciprocal(sgnb[:], sgnb[:])
                vs(e)
                e.tensor_tensor(sgnb[:], sgnb[:], tempb[:], ALU.mult)

        def phase_ln_gate(h, src_hbm, gate):
            with contextlib.ExitStack() as ph:
                _u = next(_uid)
                def pb_(name, p, f, dt):
                    return ph.enter_context(
                        nc.sbuf_tensor(f"{name}_{_u}", [p, f], dt))
                xtile = pb_("xtile", 128, 2 * D, F32) if src_hbm else None
                sq = pb_("sq", 128, 2 * D, F32)
                xlt = pb_("xlt", 128, 2 * D, BF16)
                cvar = pb_("cvar", 128, 2, F32)
                cmean = pb_("cmean", 128, 2, F32)
                crstd = pb_("crstd", 128, 2, F32)
                ccol = pb_("ccol", 128, 8, F32)
                if gate:
                    gwt = pb_("gwt", 128, NDK * 64, BF16)
                    gstg = pb_("gstg", 128, NDK * 64, F32)
                    scs = pb_("scs", 64, 2 * 128, F32)
                    sc = pb_("sc", 128, 2 * 64, F32)
                    sce = pb_("sce", 128, 2 * 64, F32)
                    cden = pb_("cden", 128, 2, F32)
                    wesm = pb_("wesm", 128, 2 * 64, F32)
                    vv = pb_("vv", 128, 2 * 64, F32)
                    mskt = pb_("mskt", 128, 2 * 64, F32)
                    wee = pb_("wee", 128, 2 * 64, F32)
                    weT = pb_("weT", 64, 2 * 128, F32)
                    flat = pb_("flat", 128, 2 * 64, F32)
                    fmn = pb_("fmn", 128, 2 * 64, F32)
                    gs8 = pb_("gs8", 128, 2 * 8, F32)
                    gm8 = pb_("gm8", 128, 2 * 8, F32)
                    gr8 = pb_("gr8", 128, 2 * 8, F32)
                    gsm = pb_("gsm", 128, 2 * 64, F32)
                    gex = pb_("gex", 128, 2 * 64, F32)
                    sfb = pb_("sfb", 128, 2 * 64, BF16)
                    sstg = pb_("sstg", 8, 2 * 1024, BF16)

                xb2 = [s.n for s in s_x2]
                ssb = [s.n for s in s_ss]
                gwb = s_gw.n
                db, pb0, stb = s_dve.n, s_pe.n, s_st.n
                EM = {}
                d2m, d3m, am, p2m, dvm, cm = {}, {}, {}, {}, {}, {}

                def dve_tile(t):
                    p = t % 2
                    xt = (xtile[:, p * D:(p + 1) * D] if src_hbm
                          else r_sb[:, t * D:(t + 1) * D])
                    L = []
                    A = L.append
                    if src_hbm:
                        A((lambda e: e.wait_ge(s_x2[t % 2].h,
                                               xb2[t % 2] + 16 * (t // 2 + 1)), []))
                    A((lambda e: e.tensor_mul(sq[:, p * D:(p + 1) * D], xt, xt), []))
                    A((lambda e: e.reduce_sum(ccol[:, 4 * p:4 * p + 1], xt,
                                              axis=AX.X), []))
                    A((lambda e: e.reduce_sum(ccol[:, 4 * p + 1:4 * p + 2],
                                              sq[:, p * D:(p + 1) * D],
                                              axis=AX.X), []))
                    A((lambda e: e.tensor_scalar(cmean[:, p:p + 1],
                                                 ccol[:, 4 * p:4 * p + 1],
                                                 1.0 / D, None, ALU.mult), []))
                    A((lambda e: e.tensor_scalar(ccol[:, 4 * p + 1:4 * p + 2],
                                                 ccol[:, 4 * p + 1:4 * p + 2],
                                                 1.0 / D, None, ALU.mult), []))
                    A((lambda e: e.tensor_mul(ccol[:, 4 * p + 2:4 * p + 3],
                                              cmean[:, p:p + 1],
                                              cmean[:, p:p + 1]), []))
                    if src_hbm:
                        def rinit(e):
                            if h == 1 and t == 0:
                                e.wait_ge(s_st.h, stb)
                            e.tensor_copy(r_sb[:, t * D:(t + 1) * D], xt)
                        A((rinit, []))
                    A((lambda e: s_dve2.inc_ev(e.scalar_tensor_tensor(
                        cvar[:, p:p + 1], ccol[:, 4 * p + 2:4 * p + 3], -1.0,
                        ccol[:, 4 * p + 1:4 * p + 2], ALU.mult, ALU.add),
                        d2m, ("var", t)), [("d2", ("var", t))]))

                    def pe_tr_cnt(tt, grp):
                        return pb0 + (tt // 2) * 4 + grp * 2 + (tt % 2) + 1

                    def xlt_op(e):
                        if t >= 2:
                            e.wait_ge(s_pe.h, pe_tr_cnt(t - 2, 1))
                        e.wait_ge(s_act.h, am[("rstd", t)])
                        s_dve.inc_ev(e.tensor_scalar(
                            xlt[:, p * D:(p + 1) * D], xt, cmean[:, p:p + 1],
                            crstd[:, p:p + 1], ALU.subtract, ALU.mult),
                            dvm, ("xlt", t))
                    A((xlt_op, [("dv", ("xlt", t))]))
                    for grp in range(2):
                        def evac(e, grp=grp):
                            e.wait_ge(s_pe.h, pe_tr_cnt(t, grp))
                            dst = xln[:, grp * 8 * HALF:(grp * 8 + 8) * HALF] \
                                .rearrange("p (dk c) -> p dk c", dk=8)[:, :, t * 128:(t + 1) * 128]
                            s_dve3.inc_ev(
                                e.tensor_copy(dst, psb[grp][:]
                                              .rearrange("p (dk c) -> p dk c", dk=8)),
                                d3m, (f"ev{grp}", t))
                        A((evac, [("d3", (f"ev{grp}", t))]))
                    if not gate:
                        return L

                    def scs_ev(e):
                        e.wait_ge(s_pe2.h, p2m[("scores", t)])
                        s_dve2.inc_ev(e.tensor_copy(
                            scs[:, p * 128:(p + 1) * 128],
                            psf[2 if p == 0 else 5][0:64, 0:128]),
                            d2m, ("scs", t))
                    A((scs_ev, [("d2", ("scs", t))]))

                    def sc_ev(e):
                        e.wait_ge(s_pe2.h, p2m[("sctr", t)])
                        s_dve2.inc_ev(e.tensor_copy(sc[:, p * 64:(p + 1) * 64],
                                                    psf[3][0:128, 0:64]),
                                      d2m, ("scev", t))
                    A((sc_ev, [("d2", ("scev", t))]))
                    A((lambda e: s_dve2.inc_ev(
                        e.reduce_max(ccol[:, 4 * p + 3:4 * p + 4],
                                     sc[:, p * 64:(p + 1) * 64],
                                     axis=AX.X, negate=True),
                        d2m, ("negmax", t)), [("d2", ("negmax", t))]))

                    def recip_op(e):
                        e.wait_ge(s_act.h, am[("exp64", t)])
                        e.reciprocal(cden[:, p:p + 1], cden[:, p:p + 1])
                    A((recip_op, []))
                    A((lambda e: e.tensor_scalar(wesm[:, p * 64:(p + 1) * 64],
                                                 sce[:, p * 64:(p + 1) * 64],
                                                 cden[:, p:p + 1], None,
                                                 ALU.mult), []))
                    A((lambda e: e.tensor_copy(vv[:, p * 64:(p + 1) * 64],
                                               wesm[:, p * 64:(p + 1) * 64]), []))
                    for it in range(8):
                        A((lambda e: e.reduce_max(ccol[:, 4 * p + 3:4 * p + 4],
                                                  vv[:, p * 64:(p + 1) * 64],
                                                  axis=AX.X), []))
                        if it < 7:
                            A((lambda e: e.tensor_scalar(
                                mskt[:, p * 64:(p + 1) * 64],
                                vv[:, p * 64:(p + 1) * 64],
                                ccol[:, 4 * p + 3:4 * p + 4],
                                NEG, ALU.is_ge, ALU.mult), []))
                            A((lambda e: e.tensor_add(
                                vv[:, p * 64:(p + 1) * 64],
                                vv[:, p * 64:(p + 1) * 64],
                                mskt[:, p * 64:(p + 1) * 64]), []))
                    A((lambda e: e.tensor_scalar(mskt[:, p * 64:(p + 1) * 64],
                                                 wesm[:, p * 64:(p + 1) * 64],
                                                 ccol[:, 4 * p + 3:4 * p + 4],
                                                 None, ALU.is_ge), []))
                    A((lambda e: s_dve2.inc_ev(
                        e.tensor_mul(wee[:, p * 64:(p + 1) * 64],
                                     mskt[:, p * 64:(p + 1) * 64],
                                     wesm[:, p * 64:(p + 1) * 64]),
                        d2m, ("we", t)), [("d2", ("we", t))]))

                    def weT_ev(e):
                        e.wait_ge(s_pe2.h, p2m[("wetr", t)])
                        s_dve2.inc_ev(e.tensor_copy(
                            weT[:, p * 128:(p + 1) * 128],
                            psf[4][0:64, 0:128]),
                            d2m, ("weT", t))
                    A((weT_ev, [("d2", ("weT", t))]))

                    def flat_ev(e):
                        e.wait_ge(s_pe2.h, p2m[("flat", t)])
                        s_dve2.inc_ev(e.tensor_copy(flat[:, p * 64:(p + 1) * 64],
                                                    psf[0][0:128, 0:64]),
                                      d2m, ("flatev", t))
                    A((flat_ev, [("d2", ("flatev", t))]))
                    A((lambda e: e.tensor_scalar(mskt[:, p * 64:(p + 1) * 64],
                                                 flat[:, p * 64:(p + 1) * 64],
                                                 0.0, -1.0e9, ALU.is_equal,
                                                 ALU.mult), []))
                    A((lambda e: e.reduce_sum(
                        gs8[:, p * 8:(p + 1) * 8].rearrange("p (g o) -> p g o", g=8),
                        flat[:, p * 64:(p + 1) * 64].rearrange("p (g i) -> p g i", g=8),
                        axis=AX.X), []))
                    A((lambda e: e.tensor_add(fmn[:, p * 64:(p + 1) * 64],
                                              flat[:, p * 64:(p + 1) * 64],
                                              mskt[:, p * 64:(p + 1) * 64]), []))
                    A((lambda e: e.tensor_scalar(scalbuf[:, t * 8:(t + 1) * 8],
                                                 gs8[:, p * 8:(p + 1) * 8],
                                                 0.5, None, ALU.mult), []))
                    A((lambda e: e.reduce_max(
                        gm8[:, p * 8:(p + 1) * 8].rearrange("p (g o) -> p g o", g=8),
                        fmn[:, p * 64:(p + 1) * 64].rearrange("p (g i) -> p g i", g=8),
                        axis=AX.X), []))
                    A((lambda e: s_dve2.inc_ev(e.tensor_tensor(
                        gsm[:, p * 64:(p + 1) * 64].rearrange("p (g i) -> p g i", g=8),
                        fmn[:, p * 64:(p + 1) * 64].rearrange("p (g i) -> p g i", g=8),
                        gm8[:, p * 8:(p + 1) * 8].rearrange("p (g o) -> p g o", g=8)
                        .broadcast_to((128, 8, 8)), ALU.subtract),
                        d2m, ("gsm", t)), [("d2", ("gsm", t))]))

                    def gsum_op(e):
                        e.wait_ge(s_act.h, am[("gex", t)])
                        e.reduce_sum(
                            gs8[:, p * 8:(p + 1) * 8].rearrange("p (g o) -> p g o", g=8),
                            gex[:, p * 64:(p + 1) * 64].rearrange("p (g i) -> p g i", g=8),
                            axis=AX.X)
                    A((gsum_op, []))
                    A((lambda e: e.reciprocal(gr8[:, p * 8:(p + 1) * 8],
                                              gs8[:, p * 8:(p + 1) * 8]), []))
                    A((lambda e: s_dve2.inc_ev(e.tensor_tensor(
                        sfb[:, p * 64:(p + 1) * 64].rearrange("p (g i) -> p g i", g=8),
                        gex[:, p * 64:(p + 1) * 64].rearrange("p (g i) -> p g i", g=8),
                        gr8[:, p * 8:(p + 1) * 8].rearrange("p (g o) -> p g o", g=8)
                        .broadcast_to((128, 8, 8)), ALU.mult),
                        d2m, ("sfb", t)), [("d2", ("sfb", t))]))

                    def softev_op(e):
                        e.wait_ge(s_pe2.h, p2m[("softtr", t)])
                        if t >= 2:
                            e.wait_ge(s_ss[t % 2].h,
                                      ssb[t % 2] + 16 * ((t - 2) // 2 + 1))
                        s_dve3.inc_ev(e.tensor_copy(
                            sstg[0:8, p * 1024:(p + 1) * 1024]
                            .rearrange("i (g c) -> i g c", g=8),
                            psb[1][0:8, :].rearrange("i (g c) -> i g c", g=8)),
                            d3m, ("softev", t))
                        s_dve2.inc_ev(e.engine_nop(), d2m, ("sstg", t))
                    A((softev_op, [("d3", ("softev", t)), ("d2", ("sstg", t))]))
                    return L

                def act_tile(t):
                    p = t % 2
                    L = []

                    def rstd_op(e):
                        e.wait_ge(s_dve2.h, d2m[("var", t)])
                        e.activation(crstd[:, p:p + 1], cvar[:, p:p + 1],
                                     AF.Ln, bias=epsc[:])
                        s_act.inc_ev(e.activation(crstd[:, p:p + 1],
                                                  crstd[:, p:p + 1],
                                                  AF.Exp, scale=-0.5),
                                     am, ("rstd", t))
                    L.append((rstd_op, [("a", ("rstd", t))]))
                    if not gate:
                        return L

                    def exp64_op(e):
                        e.wait_ge(s_dve2.h, d2m[("negmax", t)])
                        s_act.inc_ev(
                            e.activation(sce[:, p * 64:(p + 1) * 64],
                                         sc[:, p * 64:(p + 1) * 64], AF.Exp,
                                         bias=ccol[:, 4 * p + 3:4 * p + 4],
                                         accum_out=cden[:, p:p + 1]),
                            am, ("exp64", t))
                    L.append((exp64_op, [("a", ("exp64", t))]))

                    def gex_op(e):
                        e.wait_ge(s_dve2.h, d2m[("gsm", t)])
                        s_act.inc_ev(e.activation(gex[:, p * 64:(p + 1) * 64],
                                                  gsm[:, p * 64:(p + 1) * 64],
                                                  AF.Exp), am, ("gex", t))
                    L.append((gex_op, [("a", ("gex", t))]))
                    return L

                def pe_tile(t):
                    p = t % 2
                    L = []
                    for grp in range(2):
                        def tr(e, grp=grp):
                            bt = EM["bt"]
                            e.wait_ge(s_dve.h, dvm[("xlt", t)])
                            bt.acquire(e, ("psb", grp),
                                       (s_dve3, d3m[(f"ev{grp}", t)]))
                            for k in range(8):
                                dk = grp * 8 + k
                                mm = e.matmul(
                                    psb[grp][:, k * 128:(k + 1) * 128],
                                    xlt[:, p * D + dk * 128:p * D + (dk + 1) * 128],
                                    idb[:], start=(k == 0), stop=(k == 7),
                                    is_transpose=True)
                            s_pe.inc(mm)
                        L.append((tr, []))
                    if not gate:
                        return L

                    def scores_op(e):
                        bt = EM["bt"]
                        e.wait_ge(s_dve3.h, d3m[("ev1", t)])
                        if t == 0:
                            e.wait_ge(s_cast.h, cm[("gwt",)])
                        bank = 2 if p == 0 else 5
                        bt.acquire(e, ("psf", bank), (s_dve2, d2m[("scs", t)]))
                        for dk in range(NDK):
                            mm = e.matmul(
                                psf[bank][0:64, 0:128],
                                gwt[:, dk * 64:(dk + 1) * 64],
                                xln[:, dk * HALF + t * 128:dk * HALF + (t + 1) * 128],
                                start=(dk == 0), stop=(dk == NDK - 1))
                        s_pe2.inc_ev(mm, p2m, ("scores", t))
                    L.append((scores_op, [("p2", ("scores", t))]))

                    def sctr_op(e):
                        bt = EM["bt"]
                        e.wait_ge(s_dve2.h, d2m[("scs", t)])
                        bt.acquire(e, ("psf", 3), (s_dve2, d2m[("scev", t)]))
                        s_pe2.inc_ev(e.matmul(psf[3][0:128, 0:64],
                                              scs[:, p * 128:(p + 1) * 128],
                                              idf[0:64, 0:64], start=True,
                                              stop=True, is_transpose=True),
                                     p2m, ("sctr", t))
                    L.append((sctr_op, [("p2", ("sctr", t))]))

                    def wetr_op(e):
                        bt = EM["bt"]
                        e.wait_ge(s_dve2.h, d2m[("we", t)])
                        bt.acquire(e, ("psf", 4), (s_dve2, d2m[("weT", t)]))
                        s_pe2.inc_ev(e.matmul(psf[4][0:64, 0:128],
                                              wee[:, p * 64:(p + 1) * 64],
                                              idf[:], start=True, stop=True,
                                              is_transpose=True),
                                     p2m, ("wetr", t))
                    L.append((wetr_op, [("p2", ("wetr", t))]))

                    def flat_op(e):
                        bt = EM["bt"]
                        e.wait_ge(s_dve2.h, d2m[("weT", t)])
                        bt.acquire(e, ("psf", 0), (s_dve2, d2m[("flatev", t)]))
                        e.matmul(psf[0][0:128, 0:64],
                                 weT[:, p * 128:(p + 1) * 128], Pm[:],
                                 start=True, stop=True)
                        s_pe2.inc_ev(e.matmul(
                            psf[2 if p == 0 else 5][0:1, 400:401],
                            idb[0:1, 0:1], idb[0:1, 0:1],
                            start=False, stop=True),
                                     p2m, ("flat", t))
                    L.append((flat_op, [("p2", ("flat", t))]))

                    def softtr_op(e):
                        bt = EM["bt"]
                        e.wait_ge(s_dve2.h, d2m[("sfb", t)])
                        bt.acquire(e, ("psb", 1), (s_dve3, d3m[("softev", t)]))
                        for g in range(8):
                            mm = e.matmul(psb[1][0:8, g * 128:(g + 1) * 128],
                                          sfb[:, p * 64 + g * 8:p * 64 + (g + 1) * 8],
                                          idb[:], start=(g == 0), stop=(g == 7),
                                          is_transpose=True)
                        s_pe2.inc_ev(mm, p2m, ("softtr", t))
                    L.append((softtr_op, [("p2", ("softtr", t))]))
                    return L

                def zipped(mk):
                    seq = []
                    for pp in range(NT // 2):
                        a = mk(2 * pp)
                        bo = mk(2 * pp + 1)
                        for i in range(max(len(a), len(bo))):
                            if i < len(a):
                                seq.append(a[i])
                            if i < len(bo):
                                seq.append(bo[i])
                    return seq

                dve_seq = zipped(dve_tile)
                act_seq = zipped(act_tile)
                pe_seq = zipped(pe_tile)

                def evs_of(seq, ch):
                    return [k for _, evs in seq for c, k in evs if c == ch]

                d2m.update(s_dve2.evmap(evs_of(dve_seq, "d2")))
                d3m.update(s_dve3.evmap(evs_of(dve_seq, "d3")))
                dvm.update(s_dve.evmap(evs_of(dve_seq, "dv")))
                am.update(s_act.evmap(evs_of(act_seq, "a")))
                p2m.update(s_pe2.evmap(evs_of(pe_seq, "p2")))
                if gate:
                    cm.update(s_cast.evmap([("gwt",)]))

                with nc.Block() as b:
                    if src_hbm:
                        @b.sync
                        def _(e):
                            for t in range(NT):
                                if t >= 2:
                                    e.wait_ge(s_dve.h, dvm[("xlt", t - 2)])
                                s_x2[t % 2].inc(e.dma_start(
                                    xtile[:, (t % 2) * D:(t % 2 + 1) * D],
                                    x_in[h * HALF + t * 128:h * HALF + (t + 1) * 128, :]),
                                    16)

                    if gate:
                        @b.gpsimd
                        def _(e):
                            s_gw.inc(e.dma_start(gstg[:], gw_in), 16)
                            for t in range(NT):
                                e.wait_ge(s_dve2.h, d2m[("sstg", t)])
                                s_ss[t % 2].inc(e.dma_start(
                                    soft_dram[:, :, h * HALF + t * 128:
                                              h * HALF + (t + 1) * 128]
                                    .rearrange("g i c -> i g c"),
                                    sstg[0:8, (t % 2) * 1024:(t % 2 + 1) * 1024]
                                    .rearrange("i (g c) -> i g c", g=8)), 16)

                    @b.scalar
                    def _(e):
                        if gate:
                            e.wait_ge(s_gw.h, gwb + 16)
                            s_cast.inc_ev(e.activation(gwt[:], gstg[:], AF.Copy),
                                          cm, ("gwt",))
                        for fn, _evs in act_seq:
                            fn(e)

                    @b.vector
                    def _(e):
                        for fn, _evs in dve_seq:
                            fn(e)

                    @b.tensor
                    def _(e):
                        EM["bt"] = Banks()
                        for fn, _evs in pe_seq:
                            fn(e)

        def phase_ffn(h, build_mw):
            with contextlib.ExitStack() as ph:
                _u = next(_uid)
                ap_ = ph.enter_context(
                    nc.sbuf_tensor(f"apA_{_u}", [128, NICH * HALF], BF16))
                silg = ph.enter_context(
                    nc.sbuf_tensor(f"silg_{_u}", [128, 2 * HALF], F32))
                wdbf = ph.enter_context(
                    nc.sbuf_tensor(f"wdbf_{_u}", [128, 2 * NICH * 512], BF16))
                mskw = (ph.enter_context(
                    nc.sbuf_tensor(f"mskw_{_u}", [128, NG * NDK * 8], BF16))
                    if build_mw else None)
                mstg = (ph.enter_context(
                    nc.sbuf_tensor(f"mstg_{_u}", [128, NG * NDK * 8], F32))
                    if build_mw else None)
                soft_rg = ph.enter_context(
                    nc.sbuf_tensor(f"softrg_{_u}", [8, 2 * HALF], BF16))
                fstage = ph.enter_context(
                    nc.sbuf_tensor(f"fstage_{_u}", [128, 3 * 2048], F32))
                fwbf = ph.enter_context(
                    nc.sbuf_tensor(f"fwbf_{_u}", [128, 3 * 2048], BF16))
                mw_rg = ph.enter_context(
                    nc.sbuf_tensor(f"mwrg_{_u}", [8, 2 * 2 * NI], BF16))

                wfb = [s.n for s in s_wf]
                gsb = [s.n for s in s_gs]
                gmlb = [s.n for s in s_gml]
                gmsb = [s.n for s in s_gms]
                ssb_tot = [s.n for s in s_ss]
                gwb = s_gw.n
                NU = 34
                down_units = [(dblk, u * 4, min(4, NICH - u * 4))
                              for dblk in range(4) for u in range(3)]

                cast_ev = ([("mk",)] if build_mw else [])
                pe_ev, pe2_ev, act_ev, dve_ev, d2_ev, d3_ev = [], [], [], [], [], []
                for g in range(NG):
                    for k in range(22):
                        cast_ev.append(("gu", g, k))
                    for ich in range(NICH):
                        if build_mw:
                            for m in range(2):
                                pe2_ev.append(("mw", g, ich, m))
                                d3_ev.append(("mwev", g, ich, m))
                        for m in range(2):
                            for tb in range(2):
                                pe_ev.append(("gu", g, ich, m, tb))
                                if m == 0:
                                    act_ev.append(("sil", g, ich, tb))
                                else:
                                    dve_ev.append(("ap", g, ich, tb))
                    for j in range(12):
                        cast_ev.append(("dn", g, j))
                    for dblk in range(4):
                        for t in range(NT):
                            pe2_ev.append(("dn", g, dblk, t))
                            d2_ev.append(("dnev", g, dblk, t))
                cmap = s_cast.evmap(cast_ev)
                pem = s_pe.evmap(pe_ev)
                pe2m = s_pe2.evmap(pe2_ev)
                actm = s_act.evmap(act_ev)
                dvem = s_dve.evmap(dve_ev)
                d2map = s_dve2.evmap(d2_ev)
                d3map = s_dve3.evmap(d3_ev)

                def cast_key(g, k):
                    return ("gu", g, k) if k < 22 else ("dn", g, k - 22)

                def glob_unit(ui):
                    return divmod(ui, NU)

                with nc.Block() as b:
                    @b.sync
                    def _(e):
                        for g in range(NG):
                            for k in range(NU):
                                ui = g * NU + k
                                if ui >= 3:
                                    pg, pk = glob_unit(ui - 3)
                                    e.wait_ge(s_cast.h, cmap[cast_key(pg, pk)])
                                slot = fstage[:, (ui % 3) * 2048:(ui % 3 + 1) * 2048]
                                sem = s_wf[ui % 3]
                                if k < 22:
                                    ich, m = divmod(k, 2)
                                    sem.inc(e.dma_start(
                                        slot.rearrange("p (dk a) -> p dk a", dk=NDK),
                                        (wg_in if m == 0 else wu_in)[g, ich]), 16)
                                else:
                                    dblk, i0, ni = down_units[k - 22]
                                    sem.inc(e.dma_start(
                                        slot.rearrange("p (i c) -> p i c", i=4)[:, 0:ni, :],
                                        wd_in[g, dblk, i0:i0 + ni]
                                        .rearrange("i p c -> p i c")), 16)

                    @b.gpsimd
                    def _(e):
                        if build_mw:
                            s_gw.inc(e.dma_start(mstg[:], mkw_in), 16)
                        for i in range(2):
                            e.wait_ge(s_ss[i].h, s_ss[i].n)
                        if not build_mw:
                            for i in range(2):
                                e.wait_ge(s_gms[i].h, s_gms[i].n)
                        for g in range(NG):
                            if g >= 2:
                                e.wait_ge(s_pe.h, pem[("gu", g - 2, NICH - 1, 1, 1)])
                            sslot = soft_rg[0:8, (g % 2) * HALF:(g % 2 + 1) * HALF]
                            mslot = mw_rg[0:8, (g % 2) * 2 * NI:(g % 2 + 1) * 2 * NI]
                            s_gs[g % 2].inc(e.dma_start(
                                sslot,
                                soft_dram[g, :, h * HALF:(h + 1) * HALF]), 16)
                            if build_mw:
                                e.wait_ge(s_dve3.h, d3map[("mwev", g, NICH - 1, 1)])
                                s_gms[g % 2].inc(e.dma_start(
                                    mw_dram[:, g].rearrange("m i c -> i m c"),
                                    mslot.rearrange("i (m c) -> i m c", m=2)), 16)
                            else:
                                s_gml[g % 2].inc(e.dma_start(
                                    mslot.rearrange("i (m c) -> i m c", m=2),
                                    mw_dram[:, g].rearrange("m i c -> i m c")), 16)

                    def ap_evac(e, g, ich):
                        for tb in range(2):
                            e.wait_ge(s_pe.h, pem[("gu", g, ich, 1, tb)])
                            e.wait_ge(s_act.h, actm[("sil", g, ich, tb)])
                            s_dve.inc_ev(e.tensor_mul(
                                ap_[:, ich * HALF + tb * 512:ich * HALF + (tb + 1) * 512],
                                psf[2 + tb][:],
                                silg[:, (ich % 2) * HALF + tb * 512:
                                     (ich % 2) * HALF + (tb + 1) * 512]),
                                dvem, ("ap", g, ich, tb))

                    def down_evac(e, g, dblk, t):
                        e.wait_ge(s_pe2.h, pe2m[("dn", g, dblk, t)])
                        s_dve2.inc_ev(e.scalar_tensor_tensor(
                            r_sb[:, t * D + dblk * 512:t * D + (dblk + 1) * 512],
                            psf[4 + (g * 32 + dblk * 8 + t) % 2][:],
                            scalbuf[:, t * 8 + g:t * 8 + g + 1],
                            r_sb[:, t * D + dblk * 512:t * D + (dblk + 1) * 512],
                            ALU.mult, ALU.add), d2map, ("dnev", g, dblk, t))

                    @b.vector
                    def _(e):
                        for g in range(NG):
                            for ich in range(NICH):
                                if build_mw:
                                    for m in range(2):
                                        if g >= 2 and ich == 0 and m == 0:
                                            e.wait_ge(s_gms[g % 2].h,
                                                      gmsb[g % 2]
                                                      + 16 * ((g - 2) // 2 + 1))
                                        e.wait_ge(s_pe2.h, pe2m[("mw", g, ich, m)])
                                        s_dve3.inc_ev(e.tensor_copy(
                                            mw_rg[0:8, (g % 2) * 2 * NI + m * NI
                                                  + ich * 128:(g % 2) * 2 * NI
                                                  + m * NI + (ich + 1) * 128],
                                            psf[4 + m][0:8, 0:128]),
                                            d3map, ("mwev", g, ich, m))
                                if ich >= 1:
                                    ap_evac(e, g, ich - 1)
                            ap_evac(e, g, NICH - 1)
                            for dblk in range(1, 4):
                                for t in range(NT):
                                    down_evac(e, g, dblk - 1, t)
                            for t in range(NT):
                                down_evac(e, g, 3, t)

                    @b.scalar
                    def _(e):
                        if build_mw:
                            e.wait_ge(s_gw.h, s_gw.n)
                            s_cast.inc_ev(e.activation(mskw[:], mstg[:], AF.Copy),
                                          cmap, ("mk",))

                        def cast_gu(e, g, ich):
                            for m in range(2):
                                k = ich * 2 + m
                                Kgu = g * 22 + k
                                ui = g * NU + k
                                e.wait_ge(s_wf[ui % 3].h,
                                          wfb[ui % 3] + 16 * (ui // 3 + 1))
                                if Kgu >= 3:
                                    pk = Kgu - 3
                                    pg, pkk = divmod(pk, 22)
                                    pich, pm = divmod(pkk, 2)
                                    e.wait_ge(s_pe.h, pem[("gu", pg, pich, pm, 1)])
                                s_cast.inc_ev(e.activation(
                                    fwbf[:, (Kgu % 3) * 2048:(Kgu % 3 + 1) * 2048],
                                    fstage[:, (ui % 3) * 2048:(ui % 3 + 1) * 2048],
                                    AF.Copy), cmap, ("gu", g, k))

                        def sil_pair(e, g, ich):
                            I_ = g * NICH + ich
                            for tb in range(2):
                                e.wait_ge(s_pe.h, pem[("gu", g, ich, 0, tb)])
                                if I_ >= 2:
                                    pg, pich = divmod(I_ - 2, NICH)
                                    e.wait_ge(s_dve.h, dvem[("ap", pg, pich, tb)])
                                s_act.inc_ev(e.activation(
                                    silg[:, (ich % 2) * HALF + tb * 512:
                                         (ich % 2) * HALF + (tb + 1) * 512],
                                    psf[tb][:], AF.Silu),
                                    actm, ("sil", g, ich, tb))

                        for g in range(NG):
                            for ich in range(NICH):
                                cast_gu(e, g, ich)
                                if ich >= 1:
                                    sil_pair(e, g, ich - 1)
                            sil_pair(e, g, NICH - 1)
                            for j, (dblk, i0, ni) in enumerate(down_units):
                                k = 22 + j
                                ui = g * NU + k
                                DJ = g * 4 + dblk
                                e.wait_ge(s_wf[ui % 3].h,
                                          wfb[ui % 3] + 16 * (ui // 3 + 1))
                                if DJ >= 2 and i0 == 0:
                                    pg, pd = divmod(DJ - 2, 4)
                                    e.wait_ge(s_pe2.h, pe2m[("dn", pg, pd, NT - 1)])
                                s_cast.inc_ev(e.activation(
                                    wdbf[:, (dblk % 2) * NICH * 512 + i0 * 512:
                                         (dblk % 2) * NICH * 512 + (i0 + ni) * 512],
                                    fstage[:, (ui % 3) * 2048:(ui % 3) * 2048 + ni * 512],
                                    AF.Copy), cmap, ("dn", g, j))

                    @b.tensor
                    def _(e):
                        bt = Banks()
                        for g in range(NG):
                            for ich in range(NICH):
                                if build_mw:
                                    for m in range(2):
                                        k = ich * 2 + m
                                        e.wait_ge(s_cast.h, cmap[("gu", g, k)])
                                        if ich == 0 and m == 0:
                                            e.wait_ge(s_cast.h, cmap[("mk",)])
                                        bt.acquire(e, ("psf", 4 + m),
                                                   (s_dve3, d3map[("mwev", g, ich, m)]))
                                        for dk in range(NDK):
                                            mm = e.matmul(
                                                psf[4 + m][0:8, 0:128],
                                                mskw[:, g * 128 + dk * 8:
                                                     g * 128 + (dk + 1) * 8],
                                                fwbf[:, ((g * 22 + k) % 3) * 2048 + dk * 128:
                                                    ((g * 22 + k) % 3) * 2048 + (dk + 1) * 128],
                                                start=(dk == 0), stop=(dk == NDK - 1))
                                        s_pe2.inc_ev(mm, pe2m, ("mw", g, ich, m))
                                for m in range(2):
                                    k = ich * 2 + m
                                    e.wait_ge(s_cast.h, cmap[("gu", g, k)])
                                    if k == 0:
                                        e.wait_ge(s_gs[g % 2].h,
                                                  gsb[g % 2] + 16 * (g // 2 + 1))
                                        if not build_mw:
                                            e.wait_ge(s_gml[g % 2].h,
                                                      gmlb[g % 2]
                                                      + 16 * (g // 2 + 1))
                                    if build_mw:
                                        e.wait_ge(s_dve3.h, d3map[("mwev", g, ich, m)])
                                    for tb in range(2):
                                        bank = m * 2 + tb
                                        evac = ((s_act, actm[("sil", g, ich, tb)])
                                                if m == 0 else
                                                (s_dve, dvem[("ap", g, ich, tb)]))
                                        bt.acquire(e, ("psf", bank), evac)
                                        for dk in range(NDK):
                                            e.matmul(
                                                psf[bank][:],
                                                fwbf[:, ((g * 22 + k) % 3) * 2048 + dk * 128:
                                                    ((g * 22 + k) % 3) * 2048 + (dk + 1) * 128],
                                                xln[:, dk * HALF + tb * 512:
                                                    dk * HALF + (tb + 1) * 512],
                                                start=(dk == 0), stop=False)
                                        mm = e.matmul(
                                            psf[bank][:],
                                            mw_rg[0:8, (g % 2) * 2 * NI + m * NI
                                                  + ich * 128:(g % 2) * 2 * NI
                                                  + m * NI + (ich + 1) * 128],
                                            soft_rg[0:8, (g % 2) * HALF + tb * 512:
                                                    (g % 2) * HALF + (tb + 1) * 512],
                                            start=False, stop=True)
                                        s_pe.inc_ev(mm, pem, ("gu", g, ich, m, tb))
                            e.wait_ge(s_dve.h, dvem[("ap", g, NICH - 1, 1)])
                            for dblk in range(4):
                                e.wait_ge(s_cast.h, cmap[("dn", g, dblk * 3 + 2)])
                                for t in range(NT):
                                    gi = g * 32 + dblk * 8 + t
                                    bt.acquire(e, ("psf", 4 + gi % 2),
                                               (s_dve2, d2map[("dnev", g, dblk, t)]))
                                    for ich in range(NICH):
                                        mm = e.matmul(
                                            psf[4 + gi % 2][:],
                                            ap_[:, ich * HALF + t * 128:
                                                ich * HALF + (t + 1) * 128],
                                            wdbf[:, (dblk % 2) * NICH * 512 + ich * 512:
                                                 (dblk % 2) * NICH * 512 + (ich + 1) * 512],
                                            start=(ich == 0), stop=(ich == NICH - 1))
                                    s_pe2.inc_ev(mm, pe2m, ("dn", g, dblk, t))

        def phase_attn(h):
            phase_ln_gate(h, src_hbm=False, gate=False)
            with contextlib.ExitStack() as phA:
                _uA = next(_uid)
                def pbA(name, p, f, dt):
                    return phA.enter_context(
                        nc.sbuf_tensor(f"{name}_{_uA}", [p, f], dt))
                kfm = pbA("kfm", 128, NKV * 1280, BF16)
                vtm = pbA("vtm", 128, 10 * 512, BF16)
                qao = pbA("qao", 128, 8 * HALF, BF16)
                wstage = pbA("wstage", 128, 2 * 2048, F32)
                wbf = pbA("wbf", 128, 2 * 2048, BF16)
                for pass_ in range(2):
                    attn_qkv(h, pass_, kfm, vtm, qao, wstage, wbf)
                    if h == 0 and "qao" in DEBUG_TAPS:
                        qt = tap_dram(f"qao{pass_}", (128, 8 * HALF), BF16)
                        with nc.Block() as b:
                            @b.gpsimd
                            def _(e):
                                s_g.inc(e.dma_start(qt[:], qao[:]), 16)
                                e.wait_ge(s_g.h, s_g.n)
                    attn_oproj(h, pass_, qao, wstage)
                if h == 0 and "kv" in DEBUG_TAPS:
                    kt = tap_dram("kfm", (128, NKV * 1280), BF16)
                    vt = tap_dram("vtm", (128, 10 * 512), BF16)
                    with nc.Block() as b:
                        @b.gpsimd
                        def _(e):
                            s_g.inc(e.dma_start(kt[:], kfm[:]), 16)
                            s_g.inc(e.dma_start(vt[:], vtm[:]), 16)
                            e.wait_ge(s_g.h, s_g.n)
                if h == 0:
                    with nc.Block() as b:
                        @b.gpsimd
                        def _(e):
                            s_gkv.inc(e.dma_start(
                                kv_dram[0][:, 0:1024]
                                .rearrange("p (kv s) -> p kv s", kv=4),
                                kfm[:].rearrange("p (kv s) -> p kv s", kv=4)
                                [:, :, 768:1024]), 16)
                            s_gkv.inc(e.dma_start(kv_dram[1][:, 0:1024],
                                                  vtm[:, 6 * 512:8 * 512]), 16)
                            e.wait_ge(s_gkv.h, s_gkv.n)

        def attn_qkv(h, pass_, kfm, vtm, qao, wstage, wbf):
            with contextlib.ExitStack() as ph:
                _u = next(_uid)
                def pb_(name, p, f, dt):
                    return ph.enter_context(
                        nc.sbuf_tensor(f"{name}_{_u}", [p, f], dt))
                csb = pb_("csb", 128, 2 * HALF, F32)
                mskb = pb_("mskb", 128, 384, BF16)
                qraw = pb_("qraw", 128, 2 * HALF, F32)
                rot = pb_("rot", 128, HALF, F32)
                xfer = pb_("xfer", 128, HALF, BF16)
                sqtm = pb_("sqtm", 128, 2 * HALF, BF16)
                sscol = pb_("sscol", 128, 2 * 8, F32)
                rstdc = pb_("rstdc", 128, 2 * 8, F32)
                cqc = pb_("cqc", 128, 2 * 8, F32)
                exb = pb_("exb", 128, 6 * 384, F32)
                abf = pb_("abf", 128, 6 * 384, BF16)
                atT = pb_("atT", 128, 6 * 384, BF16)
                cden = pb_("cdena", 128, 6, F32)

                if pass_ == 0:
                    units = ([("k", kv) for kv in range(NKV)]
                             + [("v", kv) for kv in range(NKV)]
                             + [("q", hh) for hh in range(8)])
                else:
                    units = [("q", hh) for hh in range(8, 16)]
                NUq = len(units)
                qk_us = [i for i, (tp, _) in enumerate(units) if tp != "v"]
                q_us = [i for i, (tp, _) in enumerate(units) if tp == "q"]

                wab = [s.n for s in s_wa]
                gcb, gkvb = s_gc.n, s_gkv.n
                peb = s_pe.n

                def att_pipeline(emit_head, emit_tail):
                    for p in range(NT // 2):
                        emit_head(p)
                        if p >= 2:
                            emit_tail(p - 2)
                    emit_tail(NT // 2 - 2)
                    emit_tail(NT // 2 - 1)

                def asl(tl):
                    return ((tl // 2) % 3) * 2 + (tl % 2)

                kvu = [i for i, (tp, _) in enumerate(units) if tp != "q"]
                cast_ev, pe2_ev, act_ev, d2_ev = [], [], [], []
                for ui, (tp, idx) in enumerate(units):
                    cast_ev.append(("w", ui))
                if kvu:
                    act_ev.append(("ev", 0))
                pk = None
                for ui in kvu:
                    tp = units[ui][0]
                    if ui + 1 < len(kvu):
                        act_ev.append(("ev", ui + 1))
                    if tp == "k":
                        act_ev += [("bf", ui), ("sqT", ui), ("rstd", ui),
                                   ("knrm", ui), ("kfmev", ui)]
                        d2_ev.append(("qro", ui))
                        if pk is not None:
                            d2_ev.append(("ssm", pk))
                        pk = ui
                        pe2_ev += [("trk1", ui), ("trk2", ui)]
                    else:
                        if pk is not None:
                            d2_ev.append(("ssm", pk))
                            pk = None
                        act_ev.append(("vstg", ui))
                        d2_ev.append(("vtm", ui))
                        pe2_ev.append(("vtr", ui))
                def sched_att_pe(pu):
                    att_pipeline(
                        lambda p: pe2_ev.extend([("sc", pu, 2 * p),
                                                 ("sc", pu, 2 * p + 1)]),
                        lambda p: pe2_ev.extend([("atrp", pu, p),
                                                 ("pv", pu, 2 * p),
                                                 ("pv", pu, 2 * p + 1)]))

                def sched_att_act(pu):
                    att_pipeline(
                        lambda p: act_ev.extend([("exp", pu, 2 * p),
                                                 ("exp", pu, 2 * p + 1)]),
                        lambda p: act_ev.extend([("atT", pu, p),
                                                 ("out", pu, 2 * p),
                                                 ("out", pu, 2 * p + 1)]))

                def d2_tail(pu, p):
                    return [("rec", pu, p), ("abf", pu, 2 * p),
                            ("abf", pu, 2 * p + 1)]

                for qi, ui in enumerate(q_us):
                    act_ev.append(("ev", ui))
                    if qi >= 1:
                        pu = q_us[qi - 1]
                        sched_att_pe(pu)
                        sched_att_act(pu)
                        d2_ev += (d2_tail(pu, 0)
                                  + [("qro", ui)] + d2_tail(pu, 1)
                                  + d2_tail(pu, 2) + d2_tail(pu, 3))
                    else:
                        d2_ev += [("qro", ui)]
                    pe2_ev.append(("trq", ui))
                    act_ev += [("bf", ui), ("sqT", ui), ("rstd", ui)]
                    d2_ev += [("ssm", ui), ("cq", ui)]
                pu = q_us[-1]
                sched_att_pe(pu)
                sched_att_act(pu)
                for p in range(4):
                    d2_ev += d2_tail(pu, p)

                cmap = s_cast.evmap(cast_ev)
                pe2m = s_pe2.evmap(pe2_ev)
                actm = s_act.evmap(act_ev)
                d2map = s_dve2.evmap(d2_ev)

                def kwin(gt):
                    c0 = max(0, gt - 2)
                    return c0, gt - c0 + 1

                with nc.Block() as b:
                    @b.gpsimd
                    def _(e):
                        s_gc.inc(e.dma_start(csb[:, 0:HALF],
                                             cs_in[:, h * HALF:(h + 1) * HALF]), 16)
                        s_gc.inc(e.dma_start(
                            csb[:, HALF:],
                            cs_in[:, SEQ + h * HALF:SEQ + (h + 1) * HALF]), 16)
                        s_gc.inc(e.dma_start(mskb[:], mskh_in[:]), 16)
                        if pass_ == 0 and h == 1:
                            s_gkv.inc(e.dma_start(
                                kfm[:].rearrange("p (kv s) -> p kv s", kv=4)
                                [:, :, 0:256],
                                kv_dram[0][:, 0:1024]
                                .rearrange("p (kv s) -> p kv s", kv=4)), 16)
                            s_gkv.inc(e.dma_start(vtm[:, 0:1024],
                                                  kv_dram[1][:, 0:1024]), 16)

                    @b.sync
                    def _(e):
                        for ui, (tp, idx) in enumerate(units):
                            if ui >= 2:
                                e.wait_ge(s_cast.h, cmap[("w", ui - 2)])
                            if tp == "k":
                                src = wkv_in[idx]
                            elif tp == "v":
                                src = wkv_in[NKV + idx]
                            else:
                                src = wq_in[idx]
                            s_wa[ui % 2].inc(e.dma_start(
                                wstage[:, (ui % 2) * 2048:(ui % 2 + 1) * 2048]
                                .rearrange("p (dk a) -> p dk a", dk=NDK), src), 16)

                    @b.scalar
                    def _(e):
                        wt = Waiter()
                        def a_cast(ui):
                            wt.w(e, s_wa[ui % 2],
                                      wab[ui % 2] + 16 * (ui // 2 + 1))
                            if ui >= 2:
                                wt.w(e, s_pe, peb + 2 * (ui - 2) + 2)
                            s_cast.inc_ev(e.activation(
                                wbf[:, (ui % 2) * 2048:(ui % 2 + 1) * 2048],
                                wstage[:, (ui % 2) * 2048:(ui % 2 + 1) * 2048],
                                AF.Copy), cmap, ("w", ui))

                        def a_evac(ui):
                            prev = ui - 2
                            if prev >= 0 and units[prev][0] != "v":
                                wt.w(e, s_dve2, d2map[("qro", prev)])
                            for tb in range(2):
                                wt.w(e, s_pe, peb + 2 * ui + tb + 1)
                                mm = e.activation(
                                    qraw[:, (ui % 2) * HALF + tb * 512:
                                         (ui % 2) * HALF + (tb + 1) * 512],
                                    psf[(ui % 2) * 2 + tb][:], AF.Copy)
                            s_act.inc_ev(mm, actm, ("ev", ui))

                        def a_bf(ui, tp, idx):
                            sca(e)
                            wt.w(e, s_dve2, d2map[("qro", ui)])
                            if tp == "k" and ui >= 1:
                                wt.w(e, s_pe2, pe2m[("trk2", ui - 1)])
                            if tp == "q":
                                hl = idx - 8 * pass_
                                s_act.inc_ev(e.activation(
                                    qao[:, hl * HALF:(hl + 1) * HALF],
                                    qraw[:, (ui % 2) * HALF:
                                         (ui % 2 + 1) * HALF],
                                    AF.Copy, scale=sgnb[:, idx:idx + 1]),
                                    actm, ("bf", ui))
                            else:
                                s_act.inc_ev(e.activation(
                                    xfer[:],
                                    qraw[:, (ui % 2) * HALF:
                                         (ui % 2 + 1) * HALF],
                                    AF.Copy), actm, ("bf", ui))

                        def a_knrm(ui):
                            sca(e)
                            wt.w(e, s_pe2, pe2m[("trk1", ui)])
                            for t in range(NT):
                                mm = e.activation(
                                    xfer[:, t * 128:(t + 1) * 128],
                                    psb[0][:, t * 128:(t + 1) * 128],
                                    AF.Copy, scale=rstdc[:, t:t + 1])
                            s_act.inc_ev(mm, actm, ("knrm", ui))

                        def a_kfm(ui, idx):
                            wt.w(e, s_pe2, pe2m[("trk2", ui)])
                            s_act.inc_ev(e.activation(
                                kfm[:, idx * 1280 + 256 * h:
                                    idx * 1280 + 256 * h + HALF],
                                psb[1][:], AF.Copy), actm, ("kfmev", ui))

                        def a_vstg(ui, idx):
                            sca(e)
                            if idx > 0:
                                wt.w(e, s_pe2, pe2m[("vtr", ui - 1)])
                            s_act.inc_ev(e.activation(
                                xfer[:],
                                qraw[:, (ui % 2) * HALF:(ui % 2 + 1) * HALF],
                                AF.Copy), actm, ("vstg", ui))

                        def a_sq(ui, tp, qi):
                            wt.w(e, s_pe2, pe2m[("trq" if tp == "q" else "trk1",
                                                     ui)])
                            if tp == "q" and qi >= 2:
                                wt.w(e, s_dve2, d2map[("ssm", q_us[qi - 2])])
                            sl = (qi % 2) if tp == "q" else 0
                            s_act.inc_ev(e.activation(
                                sqtm[:, sl * HALF:(sl + 1) * HALF],
                                psb[0][:], AF.Square), actm, ("sqT", ui))

                        def a_rstd(ui, tp, qi):
                            sl = (qi % 2) if tp == "q" else 0
                            wt.w(e, s_dve2, d2map[("ssm", ui)])
                            e.activation(rstdc[:, sl * 8:(sl + 1) * 8],
                                         sscol[:, sl * 8:(sl + 1) * 8], AF.Ln)
                            s_act.inc_ev(e.activation(
                                rstdc[:, sl * 8:(sl + 1) * 8],
                                rstdc[:, sl * 8:(sl + 1) * 8],
                                AF.Exp, scale=-0.5), actm, ("rstd", ui))

                        def a_att(ui, idx):
                            hl = idx - 8 * pass_
                            qi = q_us.index(ui)
                            sl = qi % 2

                            def a_head(p):
                                wt.w(e, s_pe2, pe2m[("sc", ui, 2 * p + 1)])
                                for tl in (2 * p, 2 * p + 1):
                                    wq_ = (min(h * 8 + tl, 2) + 1) * 128
                                    s_act.inc_ev(e.activation(
                                        exb[:, asl(tl) * 384:
                                            asl(tl) * 384 + wq_],
                                        psf[(p % 2) * 2 + tl % 2][:, 0:wq_],
                                        AF.Exp,
                                        scale=cqc[:, sl * 8 + tl:sl * 8 + tl + 1],
                                        bias=tempnb[:, idx:idx + 1],
                                        accum_out=cden[:, asl(tl):
                                                       asl(tl) + 1]),
                                        actm, ("exp", ui, tl))

                            def a_tail(p):
                                wt.w(e, s_dve2, d2map[("abf", ui, 2 * p + 1)])
                                wt.w(e, s_pe2, pe2m[("atrp", ui, p)])
                                s_act.inc_ev(e.activation(
                                    atT[:, (p % 3) * 768:(p % 3) * 768 + 768],
                                    psb[p % 2][:, 0:768], AF.Copy),
                                    actm, ("atT", ui, p))
                                wt.w(e, s_pe2, pe2m[("pv", ui, 2 * p + 1)])
                                for tl in (2 * p, 2 * p + 1):
                                    s_act.inc_ev(e.activation(
                                        qao[:, hl * HALF + tl * 128:
                                            hl * HALF + (tl + 1) * 128],
                                        psf[4 + tl % 2][:, 0:128], AF.Copy),
                                        actm, ("out", ui, tl))

                            att_pipeline(a_head, a_tail)

                        if kvu:
                            a_cast(kvu[0])
                            a_evac(kvu[0])
                        for ui in kvu:
                            tp, idx = units[ui]
                            if ui + 1 < len(kvu):
                                a_cast(ui + 1)
                                a_evac(ui + 1)
                            if tp == "k":
                                a_bf(ui, tp, idx)
                                a_sq(ui, tp, 0)
                                a_rstd(ui, tp, 0)
                                a_knrm(ui)
                                a_kfm(ui, idx)
                            else:
                                a_vstg(ui, idx)
                        a_cast(q_us[0])
                        for qi, ui in enumerate(q_us):
                            tp, idx = units[ui]
                            if qi + 1 < len(q_us):
                                a_cast(q_us[qi + 1])
                            a_evac(ui)
                            if qi >= 1:
                                a_att(q_us[qi - 1], units[q_us[qi - 1]][1])
                            a_bf(ui, tp, idx)
                            a_sq(ui, tp, qi)
                            a_rstd(ui, tp, qi)
                        a_att(q_us[-1], units[q_us[-1]][1])

                    @b.tensor
                    def _(e):
                        wt = Waiter()
                        bt = Banks(wt)

                        def p_proj(ui, tp):
                            wt.w(e, s_cast, cmap[("w", ui)])
                            if ui == 0:
                                wt.w(e, s_gc, s_gc.n)
                                wt.w(e, s_gkv, s_gkv.n)
                            for tb in range(2):
                                bank = (ui % 2) * 2 + tb
                                bt.acquire(e, ("psf", bank),
                                           (s_act, actm[("ev", ui)]))
                                for dk in range(NDK):
                                    mm = e.matmul(
                                        psf[bank][:],
                                        wbf[:, (ui % 2) * 2048 + dk * 128:
                                            (ui % 2) * 2048 + (dk + 1) * 128],
                                        xln[:, dk * HALF + tb * 512:
                                            dk * HALF + (tb + 1) * 512],
                                        start=(dk == 0), stop=(dk == NDK - 1))
                                s_pe.inc(mm)

                        def p_tr(ui, tp, idx, qi):
                            wt.w(e, s_act, actm[("bf", ui)])
                            src_fm = (qao[:, (idx - 8 * pass_) * HALF:
                                          (idx - 8 * pass_ + 1) * HALF]
                                      if tp == "q" else xfer)
                            evac1 = (s_act, actm[("sqT" if tp == "q"
                                                  else "knrm", ui)])
                            bt.acquire(e, ("psb", 0), evac1)
                            for t in range(NT):
                                mm = e.matmul(psb[0][:, t * 128:(t + 1) * 128],
                                              src_fm[:, t * 128:(t + 1) * 128],
                                              idb[:], start=(t == 0),
                                              stop=(t == NT - 1), is_transpose=True)
                            s_pe2.inc_ev(mm, pe2m,
                                         ("trq" if tp == "q" else "trk1", ui))

                        def p_att(ui, idx):
                            kv = idx // 4
                            hl = idx - 8 * pass_
                            wt.w(e, s_dve2, d2map[("cq", ui)])
                            kb = 768 * h

                            def pe_head(p):
                                for tl in (2 * p, 2 * p + 1):
                                    gt = h * 8 + tl
                                    c0, nch = kwin(gt)
                                    w = nch * 128
                                    moff = (3 - nch) * 128
                                    bank = (p % 2) * 2 + tl % 2
                                    bt.acquire(e, ("psf", bank),
                                               (s_act, actm[("exp", ui, tl)]))
                                    e.matmul(psf[bank][:, 0:w], idb[:],
                                             mskb[:, moff:moff + w],
                                             start=True, stop=False)
                                    s_pe2.inc_ev(e.matmul(
                                        psf[bank][:, 0:w],
                                        qao[:, hl * HALF + tl * 128:
                                            hl * HALF + (tl + 1) * 128],
                                        kfm[:, kv * 1280 + c0 * 128 - kb:
                                            kv * 1280 + c0 * 128 - kb + w],
                                        start=False, stop=True),
                                        pe2m, ("sc", ui, tl))

                            def pe_tail(p):
                                wt.w(e, s_dve2, d2map[("abf", ui, 2 * p + 1)])
                                bt.acquire(e, ("psb", p % 2),
                                           (s_act, actm[("atT", ui, p)]))
                                mms = [(tl, c)
                                       for tl in (2 * p, 2 * p + 1)
                                       for c in range(kwin(h * 8 + tl)[1])]
                                for i, (tl, c) in enumerate(mms):
                                    mm = e.matmul(
                                        psb[p % 2][0:128,
                                                   (tl % 2) * 384 + c * 128:
                                                   (tl % 2) * 384 + (c + 1) * 128],
                                        abf[:, asl(tl) * 384 + c * 128:
                                            asl(tl) * 384 + (c + 1) * 128],
                                        idb[:], start=(i == 0),
                                        stop=(i == len(mms) - 1),
                                        is_transpose=True)
                                s_pe2.inc_ev(mm, pe2m, ("atrp", ui, p))
                                wt.w(e, s_act, actm[("atT", ui, p)])
                                for tl in (2 * p, 2 * p + 1):
                                    c0, nch = kwin(h * 8 + tl)
                                    bt.acquire(e, ("psf", 4 + tl % 2),
                                               (s_act, actm[("out", ui, tl)]))
                                    for c in range(nch):
                                        mm = e.matmul(
                                            psf[4 + tl % 2][:, 0:128],
                                            vtm[:, (c0 + c - 6 * h) * 512
                                                + kv * 128:
                                                (c0 + c - 6 * h) * 512
                                                + (kv + 1) * 128],
                                            atT[:, (p % 3) * 768
                                                + (tl % 2) * 384 + c * 128:
                                                (p % 3) * 768
                                                + (tl % 2) * 384 + (c + 1) * 128],
                                            start=(c == 0), stop=(c == nch - 1))
                                    s_pe2.inc_ev(mm, pe2m, ("pv", ui, tl))

                            att_pipeline(pe_head, pe_tail)

                        def p_trs(ui, tp, idx):
                            if tp == "v":
                                wt.w(e, s_act, actm[("vstg", ui)])
                                bt.acquire(e, ("psb", 0),
                                           (s_dve2, d2map[("vtm", ui)]))
                                for t in range(NT):
                                    mm = e.matmul(psb[0][:, t * 128:(t + 1) * 128],
                                                  xfer[:, t * 128:(t + 1) * 128],
                                                  idb[:], start=(t == 0),
                                                  stop=(t == NT - 1),
                                                  is_transpose=True)
                                s_pe2.inc_ev(mm, pe2m, ("vtr", ui))
                            else:
                                p_tr(ui, tp, idx, 0)
                                wt.w(e, s_act, actm[("knrm", ui)])
                                bt.acquire(e, ("psb", 1),
                                           (s_act, actm[("kfmev", ui)]))
                                for t in range(NT):
                                    mm = e.matmul(psb[1][:, t * 128:(t + 1) * 128],
                                                  xfer[:, t * 128:(t + 1) * 128],
                                                  idb[:], start=(t == 0),
                                                  stop=(t == NT - 1),
                                                  is_transpose=True)
                                s_pe2.inc_ev(mm, pe2m, ("trk2", ui))

                        for ui in kvu[:2]:
                            p_proj(ui, units[ui][0])
                        for j, ui in enumerate(kvu):
                            p_trs(ui, *units[ui])
                            if j + 2 < len(kvu):
                                nu = kvu[j + 2]
                                p_proj(nu, units[nu][0])
                        for qi, ui in enumerate(q_us):
                            tp, idx = units[ui]
                            p_proj(ui, tp)
                            if qi >= 1:
                                p_att(q_us[qi - 1], units[q_us[qi - 1]][1])
                            p_tr(ui, tp, idx, qi)
                        p_att(q_us[-1], units[q_us[-1]][1])

                    @b.vector
                    def _(e):
                        wt = Waiter()

                        def v_ropeq(ui, idx):
                            q0 = (ui % 2) * HALF

                            def c1():
                                wt.w(e, s_act, actm[("ev", ui)])
                                if ui == 0:
                                    wt.w(e, s_gc, s_gc.n)
                                e.tensor_copy(rot[0:64, :],
                                              qraw[64:128, q0:q0 + HALF])
                                e.tensor_copy(rot[64:128, :],
                                              qraw[0:64, q0:q0 + HALF])

                            def c2():
                                vs(e)
                                e.tensor_mul(qraw[:, q0:q0 + HALF],
                                             qraw[:, q0:q0 + HALF],
                                             csb[:, 0:HALF])
                                e.tensor_mul(rot[:], rot[:], csb[:, HALF:])
                                vs(e)
                                s_dve2.inc_ev(
                                    e.tensor_add(qraw[:, q0:q0 + HALF],
                                                 qraw[:, q0:q0 + HALF],
                                                 rot[:]),
                                    d2map, ("qro", ui))

                            return [c1, c2]

                        def v_rope(ui, idx):
                            for f in v_ropeq(ui, idx):
                                f()

                        def v_vtm(ui, idx):
                            wt.w(e, s_pe2, pe2m[("vtr", ui)])
                            dst = vtm[:].rearrange("p (t c) -> p t c", c=512)[
                                :, h * 2:h * 2 + 8, idx * 128:(idx + 1) * 128]
                            s_dve2.inc_ev(
                                e.tensor_copy(dst, psb[0][:]
                                              .rearrange("p (t c) -> p t c",
                                                         c=128)),
                                d2map, ("vtm", ui))

                        def v_norm(ui, tp, qi):
                            sl = (qi % 2) if tp == "q" else 0
                            wt.w(e, s_act, actm[("sqT", ui)])
                            e.reduce_sum(
                                sscol[:, sl * 8:(sl + 1) * 8]
                                .rearrange("p (t o) -> p t o", t=8),
                                sqtm[:, sl * HALF:(sl + 1) * HALF]
                                .rearrange("p (t c) -> p t c", t=8),
                                axis=AX.X)
                            vs(e)
                            s_dve2.inc_ev(e.tensor_scalar(
                                sscol[:, sl * 8:(sl + 1) * 8],
                                sscol[:, sl * 8:(sl + 1) * 8],
                                1e-24, None, ALU.max), d2map, ("ssm", ui))

                        def v_cq(ui, idx, qi):
                            sl = qi % 2
                            wt.w(e, s_act, actm[("rstd", ui)])
                            s_dve2.inc_ev(e.tensor_scalar(
                                cqc[:, sl * 8:(sl + 1) * 8],
                                rstdc[:, sl * 8:(sl + 1) * 8],
                                tempab[:, idx:idx + 1], None, ALU.mult),
                                d2map, ("cq", ui))

                        def v_tails(ui, idx):
                            def mk(p):
                                def v_tail():
                                    t0, t1 = 2 * p, 2 * p + 1
                                    c = asl(t0)
                                    wt.w(e, s_act, actm[("exp", ui, t1)])
                                    e.reciprocal(cden[:, c:c + 1],
                                                 cden[:, c:c + 1])
                                    s_dve2.inc_ev(e.reciprocal(
                                        cden[:, c + 1:c + 2],
                                        cden[:, c + 1:c + 2]),
                                        d2map, ("rec", ui, p))
                                    for tl in (t0, t1):
                                        wq_ = (min(h * 8 + tl, 2) + 1) * 128
                                        s_dve2.inc_ev(e.tensor_scalar(
                                            abf[:, asl(tl) * 384:
                                                asl(tl) * 384 + wq_],
                                            exb[:, asl(tl) * 384:
                                                asl(tl) * 384 + wq_],
                                            cden[:, asl(tl):asl(tl) + 1],
                                            None, ALU.mult),
                                            d2map, ("abf", ui, tl))
                                return v_tail
                            return [mk(p) for p in range(4)]

                        prev_k = None
                        for ui in kvu:
                            tp, idx = units[ui]
                            if tp == "k":
                                v_rope(ui, idx)
                                if prev_k is not None:
                                    v_norm(prev_k, "k", 0)
                                prev_k = ui
                            else:
                                if prev_k is not None:
                                    v_norm(prev_k, "k", 0)
                                    prev_k = None
                                v_vtm(ui, idx)
                        for qi, ui in enumerate(q_us):
                            tp, idx = units[ui]
                            C = v_ropeq(ui, idx)
                            if qi >= 1:
                                pu = q_us[qi - 1]
                                T = v_tails(pu, units[pu][1])
                                seq = [C[0], T[0], C[1], T[1], T[2], T[3]]
                            else:
                                seq = C
                            for f in seq:
                                f()
                            v_norm(ui, tp, qi)
                            v_cq(ui, idx, qi)
                        for f in v_tails(q_us[-1], units[q_us[-1]][1]):
                            f()

        def attn_oproj(h, pass_, qao, wstage):
            with contextlib.ExitStack() as ph:
                wobf = ph.enter_context(
                    nc.sbuf_tensor(f"wobf_{next(_uid)}", [128, 2 * 2048], BF16))
                wab = [s.n for s in s_wa]
                qs = [2 * pass_, 2 * pass_ + 1]
                cast_ev = [("q", dblk, j) for dblk in range(4) for j in range(2)]
                pe2_ev = [("o", dblk, t) for dblk in range(4) for t in range(NT)]
                d2_ev = [("oev", dblk, t) for dblk in range(4) for t in range(NT)]
                cmap = s_cast.evmap(cast_ev)
                pe2m = s_pe2.evmap(pe2_ev)
                d2map = s_dve2.evmap(d2_ev)

                with nc.Block() as b:
                    @b.sync
                    def _(e):
                        for dblk in range(4):
                            for j in range(2):
                                ui = dblk * 2 + j
                                if ui >= 2:
                                    pd, pj = divmod(ui - 2, 2)
                                    e.wait_ge(s_cast.h, cmap[("q", pd, pj)])
                                s_wa[ui % 2].inc(e.dma_start(
                                    wstage[:, (ui % 2) * 2048:(ui % 2 + 1) * 2048]
                                    .rearrange("p (a c) -> p a c", a=4),
                                    wo_in[dblk, qs[j]]), 16)

                    @b.scalar
                    def _(e):
                        for dblk in range(4):
                            for j in range(2):
                                ui = dblk * 2 + j
                                e.wait_ge(s_wa[ui % 2].h,
                                          wab[ui % 2] + 16 * (ui // 2 + 1))
                                if dblk >= 1:
                                    e.wait_ge(s_pe2.h,
                                              pe2m[("o", dblk - 1, NT - 1)])
                                s_cast.inc_ev(e.activation(
                                    wobf[:, j * 2048:(j + 1) * 2048],
                                    wstage[:, (ui % 2) * 2048:(ui % 2 + 1) * 2048],
                                    AF.Copy), cmap, ("q", dblk, j))

                    @b.tensor
                    def _(e):
                        bt = Banks()
                        for dblk in range(4):
                            for t in range(NT):
                                gi = dblk * 8 + t
                                bt.acquire(e, ("psf", gi % 6),
                                           (s_dve2, d2map[("oev", dblk, t)]))
                                for j in range(2):
                                    if t == 0:
                                        e.wait_ge(s_cast.h, cmap[("q", dblk, j)])
                                    for a in range(4):
                                        hl = j * 4 + a
                                        mm = e.matmul(
                                            psf[gi % 6][:],
                                            qao[:, hl * HALF + t * 128:
                                                hl * HALF + (t + 1) * 128],
                                            wobf[:, j * 2048 + a * 512:
                                                 j * 2048 + (a + 1) * 512],
                                            start=(j == 0 and a == 0),
                                            stop=(j == 1 and a == 3))
                                s_pe2.inc_ev(mm, pe2m, ("o", dblk, t))

                    @b.vector
                    def _(e):
                        for dblk in range(4):
                            for t in range(NT):
                                gi = dblk * 8 + t
                                e.wait_ge(s_pe2.h, pe2m[("o", dblk, t)])
                                s_dve2.inc_ev(e.tensor_add(
                                    r_sb[:, t * D + dblk * 512:t * D + (dblk + 1) * 512],
                                    psf[gi % 6][:],
                                    r_sb[:, t * D + dblk * 512:t * D + (dblk + 1) * 512]),
                                    d2map, ("oev", dblk, t))

        def phase_store(h):
            with nc.Block() as b:
                @b.gpsimd
                def _(e):
                    for t in range(NT):
                        s_st.inc(e.dma_start(
                            out[h * HALF + t * 128:h * HALF + (t + 1) * 128, :],
                            r_sb[:, t * D:(t + 1) * D]), 16)

        def tap_phase(label, h):
            if h != 0 or label not in DEBUG_TAPS:
                return
            rt = tap_dram(label + "_r", (128, NT * D))
            xt = tap_dram(label + "_xln", (128, NDK * HALF), BF16)
            sct = tap_dram(label + "_scal", (128, 64))
            with nc.Block() as b:
                @b.gpsimd
                def _(e):
                    n0 = s_g.n
                    s_g.inc(e.dma_start(rt[:], r_sb[:]), 16)
                    s_g.inc(e.dma_start(xt[:], xln[:]), 16)
                    s_g.inc(e.dma_start(sct[:], scalbuf[:]), 16)
                    e.wait_ge(s_g.h, s_g.n)

        for h in range(2):
            phase_ln_gate(h, src_hbm=True, gate=True)
            tap_phase("ln1", h)
            phase_ffn(h, build_mw=(h == 0))
            tap_phase("ffn1", h)
            phase_attn(h)
            tap_phase("attn", h)
            phase_ln_gate(h, src_hbm=False, gate=True)
            tap_phase("ln3", h)
            phase_ffn(h, build_mw=False)
            tap_phase("ffn2", h)
            phase_store(h)

        with nc.Block() as b:
            @b.gpsimd
            def _(e):
                e.wait_ge(s_st.h, s_st.n)

    return nc


def _stage_weights(i):
    f = np.float32
    gpw = np.asarray(i["L0_ffn_gate_proj_w"], f)
    upw = np.asarray(i["L0_ffn_up_proj_w"], f)
    dpw = np.asarray(i["L0_ffn_down_proj_w"], f)
    Wq = np.asarray(i["L0_attn_Wq"], f)
    Wkv = np.asarray(i["L0_attn_Wkv"], f)
    Wo = np.asarray(i["L0_attn_Wo"], f)
    gw = np.asarray(i["L0_ffn_gate_w"], f)
    mkw = np.asarray(i["L0_ffn_mask_up_w"], f)

    def gu_tiles(w):
        return np.ascontiguousarray(
            w.reshape(NG, NICH, 128, NDK, 128).transpose(0, 1, 4, 3, 2))

    return dict(
        wg=gu_tiles(gpw),
        wu=gu_tiles(upw),
        wd=np.ascontiguousarray(
            dpw.reshape(NG, 4, 512, NICH, 128).transpose(0, 1, 3, 4, 2)),
        wq=np.ascontiguousarray(
            Wq.reshape(NH, 128, NDK, 128).transpose(0, 3, 2, 1)),
        wkv=np.ascontiguousarray(
            Wkv.reshape(8, 128, NDK, 128).transpose(0, 3, 2, 1)),
        wo=np.ascontiguousarray(
            Wo.reshape(4, 512, 4, 4, 128).transpose(0, 2, 4, 3, 1)),
        gw=np.ascontiguousarray(
            gw.T.reshape(NDK, 128, 64).transpose(1, 0, 2)).reshape(128, NDK * 64),
        mkw=np.ascontiguousarray(
            mkw.reshape(NG, NDK, 128, 8).transpose(2, 0, 1, 3)).reshape(128, NG * NDK * 8),
    )


def _constants():
    f = np.float32
    pos = np.arange(SEQ, dtype=np.float64)
    invfreq = 1.0 / (10000.0 ** (np.arange(0, DH, 2, dtype=np.float64) / DH))
    ang = np.concatenate([invfreq, invfreq])[:, None] * pos[None, :]
    cos = np.cos(ang).astype(f)
    sin = np.sin(ang).astype(f)
    sin[:64] = -sin[:64]
    cs = np.concatenate([cos, sin], axis=1)

    ii = np.arange(128)[:, None]
    jj = np.arange(384)[None, :]
    mI = np.where((jj > ii) & (jj <= ii + 256), 0.0, NEG).astype(f)

    return dict(
        cs=cs,
        mskh=mI.astype(bfdt),
        identb=np.eye(128).astype(bfdt),
        identf=np.eye(128, dtype=f),
        iota64=np.arange(64, dtype=f).reshape(64, 1),
        onesrow=np.ones((1, 128), f),
        onescol=np.ones((128, 1), bfdt),
    )


_NC_CACHE = {}


def kernel(**inputs):
    x = np.asarray(inputs["L0_attn_x"], np.float32)
    shared = _stage_weights(inputs)
    shared.update(_constants())
    shared["qkt"] = np.ascontiguousarray(np.broadcast_to(
        np.asarray(inputs["L0_attn_qk_temp"], np.float32).reshape(1, NH),
        (128, NH)))
    shared["inv"] = np.ascontiguousarray(np.broadcast_to(
        np.asarray(inputs["L0_ffn_inv_map"]).astype(np.int32).reshape(1, 64),
        (64, 64)))

    if "nc" not in _NC_CACHE:
        _NC_CACHE["nc"] = build_program()
    nc = _NC_CACHE["nc"]

    in_maps = []
    for c in range(8):
        m = dict(shared)
        m["x"] = np.ascontiguousarray(x[c])
        in_maps.append(m)
    res = run_bass_kernel_spmd(nc, in_maps, core_ids=list(range(8)))
    return np.stack([np.asarray(res.results[c]["out"]) for c in range(8)], axis=0)


# version 18
# speedup vs baseline: 1.0621x; 1.0488x over previous
import contextlib
import numpy as np
import ml_dtypes

import concourse.bass as bass
import concourse.mybir as mybir
from concourse.bass_utils import run_bass_kernel_spmd

F32 = mybir.dt.float32
BF16 = mybir.dt.bfloat16
I32 = mybir.dt.int32
AF = mybir.ActivationFunctionType
ALU = mybir.AluOpType
AX = mybir.AxisListType
bfdt = ml_dtypes.bfloat16

SEQ, D, HALF, NT = 2048, 2048, 1024, 8
NDK = 16
NG, NI, NICH = 8, 1408, 11
NH, NKV, DH = 16, 4, 128
NEG = -1.0e30
DEBUG_TAPS = ()


class S:
    def __init__(self, stack, nc, name):
        self.h = stack.enter_context(nc.semaphore(name))
        self.n = 0

    def inc(self, instr, k=1):
        instr.then_inc(self.h, k)
        self.n += k
        return self.n

    def evmap(self, schedule, step=1):
        m = {}
        n = self.n
        for key in schedule:
            n += step
            m[key] = n
        return m

    def inc_ev(self, instr, m, key, step=1):
        assert self.n + step == m[key], (key, self.n, m[key])
        return self.inc(instr, step)


class Waiter:

    def __init__(self):
        self.seen = {}

    def w(self, e, sem, cnt):
        if self.seen.get(id(sem), -1) >= cnt:
            return
        e.wait_ge(sem.h, cnt)
        self.seen[id(sem)] = cnt


class Banks:

    def __init__(self, wt=None):
        self.last = {}
        self.wt = wt

    def acquire(self, e, bank_id, evac):
        prev = self.last.get(bank_id)
        if prev is not None:
            sem, cnt = prev
            if self.wt is not None:
                self.wt.w(e, sem, cnt)
            else:
                e.wait_ge(sem.h, cnt)
        self.last[bank_id] = evac


def build_program():
    nc = bass.Bass("TRN2", target_bir_lowering=False)

    def din(name, shape, dt=F32):
        return nc.dram_tensor(name, list(shape), dt, kind="ExternalInput").ap()

    x_in = din("x", (SEQ, D))
    wg_in = din("wg", (NG, NICH, 128, NDK, 128))
    wu_in = din("wu", (NG, NICH, 128, NDK, 128))
    wd_in = din("wd", (NG, 4, NICH, 128, 512))
    wq_in = din("wq", (NH, 128, NDK, 128))
    wkv_in = din("wkv", (NKV * 2, 128, NDK, 128))
    wo_in = din("wo", (4, 4, 128, 4, 512))
    gw_in = din("gw", (128, NDK * 64))
    mkw_in = din("mkw", (128, NG * NDK * 8))
    qkt_in = din("qkt", (128, NH))
    inv_in = din("inv", (64, 64), I32)
    cs_in = din("cs", (128, 2 * SEQ))
    mskh_in = din("mskh", (128, 384), BF16)
    idb_in = din("identb", (128, 128), BF16)
    idf_in = din("identf", (128, 128))
    iota_in = din("iota64", (64, 1))
    onr_in = din("onesrow", (1, 128))
    onc_in = din("onescol", (128, 1), BF16)

    out = nc.dram_tensor("out", [SEQ, D], F32, kind="ExternalOutput").ap()

    soft_dram = nc.dram_tensor("soft_scr", [NG, 8, SEQ], BF16, kind="Internal").ap()
    mw_dram = nc.dram_tensor("mw_scr", [2, NG, 8, NI], BF16, kind="Internal").ap()
    kv_dram = nc.dram_tensor("kv_scr", [2, 128, 4096], BF16, kind="Internal").ap()

    import itertools
    _uid = itertools.count()

    tap_outs = {}

    def tap_dram(label, shape, dt=F32):
        t = nc.dram_tensor("tap_" + label, list(shape), dt, kind="ExternalOutput")
        tap_outs[label] = t.ap()
        return tap_outs[label]

    st = contextlib.ExitStack()
    with st:
        def sb(name, p, f, dt):
            return st.enter_context(nc.sbuf_tensor(name, [p, f], dt))

        r_sb = sb("r_sb", 128, NT * D, F32)
        xln = sb("xln", 128, NDK * HALF, BF16)
        scalbuf = sb("scalbuf", 128, 64, F32)
        tempb = sb("tempb", 128, 16, F32)
        tempnb = sb("tempnb", 128, 16, F32)
        tempab = sb("tempab", 128, 16, F32)
        sgnb = sb("sgnb", 128, 16, F32)
        Pm = sb("Pm", 64, 64, F32)
        idb = sb("idb", 128, 128, BF16)
        idf = sb("idf", 128, 128, F32)
        iota = sb("iota", 64, 1, F32)
        onr = sb("onr", 1, 128, F32)
        onc = sb("onc", 128, 1, BF16)
        invf = sb("invf", 64, 64, F32)
        invi = sb("invi", 64, 64, I32)
        junkc = sb("junkc", 128, 1, F32)
        epsc = sb("epsc", 128, 1, F32)

        psf = [st.enter_context(nc.psum_tensor(f"psf{i}", [128, 512], F32))
               for i in range(6)]
        psb = [st.enter_context(nc.psum_tensor(f"psb{i}", [128, 1024], BF16))
               for i in range(2)]

        s_x = S(st, nc, "s_x")
        s_w = S(st, nc, "s_w")
        s_g = S(st, nc, "s_g")
        s_st = S(st, nc, "s_st")
        s_pe = S(st, nc, "s_pe")
        s_pe2 = S(st, nc, "s_pe2")
        s_cast = S(st, nc, "s_cast")
        s_act = S(st, nc, "s_act")
        s_dve = S(st, nc, "s_dve")
        s_dve2 = S(st, nc, "s_dve2")
        s_dve3 = S(st, nc, "s_dve3")
        s_sv = S(st, nc, "s_sv")
        s_pl = S(st, nc, "s_pl")
        s_x2 = [S(st, nc, f"s_x2_{i}") for i in range(2)]
        s_wf = [S(st, nc, f"s_wf_{i}") for i in range(3)]
        s_wa = [S(st, nc, f"s_wa_{i}") for i in range(2)]
        s_gs = [S(st, nc, f"s_gs_{i}") for i in range(2)]
        s_gml = [S(st, nc, f"s_gml_{i}") for i in range(2)]
        s_gms = [S(st, nc, f"s_gms_{i}") for i in range(2)]
        s_ss = [S(st, nc, f"s_ss_{i}") for i in range(2)]
        s_gc = S(st, nc, "s_gc")
        s_gkv = S(st, nc, "s_gkv")
        s_gw = S(st, nc, "s_gw")

        def vs(e):
            n = s_sv.inc(e.engine_nop())
            e.wait_ge(s_sv.h, n)

        with nc.Block() as b:
            @b.gpsimd
            def _(e):
                for dst, src in [(idb, idb_in), (idf, idf_in), (iota, iota_in),
                                 (onr, onr_in), (onc, onc_in), (tempb, qkt_in),
                                 (invi, inv_in)]:
                    s_g.inc(e.dma_start(dst[:], src[:]), 16)
                e.wait_ge(s_g.h, s_g.n)

        with nc.Block() as b:
            @b.vector
            def _(e):
                e.tensor_scalar(tempnb[:], tempb[:], -1.0, None, ALU.mult)
                e.tensor_copy(invf[:], invi[:])
                e.memset(epsc[:], 1e-5)
                vs(e)
                e.tensor_tensor(tempnb[:], tempnb[:], tempb[:], ALU.min)

        with nc.Block() as b:
            @b.vector
            def _(e):
                vs(e)
                e.tensor_scalar(tempab[:], tempnb[:], -1.0, None, ALU.mult)
                e.tensor_scalar(Pm[:], invf[:], iota[:], None, ALU.is_equal)
                vs(e)
                e.tensor_scalar(sgnb[:], tempab[:], 1e-30, None, ALU.max)
                vs(e)
                e.reciprocal(sgnb[:], sgnb[:])
                vs(e)
                e.tensor_tensor(sgnb[:], sgnb[:], tempb[:], ALU.mult)

        def phase_ln_gate(h, src_hbm, gate):
            with contextlib.ExitStack() as ph:
                _u = next(_uid)
                def pb_(name, p, f, dt):
                    return ph.enter_context(
                        nc.sbuf_tensor(f"{name}_{_u}", [p, f], dt))
                xtile = pb_("xtile", 128, 2 * D, F32) if src_hbm else None
                sq = pb_("sq", 128, 2 * D, F32)
                xlt = pb_("xlt", 128, 2 * D, BF16)
                cvar = pb_("cvar", 128, 2, F32)
                cmean = pb_("cmean", 128, 2, F32)
                crstd = pb_("crstd", 128, 2, F32)
                ccol = pb_("ccol", 128, 8, F32)
                if gate:
                    gwt = pb_("gwt", 128, NDK * 64, BF16)
                    gstg = pb_("gstg", 128, NDK * 64, F32)
                    scs = pb_("scs", 64, 2 * 128, F32)
                    sc = pb_("sc", 128, 2 * 64, F32)
                    sce = pb_("sce", 128, 2 * 64, F32)
                    cden = pb_("cden", 128, 2, F32)
                    wesm = pb_("wesm", 128, 2 * 64, F32)
                    vv = pb_("vv", 128, 2 * 64, F32)
                    mskt = pb_("mskt", 128, 2 * 64, F32)
                    wee = pb_("wee", 128, 2 * 64, F32)
                    weT = pb_("weT", 64, 2 * 128, F32)
                    flat = pb_("flat", 128, 2 * 64, F32)
                    fmn = pb_("fmn", 128, 2 * 64, F32)
                    gs8 = pb_("gs8", 128, 2 * 8, F32)
                    gm8 = pb_("gm8", 128, 2 * 8, F32)
                    gr8 = pb_("gr8", 128, 2 * 8, F32)
                    gsm = pb_("gsm", 128, 2 * 64, F32)
                    gex = pb_("gex", 128, 2 * 64, F32)
                    sfb = pb_("sfb", 128, 2 * 64, BF16)
                    sstg = pb_("sstg", 8, 2 * 1024, BF16)

                xb2 = [s.n for s in s_x2]
                ssb = [s.n for s in s_ss]
                gwb = s_gw.n
                db, pb0, stb = s_dve.n, s_pe.n, s_st.n
                EM = {}
                d2m, d3m, am, p2m, dvm, cm = {}, {}, {}, {}, {}, {}

                def dve_tile(t):
                    p = t % 2
                    xt = (xtile[:, p * D:(p + 1) * D] if src_hbm
                          else r_sb[:, t * D:(t + 1) * D])
                    L = []
                    A = L.append
                    if src_hbm:
                        A((lambda e: e.wait_ge(s_x2[t % 2].h,
                                               xb2[t % 2] + 16 * (t // 2 + 1)), []))
                    A((lambda e: e.tensor_mul(sq[:, p * D:(p + 1) * D], xt, xt), []))
                    A((lambda e: e.reduce_sum(ccol[:, 4 * p:4 * p + 1], xt,
                                              axis=AX.X), []))
                    A((lambda e: e.reduce_sum(ccol[:, 4 * p + 1:4 * p + 2],
                                              sq[:, p * D:(p + 1) * D],
                                              axis=AX.X), []))
                    A((lambda e: e.tensor_scalar(cmean[:, p:p + 1],
                                                 ccol[:, 4 * p:4 * p + 1],
                                                 1.0 / D, None, ALU.mult), []))
                    A((lambda e: e.tensor_scalar(ccol[:, 4 * p + 1:4 * p + 2],
                                                 ccol[:, 4 * p + 1:4 * p + 2],
                                                 1.0 / D, None, ALU.mult), []))
                    A((lambda e: e.tensor_mul(ccol[:, 4 * p + 2:4 * p + 3],
                                              cmean[:, p:p + 1],
                                              cmean[:, p:p + 1]), []))
                    if src_hbm:
                        def rinit(e):
                            if h == 1 and t == 0:
                                e.wait_ge(s_st.h, stb)
                            e.tensor_copy(r_sb[:, t * D:(t + 1) * D], xt)
                        A((rinit, []))
                    A((lambda e: s_dve2.inc_ev(e.scalar_tensor_tensor(
                        cvar[:, p:p + 1], ccol[:, 4 * p + 2:4 * p + 3], -1.0,
                        ccol[:, 4 * p + 1:4 * p + 2], ALU.mult, ALU.add),
                        d2m, ("var", t)), [("d2", ("var", t))]))

                    def pe_tr_cnt(tt, grp):
                        return pb0 + (tt // 2) * 4 + grp * 2 + (tt % 2) + 1

                    def xlt_op(e):
                        if t >= 2:
                            e.wait_ge(s_pe.h, pe_tr_cnt(t - 2, 1))
                        e.wait_ge(s_act.h, am[("rstd", t)])
                        s_dve.inc_ev(e.tensor_scalar(
                            xlt[:, p * D:(p + 1) * D], xt, cmean[:, p:p + 1],
                            crstd[:, p:p + 1], ALU.subtract, ALU.mult),
                            dvm, ("xlt", t))
                    A((xlt_op, [("dv", ("xlt", t))]))
                    for grp in range(2):
                        def evac(e, grp=grp):
                            e.wait_ge(s_pe.h, pe_tr_cnt(t, grp))
                            dst = xln[:, grp * 8 * HALF:(grp * 8 + 8) * HALF] \
                                .rearrange("p (dk c) -> p dk c", dk=8)[:, :, t * 128:(t + 1) * 128]
                            s_dve3.inc_ev(
                                e.tensor_copy(dst, psb[grp][:]
                                              .rearrange("p (dk c) -> p dk c", dk=8)),
                                d3m, (f"ev{grp}", t))
                        A((evac, [("d3", (f"ev{grp}", t))]))
                    if not gate:
                        return L

                    def scs_ev(e):
                        e.wait_ge(s_pe2.h, p2m[("scores", t)])
                        s_dve2.inc_ev(e.tensor_copy(
                            scs[:, p * 128:(p + 1) * 128],
                            psf[2 if p == 0 else 5][0:64, 0:128]),
                            d2m, ("scs", t))
                    A((scs_ev, [("d2", ("scs", t))]))

                    def sc_ev(e):
                        e.wait_ge(s_pe2.h, p2m[("sctr", t)])
                        s_dve2.inc_ev(e.tensor_copy(sc[:, p * 64:(p + 1) * 64],
                                                    psf[3][0:128, 0:64]),
                                      d2m, ("scev", t))
                    A((sc_ev, [("d2", ("scev", t))]))
                    A((lambda e: s_dve2.inc_ev(
                        e.reduce_max(ccol[:, 4 * p + 3:4 * p + 4],
                                     sc[:, p * 64:(p + 1) * 64],
                                     axis=AX.X, negate=True),
                        d2m, ("negmax", t)), [("d2", ("negmax", t))]))

                    def recip_op(e):
                        e.wait_ge(s_act.h, am[("exp64", t)])
                        e.reciprocal(cden[:, p:p + 1], cden[:, p:p + 1])
                    A((recip_op, []))
                    A((lambda e: e.tensor_scalar(wesm[:, p * 64:(p + 1) * 64],
                                                 sce[:, p * 64:(p + 1) * 64],
                                                 cden[:, p:p + 1], None,
                                                 ALU.mult), []))
                    A((lambda e: e.tensor_copy(vv[:, p * 64:(p + 1) * 64],
                                               wesm[:, p * 64:(p + 1) * 64]), []))
                    for it in range(8):
                        A((lambda e: e.reduce_max(ccol[:, 4 * p + 3:4 * p + 4],
                                                  vv[:, p * 64:(p + 1) * 64],
                                                  axis=AX.X), []))
                        if it < 7:
                            A((lambda e: e.tensor_scalar(
                                mskt[:, p * 64:(p + 1) * 64],
                                vv[:, p * 64:(p + 1) * 64],
                                ccol[:, 4 * p + 3:4 * p + 4],
                                NEG, ALU.is_ge, ALU.mult), []))
                            A((lambda e: e.tensor_add(
                                vv[:, p * 64:(p + 1) * 64],
                                vv[:, p * 64:(p + 1) * 64],
                                mskt[:, p * 64:(p + 1) * 64]), []))
                    A((lambda e: e.tensor_scalar(mskt[:, p * 64:(p + 1) * 64],
                                                 wesm[:, p * 64:(p + 1) * 64],
                                                 ccol[:, 4 * p + 3:4 * p + 4],
                                                 None, ALU.is_ge), []))
                    A((lambda e: s_dve2.inc_ev(
                        e.tensor_mul(wee[:, p * 64:(p + 1) * 64],
                                     mskt[:, p * 64:(p + 1) * 64],
                                     wesm[:, p * 64:(p + 1) * 64]),
                        d2m, ("we", t)), [("d2", ("we", t))]))

                    def weT_ev(e):
                        e.wait_ge(s_pe2.h, p2m[("wetr", t)])
                        s_dve2.inc_ev(e.tensor_copy(
                            weT[:, p * 128:(p + 1) * 128],
                            psf[4][0:64, 0:128]),
                            d2m, ("weT", t))
                    A((weT_ev, [("d2", ("weT", t))]))

                    def flat_ev(e):
                        e.wait_ge(s_pe2.h, p2m[("flat", t)])
                        s_dve2.inc_ev(e.tensor_copy(flat[:, p * 64:(p + 1) * 64],
                                                    psf[0][0:128, 0:64]),
                                      d2m, ("flatev", t))
                    A((flat_ev, [("d2", ("flatev", t))]))
                    A((lambda e: e.tensor_scalar(mskt[:, p * 64:(p + 1) * 64],
                                                 flat[:, p * 64:(p + 1) * 64],
                                                 0.0, -1.0e9, ALU.is_equal,
                                                 ALU.mult), []))
                    A((lambda e: e.reduce_sum(
                        gs8[:, p * 8:(p + 1) * 8].rearrange("p (g o) -> p g o", g=8),
                        flat[:, p * 64:(p + 1) * 64].rearrange("p (g i) -> p g i", g=8),
                        axis=AX.X), []))
                    A((lambda e: e.tensor_add(fmn[:, p * 64:(p + 1) * 64],
                                              flat[:, p * 64:(p + 1) * 64],
                                              mskt[:, p * 64:(p + 1) * 64]), []))
                    A((lambda e: e.tensor_scalar(scalbuf[:, t * 8:(t + 1) * 8],
                                                 gs8[:, p * 8:(p + 1) * 8],
                                                 0.5, None, ALU.mult), []))
                    A((lambda e: e.reduce_max(
                        gm8[:, p * 8:(p + 1) * 8].rearrange("p (g o) -> p g o", g=8),
                        fmn[:, p * 64:(p + 1) * 64].rearrange("p (g i) -> p g i", g=8),
                        axis=AX.X), []))
                    A((lambda e: s_dve2.inc_ev(e.tensor_tensor(
                        gsm[:, p * 64:(p + 1) * 64].rearrange("p (g i) -> p g i", g=8),
                        fmn[:, p * 64:(p + 1) * 64].rearrange("p (g i) -> p g i", g=8),
                        gm8[:, p * 8:(p + 1) * 8].rearrange("p (g o) -> p g o", g=8)
                        .broadcast_to((128, 8, 8)), ALU.subtract),
                        d2m, ("gsm", t)), [("d2", ("gsm", t))]))

                    def gsum_op(e):
                        e.wait_ge(s_act.h, am[("gex", t)])
                        e.reduce_sum(
                            gs8[:, p * 8:(p + 1) * 8].rearrange("p (g o) -> p g o", g=8),
                            gex[:, p * 64:(p + 1) * 64].rearrange("p (g i) -> p g i", g=8),
                            axis=AX.X)
                    A((gsum_op, []))
                    A((lambda e: e.reciprocal(gr8[:, p * 8:(p + 1) * 8],
                                              gs8[:, p * 8:(p + 1) * 8]), []))
                    A((lambda e: s_dve2.inc_ev(e.tensor_tensor(
                        sfb[:, p * 64:(p + 1) * 64].rearrange("p (g i) -> p g i", g=8),
                        gex[:, p * 64:(p + 1) * 64].rearrange("p (g i) -> p g i", g=8),
                        gr8[:, p * 8:(p + 1) * 8].rearrange("p (g o) -> p g o", g=8)
                        .broadcast_to((128, 8, 8)), ALU.mult),
                        d2m, ("sfb", t)), [("d2", ("sfb", t))]))

                    def softev_op(e):
                        e.wait_ge(s_pe2.h, p2m[("softtr", t)])
                        if t >= 2:
                            e.wait_ge(s_ss[t % 2].h,
                                      ssb[t % 2] + 16 * ((t - 2) // 2 + 1))
                        s_dve3.inc_ev(e.tensor_copy(
                            sstg[0:8, p * 1024:(p + 1) * 1024]
                            .rearrange("i (g c) -> i g c", g=8),
                            psb[1][0:8, :].rearrange("i (g c) -> i g c", g=8)),
                            d3m, ("softev", t))
                        s_dve2.inc_ev(e.engine_nop(), d2m, ("sstg", t))
                    A((softev_op, [("d3", ("softev", t)), ("d2", ("sstg", t))]))
                    return L

                def act_tile(t):
                    p = t % 2
                    L = []

                    def rstd_op(e):
                        e.wait_ge(s_dve2.h, d2m[("var", t)])
                        e.activation(crstd[:, p:p + 1], cvar[:, p:p + 1],
                                     AF.Ln, bias=epsc[:])
                        s_act.inc_ev(e.activation(crstd[:, p:p + 1],
                                                  crstd[:, p:p + 1],
                                                  AF.Exp, scale=-0.5),
                                     am, ("rstd", t))
                    L.append((rstd_op, [("a", ("rstd", t))]))
                    if not gate:
                        return L

                    def exp64_op(e):
                        e.wait_ge(s_dve2.h, d2m[("negmax", t)])
                        s_act.inc_ev(
                            e.activation(sce[:, p * 64:(p + 1) * 64],
                                         sc[:, p * 64:(p + 1) * 64], AF.Exp,
                                         bias=ccol[:, 4 * p + 3:4 * p + 4],
                                         accum_out=cden[:, p:p + 1]),
                            am, ("exp64", t))
                    L.append((exp64_op, [("a", ("exp64", t))]))

                    def gex_op(e):
                        e.wait_ge(s_dve2.h, d2m[("gsm", t)])
                        s_act.inc_ev(e.activation(gex[:, p * 64:(p + 1) * 64],
                                                  gsm[:, p * 64:(p + 1) * 64],
                                                  AF.Exp), am, ("gex", t))
                    L.append((gex_op, [("a", ("gex", t))]))
                    return L

                def pe_tile(t):
                    p = t % 2
                    L = []
                    for grp in range(2):
                        def tr(e, grp=grp):
                            bt = EM["bt"]
                            e.wait_ge(s_dve.h, dvm[("xlt", t)])
                            bt.acquire(e, ("psb", grp),
                                       (s_dve3, d3m[(f"ev{grp}", t)]))
                            for k in range(8):
                                dk = grp * 8 + k
                                mm = e.matmul(
                                    psb[grp][:, k * 128:(k + 1) * 128],
                                    xlt[:, p * D + dk * 128:p * D + (dk + 1) * 128],
                                    idb[:], start=(k == 0), stop=(k == 7),
                                    is_transpose=True)
                            s_pe.inc(mm)
                        L.append((tr, []))
                    if not gate:
                        return L

                    def scores_op(e):
                        bt = EM["bt"]
                        e.wait_ge(s_dve3.h, d3m[("ev1", t)])
                        if t == 0:
                            e.wait_ge(s_cast.h, cm[("gwt",)])
                        bank = 2 if p == 0 else 5
                        bt.acquire(e, ("psf", bank), (s_dve2, d2m[("scs", t)]))
                        for dk in range(NDK):
                            mm = e.matmul(
                                psf[bank][0:64, 0:128],
                                gwt[:, dk * 64:(dk + 1) * 64],
                                xln[:, dk * HALF + t * 128:dk * HALF + (t + 1) * 128],
                                start=(dk == 0), stop=(dk == NDK - 1))
                        s_pe2.inc_ev(mm, p2m, ("scores", t))
                    L.append((scores_op, [("p2", ("scores", t))]))

                    def sctr_op(e):
                        bt = EM["bt"]
                        e.wait_ge(s_dve2.h, d2m[("scs", t)])
                        bt.acquire(e, ("psf", 3), (s_dve2, d2m[("scev", t)]))
                        s_pe2.inc_ev(e.matmul(psf[3][0:128, 0:64],
                                              scs[:, p * 128:(p + 1) * 128],
                                              idf[0:64, 0:64], start=True,
                                              stop=True, is_transpose=True),
                                     p2m, ("sctr", t))
                    L.append((sctr_op, [("p2", ("sctr", t))]))

                    def wetr_op(e):
                        bt = EM["bt"]
                        e.wait_ge(s_dve2.h, d2m[("we", t)])
                        bt.acquire(e, ("psf", 4), (s_dve2, d2m[("weT", t)]))
                        s_pe2.inc_ev(e.matmul(psf[4][0:64, 0:128],
                                              wee[:, p * 64:(p + 1) * 64],
                                              idf[:], start=True, stop=True,
                                              is_transpose=True),
                                     p2m, ("wetr", t))
                    L.append((wetr_op, [("p2", ("wetr", t))]))

                    def flat_op(e):
                        bt = EM["bt"]
                        e.wait_ge(s_dve2.h, d2m[("weT", t)])
                        bt.acquire(e, ("psf", 0), (s_dve2, d2m[("flatev", t)]))
                        e.matmul(psf[0][0:128, 0:64],
                                 weT[:, p * 128:(p + 1) * 128], Pm[:],
                                 start=True, stop=True)
                        s_pe2.inc_ev(e.matmul(
                            psf[2 if p == 0 else 5][0:1, 400:401],
                            idb[0:1, 0:1], idb[0:1, 0:1],
                            start=False, stop=True),
                                     p2m, ("flat", t))
                    L.append((flat_op, [("p2", ("flat", t))]))

                    def softtr_op(e):
                        bt = EM["bt"]
                        e.wait_ge(s_dve2.h, d2m[("sfb", t)])
                        bt.acquire(e, ("psb", 1), (s_dve3, d3m[("softev", t)]))
                        for g in range(8):
                            mm = e.matmul(psb[1][0:8, g * 128:(g + 1) * 128],
                                          sfb[:, p * 64 + g * 8:p * 64 + (g + 1) * 8],
                                          idb[:], start=(g == 0), stop=(g == 7),
                                          is_transpose=True)
                        s_pe2.inc_ev(mm, p2m, ("softtr", t))
                    L.append((softtr_op, [("p2", ("softtr", t))]))
                    return L

                def zipped(mk):
                    seq = []
                    for pp in range(NT // 2):
                        a = mk(2 * pp)
                        bo = mk(2 * pp + 1)
                        for i in range(max(len(a), len(bo))):
                            if i < len(a):
                                seq.append(a[i])
                            if i < len(bo):
                                seq.append(bo[i])
                    return seq

                dve_seq = zipped(dve_tile)
                act_seq = zipped(act_tile)
                pe_seq = zipped(pe_tile)

                def evs_of(seq, ch):
                    return [k for _, evs in seq for c, k in evs if c == ch]

                d2m.update(s_dve2.evmap(evs_of(dve_seq, "d2")))
                d3m.update(s_dve3.evmap(evs_of(dve_seq, "d3")))
                dvm.update(s_dve.evmap(evs_of(dve_seq, "dv")))
                am.update(s_act.evmap(evs_of(act_seq, "a")))
                p2m.update(s_pe2.evmap(evs_of(pe_seq, "p2")))
                if gate:
                    cm.update(s_cast.evmap([("gwt",)]))

                with nc.Block() as b:
                    if src_hbm:
                        @b.sync
                        def _(e):
                            for t in range(NT):
                                if t >= 2:
                                    e.wait_ge(s_dve.h, dvm[("xlt", t - 2)])
                                s_x2[t % 2].inc(e.dma_start(
                                    xtile[:, (t % 2) * D:(t % 2 + 1) * D],
                                    x_in[h * HALF + t * 128:h * HALF + (t + 1) * 128, :]),
                                    16)

                    if gate:
                        @b.gpsimd
                        def _(e):
                            s_gw.inc(e.dma_start(gstg[:], gw_in), 16)
                            for t in range(NT):
                                e.wait_ge(s_dve2.h, d2m[("sstg", t)])
                                s_ss[t % 2].inc(e.dma_start(
                                    soft_dram[:, :, h * HALF + t * 128:
                                              h * HALF + (t + 1) * 128]
                                    .rearrange("g i c -> i g c"),
                                    sstg[0:8, (t % 2) * 1024:(t % 2 + 1) * 1024]
                                    .rearrange("i (g c) -> i g c", g=8)), 16)

                    @b.scalar
                    def _(e):
                        if gate:
                            e.wait_ge(s_gw.h, gwb + 16)
                            s_cast.inc_ev(e.activation(gwt[:], gstg[:], AF.Copy),
                                          cm, ("gwt",))
                        for fn, _evs in act_seq:
                            fn(e)

                    @b.vector
                    def _(e):
                        for fn, _evs in dve_seq:
                            fn(e)

                    @b.tensor
                    def _(e):
                        EM["bt"] = Banks()
                        for fn, _evs in pe_seq:
                            fn(e)

        def phase_ffn(h, build_mw):
            with contextlib.ExitStack() as ph:
                _u = next(_uid)
                ap_ = ph.enter_context(
                    nc.sbuf_tensor(f"apA_{_u}", [128, NICH * HALF], BF16))
                silg = ph.enter_context(
                    nc.sbuf_tensor(f"silg_{_u}", [128, 2 * HALF], F32))
                wdbf = ph.enter_context(
                    nc.sbuf_tensor(f"wdbf_{_u}", [128, 2 * NICH * 512], BF16))
                mskw = (ph.enter_context(
                    nc.sbuf_tensor(f"mskw_{_u}", [128, NG * NDK * 8], BF16))
                    if build_mw else None)
                mstg = (ph.enter_context(
                    nc.sbuf_tensor(f"mstg_{_u}", [128, NG * NDK * 8], F32))
                    if build_mw else None)
                soft_rg = ph.enter_context(
                    nc.sbuf_tensor(f"softrg_{_u}", [8, 2 * HALF], BF16))
                fstage = ph.enter_context(
                    nc.sbuf_tensor(f"fstage_{_u}", [128, 3 * 2048], F32))
                fwbf = ph.enter_context(
                    nc.sbuf_tensor(f"fwbf_{_u}", [128, 3 * 2048], BF16))
                mw_rg = ph.enter_context(
                    nc.sbuf_tensor(f"mwrg_{_u}", [8, 2 * 2 * NI], BF16))

                wfb = [s.n for s in s_wf]
                gsb = [s.n for s in s_gs]
                gmlb = [s.n for s in s_gml]
                gmsb = [s.n for s in s_gms]
                ssb_tot = [s.n for s in s_ss]
                gwb = s_gw.n
                NU = 34
                down_units = [(dblk, u * 4, min(4, NICH - u * 4))
                              for dblk in range(4) for u in range(3)]

                cast_ev = ([("mk",)] if build_mw else [])
                pe_ev, pe2_ev, act_ev, dve_ev, d2_ev, d3_ev = [], [], [], [], [], []
                for g in range(NG):
                    for k in range(22):
                        cast_ev.append(("gu", g, k))
                    for ich in range(NICH):
                        if build_mw:
                            for m in range(2):
                                pe2_ev.append(("mw", g, ich, m))
                                d3_ev.append(("mwev", g, ich, m))
                        for m in range(2):
                            for tb in range(2):
                                pe_ev.append(("gu", g, ich, m, tb))
                                if m == 0:
                                    act_ev.append(("sil", g, ich, tb))
                                else:
                                    dve_ev.append(("ap", g, ich, tb))
                    for j in range(12):
                        cast_ev.append(("dn", g, j))
                    for dblk in range(4):
                        for t in range(NT):
                            pe2_ev.append(("dn", g, dblk, t))
                            d2_ev.append(("dnev", g, dblk, t))
                cmap = s_cast.evmap(cast_ev)
                pem = s_pe.evmap(pe_ev)
                pe2m = s_pe2.evmap(pe2_ev)
                actm = s_act.evmap(act_ev)
                dvem = s_dve.evmap(dve_ev)
                d2map = s_dve2.evmap(d2_ev)
                d3map = s_dve3.evmap(d3_ev)

                def cast_key(g, k):
                    return ("gu", g, k) if k < 22 else ("dn", g, k - 22)

                def glob_unit(ui):
                    return divmod(ui, NU)

                with nc.Block() as b:
                    @b.sync
                    def _(e):
                        for g in range(NG):
                            for k in range(NU):
                                ui = g * NU + k
                                if ui >= 3:
                                    pg, pk = glob_unit(ui - 3)
                                    e.wait_ge(s_cast.h, cmap[cast_key(pg, pk)])
                                slot = fstage[:, (ui % 3) * 2048:(ui % 3 + 1) * 2048]
                                sem = s_wf[ui % 3]
                                if k < 22:
                                    ich, m = divmod(k, 2)
                                    sem.inc(e.dma_start(
                                        slot.rearrange("p (dk a) -> p dk a", dk=NDK),
                                        (wg_in if m == 0 else wu_in)[g, ich]), 16)
                                else:
                                    dblk, i0, ni = down_units[k - 22]
                                    sem.inc(e.dma_start(
                                        slot.rearrange("p (i c) -> p i c", i=4)[:, 0:ni, :],
                                        wd_in[g, dblk, i0:i0 + ni]
                                        .rearrange("i p c -> p i c")), 16)

                    @b.gpsimd
                    def _(e):
                        if build_mw:
                            s_gw.inc(e.dma_start(mstg[:], mkw_in), 16)
                        for i in range(2):
                            e.wait_ge(s_ss[i].h, s_ss[i].n)
                        if not build_mw:
                            for i in range(2):
                                e.wait_ge(s_gms[i].h, s_gms[i].n)
                        for g in range(NG):
                            if g >= 2:
                                e.wait_ge(s_pe.h, pem[("gu", g - 2, NICH - 1, 1, 1)])
                            sslot = soft_rg[0:8, (g % 2) * HALF:(g % 2 + 1) * HALF]
                            mslot = mw_rg[0:8, (g % 2) * 2 * NI:(g % 2 + 1) * 2 * NI]
                            s_gs[g % 2].inc(e.dma_start(
                                sslot,
                                soft_dram[g, :, h * HALF:(h + 1) * HALF]), 16)
                            if build_mw:
                                e.wait_ge(s_dve3.h, d3map[("mwev", g, NICH - 1, 1)])
                                s_gms[g % 2].inc(e.dma_start(
                                    mw_dram[:, g].rearrange("m i c -> i m c"),
                                    mslot.rearrange("i (m c) -> i m c", m=2)), 16)
                            else:
                                s_gml[g % 2].inc(e.dma_start(
                                    mslot.rearrange("i (m c) -> i m c", m=2),
                                    mw_dram[:, g].rearrange("m i c -> i m c")), 16)

                    def ap_evac(e, g, ich):
                        for tb in range(2):
                            e.wait_ge(s_pe.h, pem[("gu", g, ich, 1, tb)])
                            e.wait_ge(s_act.h, actm[("sil", g, ich, tb)])
                            s_dve.inc_ev(e.tensor_mul(
                                ap_[:, ich * HALF + tb * 512:ich * HALF + (tb + 1) * 512],
                                psf[2 + tb][:],
                                silg[:, (ich % 2) * HALF + tb * 512:
                                     (ich % 2) * HALF + (tb + 1) * 512]),
                                dvem, ("ap", g, ich, tb))

                    def down_evac(e, g, dblk, t):
                        e.wait_ge(s_pe2.h, pe2m[("dn", g, dblk, t)])
                        s_dve2.inc_ev(e.scalar_tensor_tensor(
                            r_sb[:, t * D + dblk * 512:t * D + (dblk + 1) * 512],
                            psf[4 + (g * 32 + dblk * 8 + t) % 2][:],
                            scalbuf[:, t * 8 + g:t * 8 + g + 1],
                            r_sb[:, t * D + dblk * 512:t * D + (dblk + 1) * 512],
                            ALU.mult, ALU.add), d2map, ("dnev", g, dblk, t))

                    @b.vector
                    def _(e):
                        for g in range(NG):
                            for ich in range(NICH):
                                if build_mw:
                                    for m in range(2):
                                        if g >= 2 and ich == 0 and m == 0:
                                            e.wait_ge(s_gms[g % 2].h,
                                                      gmsb[g % 2]
                                                      + 16 * ((g - 2) // 2 + 1))
                                        e.wait_ge(s_pe2.h, pe2m[("mw", g, ich, m)])
                                        s_dve3.inc_ev(e.tensor_copy(
                                            mw_rg[0:8, (g % 2) * 2 * NI + m * NI
                                                  + ich * 128:(g % 2) * 2 * NI
                                                  + m * NI + (ich + 1) * 128],
                                            psf[4 + m][0:8, 0:128]),
                                            d3map, ("mwev", g, ich, m))
                                if ich >= 1:
                                    ap_evac(e, g, ich - 1)
                            ap_evac(e, g, NICH - 1)
                            for dblk in range(1, 4):
                                for t in range(NT):
                                    down_evac(e, g, dblk - 1, t)
                            for t in range(NT):
                                down_evac(e, g, 3, t)

                    @b.scalar
                    def _(e):
                        if build_mw:
                            e.wait_ge(s_gw.h, s_gw.n)
                            s_cast.inc_ev(e.activation(mskw[:], mstg[:], AF.Copy),
                                          cmap, ("mk",))

                        def cast_gu(e, g, ich):
                            for m in range(2):
                                k = ich * 2 + m
                                Kgu = g * 22 + k
                                ui = g * NU + k
                                e.wait_ge(s_wf[ui % 3].h,
                                          wfb[ui % 3] + 16 * (ui // 3 + 1))
                                if Kgu >= 3:
                                    pk = Kgu - 3
                                    pg, pkk = divmod(pk, 22)
                                    pich, pm = divmod(pkk, 2)
                                    e.wait_ge(s_pe.h, pem[("gu", pg, pich, pm, 1)])
                                s_cast.inc_ev(e.activation(
                                    fwbf[:, (Kgu % 3) * 2048:(Kgu % 3 + 1) * 2048],
                                    fstage[:, (ui % 3) * 2048:(ui % 3 + 1) * 2048],
                                    AF.Copy), cmap, ("gu", g, k))

                        def sil_pair(e, g, ich):
                            I_ = g * NICH + ich
                            for tb in range(2):
                                e.wait_ge(s_pe.h, pem[("gu", g, ich, 0, tb)])
                                if I_ >= 2:
                                    pg, pich = divmod(I_ - 2, NICH)
                                    e.wait_ge(s_dve.h, dvem[("ap", pg, pich, tb)])
                                s_act.inc_ev(e.activation(
                                    silg[:, (ich % 2) * HALF + tb * 512:
                                         (ich % 2) * HALF + (tb + 1) * 512],
                                    psf[tb][:], AF.Silu),
                                    actm, ("sil", g, ich, tb))

                        for g in range(NG):
                            for ich in range(NICH):
                                cast_gu(e, g, ich)
                                if ich >= 1:
                                    sil_pair(e, g, ich - 1)
                            sil_pair(e, g, NICH - 1)
                            for j, (dblk, i0, ni) in enumerate(down_units):
                                k = 22 + j
                                ui = g * NU + k
                                DJ = g * 4 + dblk
                                e.wait_ge(s_wf[ui % 3].h,
                                          wfb[ui % 3] + 16 * (ui // 3 + 1))
                                if DJ >= 2 and i0 == 0:
                                    pg, pd = divmod(DJ - 2, 4)
                                    e.wait_ge(s_pe2.h, pe2m[("dn", pg, pd, NT - 1)])
                                s_cast.inc_ev(e.activation(
                                    wdbf[:, (dblk % 2) * NICH * 512 + i0 * 512:
                                         (dblk % 2) * NICH * 512 + (i0 + ni) * 512],
                                    fstage[:, (ui % 3) * 2048:(ui % 3) * 2048 + ni * 512],
                                    AF.Copy), cmap, ("dn", g, j))

                    @b.tensor
                    def _(e):
                        bt = Banks()
                        for g in range(NG):
                            for ich in range(NICH):
                                if build_mw:
                                    for m in range(2):
                                        k = ich * 2 + m
                                        e.wait_ge(s_cast.h, cmap[("gu", g, k)])
                                        if ich == 0 and m == 0:
                                            e.wait_ge(s_cast.h, cmap[("mk",)])
                                        bt.acquire(e, ("psf", 4 + m),
                                                   (s_dve3, d3map[("mwev", g, ich, m)]))
                                        for dk in range(NDK):
                                            mm = e.matmul(
                                                psf[4 + m][0:8, 0:128],
                                                mskw[:, g * 128 + dk * 8:
                                                     g * 128 + (dk + 1) * 8],
                                                fwbf[:, ((g * 22 + k) % 3) * 2048 + dk * 128:
                                                    ((g * 22 + k) % 3) * 2048 + (dk + 1) * 128],
                                                start=(dk == 0), stop=(dk == NDK - 1))
                                        s_pe2.inc_ev(mm, pe2m, ("mw", g, ich, m))
                                for m in range(2):
                                    k = ich * 2 + m
                                    e.wait_ge(s_cast.h, cmap[("gu", g, k)])
                                    if k == 0:
                                        e.wait_ge(s_gs[g % 2].h,
                                                  gsb[g % 2] + 16 * (g // 2 + 1))
                                        if not build_mw:
                                            e.wait_ge(s_gml[g % 2].h,
                                                      gmlb[g % 2]
                                                      + 16 * (g // 2 + 1))
                                    if build_mw:
                                        e.wait_ge(s_dve3.h, d3map[("mwev", g, ich, m)])
                                    for tb in range(2):
                                        bank = m * 2 + tb
                                        evac = ((s_act, actm[("sil", g, ich, tb)])
                                                if m == 0 else
                                                (s_dve, dvem[("ap", g, ich, tb)]))
                                        bt.acquire(e, ("psf", bank), evac)
                                        for dk in range(NDK):
                                            e.matmul(
                                                psf[bank][:],
                                                fwbf[:, ((g * 22 + k) % 3) * 2048 + dk * 128:
                                                    ((g * 22 + k) % 3) * 2048 + (dk + 1) * 128],
                                                xln[:, dk * HALF + tb * 512:
                                                    dk * HALF + (tb + 1) * 512],
                                                start=(dk == 0), stop=False)
                                        mm = e.matmul(
                                            psf[bank][:],
                                            mw_rg[0:8, (g % 2) * 2 * NI + m * NI
                                                  + ich * 128:(g % 2) * 2 * NI
                                                  + m * NI + (ich + 1) * 128],
                                            soft_rg[0:8, (g % 2) * HALF + tb * 512:
                                                    (g % 2) * HALF + (tb + 1) * 512],
                                            start=False, stop=True)
                                        s_pe.inc_ev(mm, pem, ("gu", g, ich, m, tb))
                            e.wait_ge(s_dve.h, dvem[("ap", g, NICH - 1, 1)])
                            for dblk in range(4):
                                e.wait_ge(s_cast.h, cmap[("dn", g, dblk * 3 + 2)])
                                for t in range(NT):
                                    gi = g * 32 + dblk * 8 + t
                                    bt.acquire(e, ("psf", 4 + gi % 2),
                                               (s_dve2, d2map[("dnev", g, dblk, t)]))
                                    for ich in range(NICH):
                                        mm = e.matmul(
                                            psf[4 + gi % 2][:],
                                            ap_[:, ich * HALF + t * 128:
                                                ich * HALF + (t + 1) * 128],
                                            wdbf[:, (dblk % 2) * NICH * 512 + ich * 512:
                                                 (dblk % 2) * NICH * 512 + (ich + 1) * 512],
                                            start=(ich == 0), stop=(ich == NICH - 1))
                                    s_pe2.inc_ev(mm, pe2m, ("dn", g, dblk, t))

        def phase_attn(h):
            phase_ln_gate(h, src_hbm=False, gate=False)
            with contextlib.ExitStack() as phA:
                _uA = next(_uid)
                def pbA(name, p, f, dt):
                    return phA.enter_context(
                        nc.sbuf_tensor(f"{name}_{_uA}", [p, f], dt))
                kfm = pbA("kfm", 128, NKV * 1280, BF16)
                vtm = pbA("vtm", 128, 10 * 512, BF16)
                qao = pbA("qao", 128, 8 * HALF, BF16)
                wstage = pbA("wstage", 128, 2 * 2048, F32)
                wbf = pbA("wbf", 128, 2 * 2048, BF16)
                for pass_ in range(2):
                    attn_qkv(h, pass_, kfm, vtm, qao, wstage, wbf)
                    if h == 0 and "qao" in DEBUG_TAPS:
                        qt = tap_dram(f"qao{pass_}", (128, 8 * HALF), BF16)
                        with nc.Block() as b:
                            @b.gpsimd
                            def _(e):
                                s_g.inc(e.dma_start(qt[:], qao[:]), 16)
                                e.wait_ge(s_g.h, s_g.n)
                    attn_oproj(h, pass_, qao, wstage)
                if h == 0 and "kv" in DEBUG_TAPS:
                    kt = tap_dram("kfm", (128, NKV * 1280), BF16)
                    vt = tap_dram("vtm", (128, 10 * 512), BF16)
                    with nc.Block() as b:
                        @b.gpsimd
                        def _(e):
                            s_g.inc(e.dma_start(kt[:], kfm[:]), 16)
                            s_g.inc(e.dma_start(vt[:], vtm[:]), 16)
                            e.wait_ge(s_g.h, s_g.n)
                if h == 0:
                    with nc.Block() as b:
                        @b.gpsimd
                        def _(e):
                            s_gkv.inc(e.dma_start(
                                kv_dram[0][:, 0:1024]
                                .rearrange("p (kv s) -> p kv s", kv=4),
                                kfm[:].rearrange("p (kv s) -> p kv s", kv=4)
                                [:, :, 768:1024]), 16)
                            s_gkv.inc(e.dma_start(kv_dram[1][:, 0:1024],
                                                  vtm[:, 6 * 512:8 * 512]), 16)
                            e.wait_ge(s_gkv.h, s_gkv.n)

        def attn_qkv(h, pass_, kfm, vtm, qao, wstage, wbf):
            with contextlib.ExitStack() as ph:
                _u = next(_uid)
                def pb_(name, p, f, dt):
                    return ph.enter_context(
                        nc.sbuf_tensor(f"{name}_{_u}", [p, f], dt))
                csb = pb_("csb", 128, 2 * HALF, F32)
                mskb = pb_("mskb", 128, 384, BF16)
                qraw = pb_("qraw", 128, HALF, F32)
                rot = pb_("rot", 128, HALF, F32)
                xfer = pb_("xfer", 128, HALF, BF16)
                sqtm = pb_("sqtm", 128, 2 * HALF, BF16)
                sscol = pb_("sscol", 128, 2 * 8, F32)
                rstdc = pb_("rstdc", 128, 2 * 8, F32)
                cqc = pb_("cqc", 128, 2 * 8, F32)
                exb = pb_("exb", 128, 6 * 384, F32)
                abf = pb_("abf", 128, 6 * 384, BF16)
                atT = pb_("atT", 128, 6 * 384, BF16)
                cden = pb_("cdena", 128, 6, F32)

                if pass_ == 0:
                    units = ([("k", kv) for kv in range(NKV)]
                             + [("v", kv) for kv in range(NKV)]
                             + [("q", hh) for hh in range(8)])
                else:
                    units = [("q", hh) for hh in range(8, 16)]
                NUq = len(units)
                qk_us = [i for i, (tp, _) in enumerate(units) if tp != "v"]
                q_us = [i for i, (tp, _) in enumerate(units) if tp == "q"]

                wab = [s.n for s in s_wa]
                gcb, gkvb = s_gc.n, s_gkv.n
                peb = s_pe.n

                def att_pipeline(emit_head, emit_tail):
                    for p in range(NT // 2):
                        emit_head(p)
                        if p >= 2:
                            emit_tail(p - 2)
                    emit_tail(NT // 2 - 2)
                    emit_tail(NT // 2 - 1)

                def asl(tl):
                    return ((tl // 2) % 3) * 2 + (tl % 2)

                cast_ev, pe2_ev, act_ev, d2_ev = [], [], [], []
                for ui, (tp, idx) in enumerate(units):
                    cast_ev.append(("w", ui))
                for ui, (tp, idx) in enumerate(units):
                    if tp == "q":
                        break
                    if tp == "k":
                        d2_ev += [("ev", ui), ("qro", ui), ("bf", ui), ("ssm", ui)]
                        act_ev += [("sqT", ui), ("rstd", ui)]
                        pe2_ev += [("trk1", ui), ("trk2", ui)]
                        d2_ev += [("knrm", ui), ("kfmev", ui)]
                    else:
                        d2_ev += [("ev", ui), ("vstg", ui), ("vtm", ui)]
                        pe2_ev.append(("vtr", ui))
                def sched_att_pe(pu):
                    att_pipeline(
                        lambda p: pe2_ev.extend([("sc", pu, 2 * p),
                                                 ("sc", pu, 2 * p + 1)]),
                        lambda p: pe2_ev.extend([("atrp", pu, p),
                                                 ("pv", pu, 2 * p),
                                                 ("pv", pu, 2 * p + 1)]))

                def sched_att_act(pu):
                    att_pipeline(
                        lambda p: act_ev.extend([("exp", pu, 2 * p),
                                                 ("exp", pu, 2 * p + 1)]),
                        lambda p: act_ev.extend([("out", pu, 2 * p),
                                                 ("out", pu, 2 * p + 1)]))

                def d2_tail(pu, p):
                    return [("rec", pu, p), ("abf", pu, 2 * p),
                            ("abf", pu, 2 * p + 1), ("atTp", pu, p)]

                for qi, ui in enumerate(q_us):
                    if qi >= 1:
                        pu = q_us[qi - 1]
                        sched_att_pe(pu)
                        sched_att_act(pu)
                        d2_ev += ([("ev", ui)] + d2_tail(pu, 0)
                                  + [("qro", ui)] + d2_tail(pu, 1)
                                  + [("bf", ui)] + d2_tail(pu, 2)
                                  + d2_tail(pu, 3))
                    else:
                        d2_ev += [("ev", ui), ("qro", ui), ("bf", ui)]
                    pe2_ev.append(("trq", ui))
                    act_ev += [("sqT", ui), ("rstd", ui)]
                    d2_ev += [("ssm", ui), ("cq", ui)]
                pu = q_us[-1]
                sched_att_pe(pu)
                sched_att_act(pu)
                for p in range(4):
                    d2_ev += d2_tail(pu, p)

                cmap = s_cast.evmap(cast_ev)
                pe2m = s_pe2.evmap(pe2_ev)
                actm = s_act.evmap(act_ev)
                d2map = s_dve2.evmap(d2_ev)

                def kwin(gt):
                    c0 = max(0, gt - 2)
                    return c0, gt - c0 + 1

                with nc.Block() as b:
                    @b.gpsimd
                    def _(e):
                        s_gc.inc(e.dma_start(csb[:, 0:HALF],
                                             cs_in[:, h * HALF:(h + 1) * HALF]), 16)
                        s_gc.inc(e.dma_start(
                            csb[:, HALF:],
                            cs_in[:, SEQ + h * HALF:SEQ + (h + 1) * HALF]), 16)
                        s_gc.inc(e.dma_start(mskb[:], mskh_in[:]), 16)
                        if pass_ == 0 and h == 1:
                            s_gkv.inc(e.dma_start(
                                kfm[:].rearrange("p (kv s) -> p kv s", kv=4)
                                [:, :, 0:256],
                                kv_dram[0][:, 0:1024]
                                .rearrange("p (kv s) -> p kv s", kv=4)), 16)
                            s_gkv.inc(e.dma_start(vtm[:, 0:1024],
                                                  kv_dram[1][:, 0:1024]), 16)

                    @b.sync
                    def _(e):
                        for ui, (tp, idx) in enumerate(units):
                            if ui >= 2:
                                e.wait_ge(s_cast.h, cmap[("w", ui - 2)])
                            if tp == "k":
                                src = wkv_in[idx]
                            elif tp == "v":
                                src = wkv_in[NKV + idx]
                            else:
                                src = wq_in[idx]
                            s_wa[ui % 2].inc(e.dma_start(
                                wstage[:, (ui % 2) * 2048:(ui % 2 + 1) * 2048]
                                .rearrange("p (dk a) -> p dk a", dk=NDK), src), 16)

                    @b.scalar
                    def _(e):
                        wt = Waiter()
                        def a_cast(ui):
                            wt.w(e, s_wa[ui % 2],
                                      wab[ui % 2] + 16 * (ui // 2 + 1))
                            if ui >= 2:
                                wt.w(e, s_pe, peb + 2 * (ui - 2) + 2)
                            s_cast.inc_ev(e.activation(
                                wbf[:, (ui % 2) * 2048:(ui % 2 + 1) * 2048],
                                wstage[:, (ui % 2) * 2048:(ui % 2 + 1) * 2048],
                                AF.Copy), cmap, ("w", ui))

                        def a_sq(ui, tp, qi):
                            wt.w(e, s_pe2, pe2m[("trq" if tp == "q" else "trk1",
                                                     ui)])
                            if tp == "q" and qi >= 2:
                                wt.w(e, s_dve2, d2map[("ssm", q_us[qi - 2])])
                            sl = (qi % 2) if tp == "q" else 0
                            s_act.inc_ev(e.activation(
                                sqtm[:, sl * HALF:(sl + 1) * HALF],
                                psb[0][:], AF.Square), actm, ("sqT", ui))

                        def a_rstd(ui, tp, qi):
                            sl = (qi % 2) if tp == "q" else 0
                            wt.w(e, s_dve2, d2map[("ssm", ui)])
                            e.activation(rstdc[:, sl * 8:(sl + 1) * 8],
                                         sscol[:, sl * 8:(sl + 1) * 8], AF.Ln)
                            s_act.inc_ev(e.activation(
                                rstdc[:, sl * 8:(sl + 1) * 8],
                                rstdc[:, sl * 8:(sl + 1) * 8],
                                AF.Exp, scale=-0.5), actm, ("rstd", ui))

                        def a_att(ui, idx):
                            hl = idx - 8 * pass_
                            qi = q_us.index(ui)
                            sl = qi % 2

                            def a_head(p):
                                wt.w(e, s_pe2, pe2m[("sc", ui, 2 * p + 1)])
                                for tl in (2 * p, 2 * p + 1):
                                    wq_ = (min(h * 8 + tl, 2) + 1) * 128
                                    s_act.inc_ev(e.activation(
                                        exb[:, asl(tl) * 384:
                                            asl(tl) * 384 + wq_],
                                        psf[(p % 2) * 2 + tl % 2][:, 0:wq_],
                                        AF.Exp,
                                        scale=cqc[:, sl * 8 + tl:sl * 8 + tl + 1],
                                        bias=tempnb[:, idx:idx + 1],
                                        accum_out=cden[:, asl(tl):
                                                       asl(tl) + 1]),
                                        actm, ("exp", ui, tl))

                            def a_tail(p):
                                wt.w(e, s_pe2, pe2m[("pv", ui, 2 * p + 1)])
                                for tl in (2 * p, 2 * p + 1):
                                    s_act.inc_ev(e.activation(
                                        qao[:, hl * HALF + tl * 128:
                                            hl * HALF + (tl + 1) * 128],
                                        psf[4 + tl % 2][:, 0:128], AF.Copy),
                                        actm, ("out", ui, tl))

                            att_pipeline(a_head, a_tail)

                        for ui, (tp, idx) in enumerate(units):
                            if tp == "q":
                                break
                            a_cast(ui)
                            if tp == "k":
                                a_sq(ui, tp, 0)
                                a_rstd(ui, tp, 0)
                        a_cast(q_us[0])
                        for qi, ui in enumerate(q_us):
                            tp, idx = units[ui]
                            if qi + 1 < len(q_us):
                                a_cast(q_us[qi + 1])
                            if qi >= 1:
                                a_att(q_us[qi - 1], units[q_us[qi - 1]][1])
                            a_sq(ui, tp, qi)
                            a_rstd(ui, tp, qi)
                        a_att(q_us[-1], units[q_us[-1]][1])

                    @b.tensor
                    def _(e):
                        wt = Waiter()
                        bt = Banks(wt)

                        def p_proj(ui, tp):
                            wt.w(e, s_cast, cmap[("w", ui)])
                            if ui == 0:
                                wt.w(e, s_gc, s_gc.n)
                                wt.w(e, s_gkv, s_gkv.n)
                            for tb in range(2):
                                bank = (ui % 2) * 2 + tb
                                bt.acquire(e, ("psf", bank),
                                           (s_dve2, d2map[("ev", ui)]))
                                for dk in range(NDK):
                                    mm = e.matmul(
                                        psf[bank][:],
                                        wbf[:, (ui % 2) * 2048 + dk * 128:
                                            (ui % 2) * 2048 + (dk + 1) * 128],
                                        xln[:, dk * HALF + tb * 512:
                                            dk * HALF + (tb + 1) * 512],
                                        start=(dk == 0), stop=(dk == NDK - 1))
                                s_pe.inc(mm)

                        def p_tr(ui, tp, idx, qi):
                            wt.w(e, s_dve2, d2map[("bf", ui)])
                            src_fm = (qao[:, (idx - 8 * pass_) * HALF:
                                          (idx - 8 * pass_ + 1) * HALF]
                                      if tp == "q" else xfer)
                            evac1 = ((s_act, actm[("sqT", ui)]) if tp == "q"
                                     else (s_dve2, d2map[("knrm", ui)]))
                            bt.acquire(e, ("psb", 0), evac1)
                            for t in range(NT):
                                mm = e.matmul(psb[0][:, t * 128:(t + 1) * 128],
                                              src_fm[:, t * 128:(t + 1) * 128],
                                              idb[:], start=(t == 0),
                                              stop=(t == NT - 1), is_transpose=True)
                            s_pe2.inc_ev(mm, pe2m,
                                         ("trq" if tp == "q" else "trk1", ui))

                        def p_att(ui, idx):
                            kv = idx // 4
                            hl = idx - 8 * pass_
                            wt.w(e, s_dve2, d2map[("cq", ui)])
                            kb = 768 * h

                            def pe_head(p):
                                for tl in (2 * p, 2 * p + 1):
                                    gt = h * 8 + tl
                                    c0, nch = kwin(gt)
                                    w = nch * 128
                                    moff = (3 - nch) * 128
                                    bank = (p % 2) * 2 + tl % 2
                                    bt.acquire(e, ("psf", bank),
                                               (s_act, actm[("exp", ui, tl)]))
                                    e.matmul(psf[bank][:, 0:w], idb[:],
                                             mskb[:, moff:moff + w],
                                             start=True, stop=False)
                                    s_pe2.inc_ev(e.matmul(
                                        psf[bank][:, 0:w],
                                        qao[:, hl * HALF + tl * 128:
                                            hl * HALF + (tl + 1) * 128],
                                        kfm[:, kv * 1280 + c0 * 128 - kb:
                                            kv * 1280 + c0 * 128 - kb + w],
                                        start=False, stop=True),
                                        pe2m, ("sc", ui, tl))

                            def pe_tail(p):
                                wt.w(e, s_dve2, d2map[("abf", ui, 2 * p + 1)])
                                bt.acquire(e, ("psb", p % 2),
                                           (s_dve2, d2map[("atTp", ui, p)]))
                                mms = [(tl, c)
                                       for tl in (2 * p, 2 * p + 1)
                                       for c in range(kwin(h * 8 + tl)[1])]
                                for i, (tl, c) in enumerate(mms):
                                    mm = e.matmul(
                                        psb[p % 2][0:128,
                                                   (tl % 2) * 384 + c * 128:
                                                   (tl % 2) * 384 + (c + 1) * 128],
                                        abf[:, asl(tl) * 384 + c * 128:
                                            asl(tl) * 384 + (c + 1) * 128],
                                        idb[:], start=(i == 0),
                                        stop=(i == len(mms) - 1),
                                        is_transpose=True)
                                s_pe2.inc_ev(mm, pe2m, ("atrp", ui, p))
                                wt.w(e, s_dve2, d2map[("atTp", ui, p)])
                                for tl in (2 * p, 2 * p + 1):
                                    c0, nch = kwin(h * 8 + tl)
                                    bt.acquire(e, ("psf", 4 + tl % 2),
                                               (s_act, actm[("out", ui, tl)]))
                                    for c in range(nch):
                                        mm = e.matmul(
                                            psf[4 + tl % 2][:, 0:128],
                                            vtm[:, (c0 + c - 6 * h) * 512
                                                + kv * 128:
                                                (c0 + c - 6 * h) * 512
                                                + (kv + 1) * 128],
                                            atT[:, (p % 3) * 768
                                                + (tl % 2) * 384 + c * 128:
                                                (p % 3) * 768
                                                + (tl % 2) * 384 + (c + 1) * 128],
                                            start=(c == 0), stop=(c == nch - 1))
                                    s_pe2.inc_ev(mm, pe2m, ("pv", ui, tl))

                            att_pipeline(pe_head, pe_tail)

                        for ui, (tp, idx) in enumerate(units):
                            if tp == "q":
                                break
                            p_proj(ui, tp)
                            if tp == "v":
                                wt.w(e, s_dve2, d2map[("vstg", ui)])
                                bt.acquire(e, ("psb", 0),
                                           (s_dve2, d2map[("vtm", ui)]))
                                for t in range(NT):
                                    mm = e.matmul(psb[0][:, t * 128:(t + 1) * 128],
                                                  xfer[:, t * 128:(t + 1) * 128],
                                                  idb[:], start=(t == 0),
                                                  stop=(t == NT - 1),
                                                  is_transpose=True)
                                s_pe2.inc_ev(mm, pe2m, ("vtr", ui))
                            else:
                                p_tr(ui, tp, idx, 0)
                                wt.w(e, s_dve2, d2map[("knrm", ui)])
                                bt.acquire(e, ("psb", 1),
                                           (s_dve2, d2map[("kfmev", ui)]))
                                for t in range(NT):
                                    mm = e.matmul(psb[1][:, t * 128:(t + 1) * 128],
                                                  xfer[:, t * 128:(t + 1) * 128],
                                                  idb[:], start=(t == 0),
                                                  stop=(t == NT - 1),
                                                  is_transpose=True)
                                s_pe2.inc_ev(mm, pe2m, ("trk2", ui))
                        for qi, ui in enumerate(q_us):
                            tp, idx = units[ui]
                            p_proj(ui, tp)
                            if qi >= 1:
                                p_att(q_us[qi - 1], units[q_us[qi - 1]][1])
                            p_tr(ui, tp, idx, qi)
                        p_att(q_us[-1], units[q_us[-1]][1])

                    @b.vector
                    def _(e):
                        wt = Waiter()
                        def v_d1(ui, tp, idx, qi):
                            qkprev = qk_us.index(ui) if ui in qk_us else None
                            if tp in ("k", "q") and qkprev is not None and qkprev >= 1:
                                wt.w(e, s_act, actm[("sqT", qk_us[qkprev - 1])])
                            for tb in range(2):
                                wt.w(e, s_pe, peb + 2 * ui + tb + 1)
                                e.tensor_copy(qraw[:, tb * 512:(tb + 1) * 512],
                                              psf[(ui % 2) * 2 + tb][:])
                            if tp == "v":
                                vs(e)
                                s_dve2.inc_ev(e.engine_nop(), d2map, ("ev", ui))
                                if idx == 0:
                                    wt.w(e, s_act, actm[("sqT", qk_us[NKV - 1])])
                                else:
                                    wt.w(e, s_pe2, pe2m[("vtr", ui - 1)])
                                e.tensor_copy(xfer[:], qraw[:])
                                vs(e)
                                s_dve2.inc_ev(e.engine_nop(), d2map, ("vstg", ui))
                                wt.w(e, s_pe2, pe2m[("vtr", ui)])
                                dst = vtm[:].rearrange("p (t c) -> p t c", c=512)[
                                    :, h * 2:h * 2 + 8, idx * 128:(idx + 1) * 128]
                                s_dve2.inc_ev(
                                    e.tensor_copy(dst, psb[0][:]
                                                  .rearrange("p (t c) -> p t c",
                                                             c=128)),
                                    d2map, ("vtm", ui))
                                return
                            if ui == 0:
                                wt.w(e, s_gc, s_gc.n)
                            vs(e)
                            s_dve2.inc_ev(e.engine_nop(), d2map, ("ev", ui))
                            e.tensor_copy(rot[0:64, :], qraw[64:128, :])
                            e.tensor_copy(rot[64:128, :], qraw[0:64, :])
                            vs(e)
                            e.tensor_mul(qraw[:], qraw[:], csb[:, 0:HALF])
                            e.tensor_mul(rot[:], rot[:], csb[:, HALF:])
                            vs(e)
                            s_dve2.inc_ev(e.tensor_add(qraw[:], qraw[:], rot[:]),
                                          d2map, ("qro", ui))
                            vs(e)
                            if tp == "q":
                                hl = idx - 8 * pass_
                                s_dve2.inc_ev(
                                    e.tensor_scalar(
                                        qao[:, hl * HALF:(hl + 1) * HALF],
                                        qraw[:], sgnb[:, idx:idx + 1], None,
                                        ALU.mult),
                                    d2map, ("bf", ui))
                            else:
                                s_dve2.inc_ev(e.tensor_copy(xfer[:], qraw[:]),
                                              d2map, ("bf", ui))

                        def v_d1q(ui, idx, qi):
                            qkprev = qk_us.index(ui)

                            def c0():
                                if qkprev >= 1:
                                    wt.w(e, s_act,
                                         actm[("sqT", qk_us[qkprev - 1])])
                                for tb in range(2):
                                    wt.w(e, s_pe, peb + 2 * ui + tb + 1)
                                    e.tensor_copy(
                                        qraw[:, tb * 512:(tb + 1) * 512],
                                        psf[(ui % 2) * 2 + tb][:])
                                if ui == 0:
                                    wt.w(e, s_gc, s_gc.n)
                                vs(e)
                                s_dve2.inc_ev(e.engine_nop(), d2map,
                                              ("ev", ui))

                            def c1():
                                e.tensor_copy(rot[0:64, :], qraw[64:128, :])
                                e.tensor_copy(rot[64:128, :], qraw[0:64, :])

                            def c2():
                                vs(e)
                                e.tensor_mul(qraw[:], qraw[:], csb[:, 0:HALF])
                                e.tensor_mul(rot[:], rot[:], csb[:, HALF:])
                                vs(e)
                                s_dve2.inc_ev(
                                    e.tensor_add(qraw[:], qraw[:], rot[:]),
                                    d2map, ("qro", ui))

                            def c3():
                                vs(e)
                                hl = idx - 8 * pass_
                                s_dve2.inc_ev(
                                    e.tensor_scalar(
                                        qao[:, hl * HALF:(hl + 1) * HALF],
                                        qraw[:], sgnb[:, idx:idx + 1], None,
                                        ALU.mult),
                                    d2map, ("bf", ui))

                            return [c0, c1, c2, c3]

                        def v_norm(ui, tp, qi):
                            sl = (qi % 2) if tp == "q" else 0
                            wt.w(e, s_act, actm[("sqT", ui)])
                            e.reduce_sum(
                                sscol[:, sl * 8:(sl + 1) * 8]
                                .rearrange("p (t o) -> p t o", t=8),
                                sqtm[:, sl * HALF:(sl + 1) * HALF]
                                .rearrange("p (t c) -> p t c", t=8),
                                axis=AX.X)
                            vs(e)
                            s_dve2.inc_ev(e.tensor_scalar(
                                sscol[:, sl * 8:(sl + 1) * 8],
                                sscol[:, sl * 8:(sl + 1) * 8],
                                1e-24, None, ALU.max), d2map, ("ssm", ui))

                        def v_katt(ui, idx):
                            wt.w(e, s_act, actm[("rstd", ui)])
                            for t in range(NT):
                                e.tensor_scalar(xfer[:, t * 128:(t + 1) * 128],
                                                psb[0][:, t * 128:(t + 1) * 128],
                                                rstdc[:, t:t + 1], None, ALU.mult)
                            s_dve2.inc_ev(e.engine_nop(), d2map, ("knrm", ui))
                            wt.w(e, s_pe2, pe2m[("trk2", ui)])
                            s_dve2.inc_ev(e.tensor_copy(
                                kfm[:, idx * 1280 + 256 * h:
                                    idx * 1280 + 256 * h + HALF],
                                psb[1][:]), d2map, ("kfmev", ui))

                        def v_cq(ui, idx, qi):
                            sl = qi % 2
                            wt.w(e, s_act, actm[("rstd", ui)])
                            s_dve2.inc_ev(e.tensor_scalar(
                                cqc[:, sl * 8:(sl + 1) * 8],
                                rstdc[:, sl * 8:(sl + 1) * 8],
                                tempab[:, idx:idx + 1], None, ALU.mult),
                                d2map, ("cq", ui))

                        def v_tails(ui, idx):
                            def mk(p):
                                def v_tail():
                                    t0, t1 = 2 * p, 2 * p + 1
                                    c = asl(t0)
                                    wt.w(e, s_act, actm[("exp", ui, t1)])
                                    e.reciprocal(cden[:, c:c + 1],
                                                 cden[:, c:c + 1])
                                    s_dve2.inc_ev(e.reciprocal(
                                        cden[:, c + 1:c + 2],
                                        cden[:, c + 1:c + 2]),
                                        d2map, ("rec", ui, p))
                                    for tl in (t0, t1):
                                        wq_ = (min(h * 8 + tl, 2) + 1) * 128
                                        s_dve2.inc_ev(e.tensor_scalar(
                                            abf[:, asl(tl) * 384:
                                                asl(tl) * 384 + wq_],
                                            exb[:, asl(tl) * 384:
                                                asl(tl) * 384 + wq_],
                                            cden[:, asl(tl):asl(tl) + 1],
                                            None, ALU.mult),
                                            d2map, ("abf", ui, tl))
                                    wt.w(e, s_pe2, pe2m[("atrp", ui, p)])
                                    s_dve2.inc_ev(e.tensor_copy(
                                        atT[:, (p % 3) * 768:
                                            (p % 3) * 768 + 768],
                                        psb[p % 2][:, 0:768]),
                                        d2map, ("atTp", ui, p))
                                return v_tail
                            return [mk(p) for p in range(4)]

                        for ui, (tp, idx) in enumerate(units):
                            if tp == "q":
                                break
                            v_d1(ui, tp, idx, 0)
                            if tp == "k":
                                v_norm(ui, tp, 0)
                                v_katt(ui, idx)
                        for qi, ui in enumerate(q_us):
                            tp, idx = units[ui]
                            C = v_d1q(ui, idx, qi)
                            if qi >= 1:
                                pu = q_us[qi - 1]
                                T = v_tails(pu, units[pu][1])
                                seq = [C[0], C[1], T[0], C[2], T[1], C[3],
                                       T[2], T[3]]
                            else:
                                seq = C
                            for f in seq:
                                f()
                            v_norm(ui, tp, qi)
                            v_cq(ui, idx, qi)
                        for f in v_tails(q_us[-1], units[q_us[-1]][1]):
                            f()

        def attn_oproj(h, pass_, qao, wstage):
            with contextlib.ExitStack() as ph:
                wobf = ph.enter_context(
                    nc.sbuf_tensor(f"wobf_{next(_uid)}", [128, 2 * 2048], BF16))
                wab = [s.n for s in s_wa]
                qs = [2 * pass_, 2 * pass_ + 1]
                cast_ev = [("q", dblk, j) for dblk in range(4) for j in range(2)]
                pe2_ev = [("o", dblk, t) for dblk in range(4) for t in range(NT)]
                d2_ev = [("oev", dblk, t) for dblk in range(4) for t in range(NT)]
                cmap = s_cast.evmap(cast_ev)
                pe2m = s_pe2.evmap(pe2_ev)
                d2map = s_dve2.evmap(d2_ev)

                with nc.Block() as b:
                    @b.sync
                    def _(e):
                        for dblk in range(4):
                            for j in range(2):
                                ui = dblk * 2 + j
                                if ui >= 2:
                                    pd, pj = divmod(ui - 2, 2)
                                    e.wait_ge(s_cast.h, cmap[("q", pd, pj)])
                                s_wa[ui % 2].inc(e.dma_start(
                                    wstage[:, (ui % 2) * 2048:(ui % 2 + 1) * 2048]
                                    .rearrange("p (a c) -> p a c", a=4),
                                    wo_in[dblk, qs[j]]), 16)

                    @b.scalar
                    def _(e):
                        for dblk in range(4):
                            for j in range(2):
                                ui = dblk * 2 + j
                                e.wait_ge(s_wa[ui % 2].h,
                                          wab[ui % 2] + 16 * (ui // 2 + 1))
                                if dblk >= 1:
                                    e.wait_ge(s_pe2.h,
                                              pe2m[("o", dblk - 1, NT - 1)])
                                s_cast.inc_ev(e.activation(
                                    wobf[:, j * 2048:(j + 1) * 2048],
                                    wstage[:, (ui % 2) * 2048:(ui % 2 + 1) * 2048],
                                    AF.Copy), cmap, ("q", dblk, j))

                    @b.tensor
                    def _(e):
                        bt = Banks()
                        for dblk in range(4):
                            for t in range(NT):
                                gi = dblk * 8 + t
                                bt.acquire(e, ("psf", gi % 6),
                                           (s_dve2, d2map[("oev", dblk, t)]))
                                for j in range(2):
                                    if t == 0:
                                        e.wait_ge(s_cast.h, cmap[("q", dblk, j)])
                                    for a in range(4):
                                        hl = j * 4 + a
                                        mm = e.matmul(
                                            psf[gi % 6][:],
                                            qao[:, hl * HALF + t * 128:
                                                hl * HALF + (t + 1) * 128],
                                            wobf[:, j * 2048 + a * 512:
                                                 j * 2048 + (a + 1) * 512],
                                            start=(j == 0 and a == 0),
                                            stop=(j == 1 and a == 3))
                                s_pe2.inc_ev(mm, pe2m, ("o", dblk, t))

                    @b.vector
                    def _(e):
                        for dblk in range(4):
                            for t in range(NT):
                                gi = dblk * 8 + t
                                e.wait_ge(s_pe2.h, pe2m[("o", dblk, t)])
                                s_dve2.inc_ev(e.tensor_add(
                                    r_sb[:, t * D + dblk * 512:t * D + (dblk + 1) * 512],
                                    psf[gi % 6][:],
                                    r_sb[:, t * D + dblk * 512:t * D + (dblk + 1) * 512]),
                                    d2map, ("oev", dblk, t))

        def phase_store(h):
            with nc.Block() as b:
                @b.gpsimd
                def _(e):
                    for t in range(NT):
                        s_st.inc(e.dma_start(
                            out[h * HALF + t * 128:h * HALF + (t + 1) * 128, :],
                            r_sb[:, t * D:(t + 1) * D]), 16)

        def tap_phase(label, h):
            if h != 0 or label not in DEBUG_TAPS:
                return
            rt = tap_dram(label + "_r", (128, NT * D))
            xt = tap_dram(label + "_xln", (128, NDK * HALF), BF16)
            sct = tap_dram(label + "_scal", (128, 64))
            with nc.Block() as b:
                @b.gpsimd
                def _(e):
                    n0 = s_g.n
                    s_g.inc(e.dma_start(rt[:], r_sb[:]), 16)
                    s_g.inc(e.dma_start(xt[:], xln[:]), 16)
                    s_g.inc(e.dma_start(sct[:], scalbuf[:]), 16)
                    e.wait_ge(s_g.h, s_g.n)

        for h in range(2):
            phase_ln_gate(h, src_hbm=True, gate=True)
            tap_phase("ln1", h)
            phase_ffn(h, build_mw=(h == 0))
            tap_phase("ffn1", h)
            phase_attn(h)
            tap_phase("attn", h)
            phase_ln_gate(h, src_hbm=False, gate=True)
            tap_phase("ln3", h)
            phase_ffn(h, build_mw=False)
            tap_phase("ffn2", h)
            phase_store(h)

        with nc.Block() as b:
            @b.gpsimd
            def _(e):
                e.wait_ge(s_st.h, s_st.n)

    return nc


def _stage_weights(i):
    f = np.float32
    gpw = np.asarray(i["L0_ffn_gate_proj_w"], f)
    upw = np.asarray(i["L0_ffn_up_proj_w"], f)
    dpw = np.asarray(i["L0_ffn_down_proj_w"], f)
    Wq = np.asarray(i["L0_attn_Wq"], f)
    Wkv = np.asarray(i["L0_attn_Wkv"], f)
    Wo = np.asarray(i["L0_attn_Wo"], f)
    gw = np.asarray(i["L0_ffn_gate_w"], f)
    mkw = np.asarray(i["L0_ffn_mask_up_w"], f)

    def gu_tiles(w):
        return np.ascontiguousarray(
            w.reshape(NG, NICH, 128, NDK, 128).transpose(0, 1, 4, 3, 2))

    return dict(
        wg=gu_tiles(gpw),
        wu=gu_tiles(upw),
        wd=np.ascontiguousarray(
            dpw.reshape(NG, 4, 512, NICH, 128).transpose(0, 1, 3, 4, 2)),
        wq=np.ascontiguousarray(
            Wq.reshape(NH, 128, NDK, 128).transpose(0, 3, 2, 1)),
        wkv=np.ascontiguousarray(
            Wkv.reshape(8, 128, NDK, 128).transpose(0, 3, 2, 1)),
        wo=np.ascontiguousarray(
            Wo.reshape(4, 512, 4, 4, 128).transpose(0, 2, 4, 3, 1)),
        gw=np.ascontiguousarray(
            gw.T.reshape(NDK, 128, 64).transpose(1, 0, 2)).reshape(128, NDK * 64),
        mkw=np.ascontiguousarray(
            mkw.reshape(NG, NDK, 128, 8).transpose(2, 0, 1, 3)).reshape(128, NG * NDK * 8),
    )


def _constants():
    f = np.float32
    pos = np.arange(SEQ, dtype=np.float64)
    invfreq = 1.0 / (10000.0 ** (np.arange(0, DH, 2, dtype=np.float64) / DH))
    ang = np.concatenate([invfreq, invfreq])[:, None] * pos[None, :]
    cos = np.cos(ang).astype(f)
    sin = np.sin(ang).astype(f)
    sin[:64] = -sin[:64]
    cs = np.concatenate([cos, sin], axis=1)

    ii = np.arange(128)[:, None]
    jj = np.arange(384)[None, :]
    mI = np.where((jj > ii) & (jj <= ii + 256), 0.0, NEG).astype(f)

    return dict(
        cs=cs,
        mskh=mI.astype(bfdt),
        identb=np.eye(128).astype(bfdt),
        identf=np.eye(128, dtype=f),
        iota64=np.arange(64, dtype=f).reshape(64, 1),
        onesrow=np.ones((1, 128), f),
        onescol=np.ones((128, 1), bfdt),
    )


_NC_CACHE = {}


def kernel(**inputs):
    x = np.asarray(inputs["L0_attn_x"], np.float32)
    shared = _stage_weights(inputs)
    shared.update(_constants())
    shared["qkt"] = np.ascontiguousarray(np.broadcast_to(
        np.asarray(inputs["L0_attn_qk_temp"], np.float32).reshape(1, NH),
        (128, NH)))
    shared["inv"] = np.ascontiguousarray(np.broadcast_to(
        np.asarray(inputs["L0_ffn_inv_map"]).astype(np.int32).reshape(1, 64),
        (64, 64)))

    if "nc" not in _NC_CACHE:
        _NC_CACHE["nc"] = build_program()
    nc = _NC_CACHE["nc"]

    in_maps = []
    for c in range(8):
        m = dict(shared)
        m["x"] = np.ascontiguousarray(x[c])
        in_maps.append(m)
    res = run_bass_kernel_spmd(nc, in_maps, core_ids=list(range(8)))
    return np.stack([np.asarray(res.results[c]["out"]) for c in range(8)], axis=0)
